# version 05
# baseline (speedup 1.0000x reference)
import numpy as np
import ml_dtypes
from contextlib import ExitStack
import concourse.bass as bass
import concourse.mybir as mybir
from concourse.bass_utils import run_bass_kernel_spmd
from concourse.ap import AP

F32 = mybir.dt.float32
BF16 = mybir.dt.bfloat16
U32 = mybir.dt.uint32
I32 = mybir.dt.int32
OP = mybir.AluOpType
AF = mybir.ActivationFunctionType
AX = mybir.AxisListType
BF = ml_dtypes.bfloat16

DM = 4096
SEQ = 4096
RPAD = 2560
OWN0 = 256
NOWN = 2048
BLK = 512
NBLK = RPAD // BLK
NFC = DM // 128
NSTATE = 64
EPS = 1e-5
INTER = 11008
NGF = INTER // 128
DEV_DUMP = False

def _host_consts():
    c = {}
    c["ident16"] = np.eye(128, dtype=np.float32).astype(BF)
    c["identf"] = np.eye(128, dtype=np.float32)
    c["J16"] = np.eye(128, dtype=np.float32)[::-1].copy().astype(BF)
    rot = np.zeros((128, 128), dtype=np.float32)
    for d in range(128):
        rot[(d + 64) % 128, d] = 1.0
    c["J2"] = rot.astype(BF)
    c["ones1f"] = np.ones((1, 512), dtype=np.float32)
    c["ones16"] = np.ones((128, 1), dtype=BF)
    L = DM
    kk = np.arange(L, dtype=np.float32) / np.float32(L)
    th = (np.float32(-2.0 * np.pi) * kk).astype(np.float32)
    Om = (np.cos(th) + 1j * np.sin(th)).astype(np.complex64)
    itan = ((1.0 - Om) / (1.0 + Om)).astype(np.complex64)
    cc = (2.0 / (1.0 + Om)).astype(np.complex64)
    bq = lambda v: np.ascontiguousarray(v.reshape(128, 32))
    c["itan_re"] = bq(itan.real.astype(np.float32))
    c["itan_im"] = bq(itan.imag.astype(np.float32))
    c["c_re"] = bq(cc.real.astype(np.float32))
    c["c_im"] = bq(cc.imag.astype(np.float32))
    b_ = np.arange(128); p_ = np.arange(128)
    W1 = np.exp(2j * np.pi * np.outer(b_, p_) / 128.0)
    c["W1re"] = W1.real.astype(np.float32)
    c["W1imneg"] = (-W1.imag).astype(np.float32)
    c["W1im"] = W1.imag.astype(np.float32)
    q_ = np.arange(32)
    tw = np.exp(2j * np.pi * np.outer(p_, q_) / 4096.0)
    c["twre"] = tw.real.astype(np.float32)
    c["twim"] = tw.imag.astype(np.float32)
    cix = np.arange(32)
    W2 = np.exp(2j * np.pi * np.outer(q_, cix) / 32.0) / 4096.0
    c["W2re"] = W2.real.astype(np.float32)
    c["W2imneg"] = (-W2.imag).astype(np.float32)
    return c

class Sem:
    __slots__ = ("h", "c")
    def __init__(self, h):
        self.h = h
        self.c = 0

class Prog:
    ENGS = ("sync", "scalar", "vector", "tensor", "gpsimd")
    def __init__(self, nc, es):
        self.nc = nc
        self.es = es
        self.q = {e: [] for e in self.ENGS}
        self.sems = {}
    def sem(self, name):
        if name not in self.sems:
            self.sems[name] = Sem(self.es.enter_context(self.nc.semaphore(name)))
        return self.sems[name]
    def on(self, eng, fn):
        self.q[eng].append(fn)
    def wait(self, eng, sem, val=None):
        v = sem.c if val is None else val
        if v > 0:
            self.on(eng, lambda e, sem=sem, v=v: e.wait_ge(sem.h, v))
    def dma(self, eng, out, in_, sem, **kw):
        sem.c += 16
        self.on(eng, lambda e, out=out, in_=in_, sem=sem, kw=kw:
                e.dma_start(out=out, in_=in_, **kw).then_inc(sem.h, 16))
    def run(self, eng, fn, sem=None, n=1):
        if sem is not None:
            sem.c += n
            self.on(eng, lambda e, fn=fn, sem=sem, n=n: fn(e).then_inc(sem.h, n))
        else:
            self.on(eng, fn)
    def emit(self):
        with self.nc.Block() as blk:
            for name in self.ENGS:
                dec = getattr(blk, name)
                thunks = self.q[name]
                def body(e, thunks=thunks):
                    for th in thunks:
                        th(e)
                dec(body)

def emit_P0(p, t, sb, ps):
    V = sb
    qs = p.sem("qs"); qg = p.sem("qg"); tm = p.sem("tm"); vm = p.sem("vm"); sm = p.sem("sm")

    for nm in ["identf", "ident16", "J16", "itan_re", "itan_im", "c_re", "c_im",
               "W1re", "W1imneg", "W1im", "twre", "twim", "W2re", "W2imneg",
               "VcReT", "VcImT", "Bp", "dD", "lstep", "ones1f"]:
        p.dma("sync", sb[nm][:, :], t[nm][:, :], qs)
    for i, nm in enumerate(["lamRe", "lamIm", "CtRe", "CtIm", "pRe", "pIm", "qRe", "qIm"]):
        p.dma("sync", sb["sm64"][0:64, i:i+1], t[nm][:, :], qs)
    ld_all = qs.c

    psB = ps[5][0:64, 0:1]
    psB2 = ps[5][0:64, 4:5]
    p.wait("tensor", qs)
    p.run("tensor", lambda e: e.matmul(psB, sb["VcReT"][:, :], sb["Bp"][:, :], start=True, stop=True))
    p.run("tensor", lambda e: e.matmul(psB2, sb["VcImT"][:, :], sb["Bp"][:, :], start=True, stop=True), tm)
    p.wait("scalar", tm)
    p.run("scalar", lambda e: e.copy(out=sb["sm64"][0:64, 8:9], in_=psB), sm)
    p.run("scalar", lambda e: e.copy(out=sb["sm64"][0:64, 9:10], in_=psB2), sm)

    S = sb["sm64"]
    def col(i): return S[0:64, i:i+1]
    TMP = sb["sm64t"]
    def tc(i): return TMP[0:64, i:i+1]
    p.wait("vector", sm)
    p.wait("vector", qs, ld_all)
    prods = [
        (10, 2, 8, 3, 9, OP.add), (11, 2, 9, 3, 8, OP.subtract),
        (12, 2, 4, 3, 5, OP.add), (13, 2, 5, 3, 4, OP.subtract),
        (14, 6, 8, 7, 9, OP.add), (15, 6, 9, 7, 8, OP.subtract),
        (16, 6, 4, 7, 5, OP.add), (17, 6, 5, 7, 4, OP.subtract),
    ]
    for i, (dst, a1, b1, a2, b2, op2) in enumerate(prods):
        p.run("vector", lambda e, a=col(a1), b=col(b1), i=i: e.tensor_tensor(out=tc(2*i), in0=a, in1=b, op=OP.mult))
        p.run("vector", lambda e, a=col(a2), b=col(b2), i=i: e.tensor_tensor(out=tc(2*i+1), in0=a, in1=b, op=OP.mult))
    for i, (dst, a1, b1, a2, b2, op2) in enumerate(prods):
        p.run("vector", lambda e, d=col(dst), op2=op2, i=i: e.tensor_tensor(out=d, in0=tc(2*i), in1=tc(2*i+1), op=op2), vm)

    p.wait("tensor", vm)
    p.run("tensor", lambda e: e.transpose(ps[2][0:18, 0:64], S[0:64, 0:18], sb["identf"][0:64, 0:64]), tm)
    p.wait("scalar", tm)
    p.run("scalar", lambda e: e.copy(out=sb["smT"][0:18, :], in_=ps[2][0:18, 0:64]), sm)
    p.wait("gpsimd", sm)
    p.dma("gpsimd", t["smTd"][:, :], sb["smT"][0:18, :], qg)
    p.wait("sync", qg)
    bc_rows = {"lam_re": 0, "lam_im": 1, "v00re": 10, "v00im": 11, "v01re": 12, "v01im": 13,
               "v10re": 14, "v10im": 15, "v11re": 16, "v11im": 17}
    for nm, row in bc_rows.items():
        p.dma("sync", sb[nm][:, :], t["smTd"][row:row+1, :].partition_broadcast(128), qs)
    p.wait("vector", qs)
    for nm in ["v00re", "v01re", "v10re", "v11re"]:
        p.run("vector", lambda e, nm=nm: e.tensor_scalar(out=sb[nm + "n"][:, :], in0=sb[nm][:, :], scalar1=-1.0, scalar2=None, op0=OP.mult))

    p.wait("scalar", qs, ld_all)
    p.run("scalar", lambda e: e.activation(out=sb["step"][0:1, 0:1], in_=sb["lstep"][0:1, 0:1], func=AF.Exp), sm)
    p.wait("vector", sm)
    p.run("vector", lambda e: e.reciprocal(out=sb["step"][0:1, 1:2], in_=sb["step"][0:1, 0:1]))
    p.run("vector", lambda e: e.tensor_copy(out=sb["vjunk"][:, :], in_=sb["vjunk"][:, :]))
    p.run("vector", lambda e: e.tensor_scalar(out=sb["step"][0:1, 2:3], in0=sb["step"][0:1, 1:2], scalar1=2.0, scalar2=None, op0=OP.mult), vm)
    p.wait("gpsimd", vm)
    p.dma("gpsimd", t["stepd"][:, :], sb["step"][0:1, 2:3], qg)
    p.wait("sync", qg)
    p.dma("sync", sb["r2bc"][:, :], t["stepd"][0:1, 0:1].partition_broadcast(128), qs)

    p.wait("vector", qs)
    for nm in ["k00re", "k00im", "k01re", "k01im", "k10re", "k10im", "k11re", "k11im"]:
        p.run("vector", lambda e, nm=nm: e.memset(V[nm][:, :], 0.0))
    p.run("vector", lambda e: e.tensor_scalar(out=V["g_re"][:, :], in0=V["itan_re"][:, :], scalar1=V["r2bc"][:, 0:1], scalar2=None, op0=OP.mult))
    p.run("vector", lambda e: e.tensor_scalar(out=V["g_im"][:, :], in0=V["itan_im"][:, :], scalar1=V["r2bc"][:, 0:1], scalar2=None, op0=OP.mult))
    bufs = {}
    for side in "AB":
        bufs[side] = {nm: V[nm + side] for nm in ["dre", "dim", "u1", "u2", "inv"]}
    def cauchy_half(e_ops, n, B):
        dre, dim_, u1, u2, inv = B["dre"], B["dim"], B["u1"], B["u2"], B["inv"]
        o = []
        o.append(lambda e: e.tensor_scalar(out=dre[:, :], in0=V["g_re"][:, :], scalar1=V["lam_re"][:, n:n+1], scalar2=None, op0=OP.subtract))
        o.append(lambda e: e.tensor_scalar(out=dim_[:, :], in0=V["g_im"][:, :], scalar1=V["lam_im"][:, n:n+1], scalar2=None, op0=OP.subtract))
        o.append(lambda e: e.tensor_tensor(out=u1[:, :], in0=dre[:, :], in1=dre[:, :], op=OP.mult))
        o.append(lambda e: e.tensor_tensor(out=u2[:, :], in0=dim_[:, :], in1=dim_[:, :], op=OP.mult))
        o.append(lambda e: e.tensor_tensor(out=u1[:, :], in0=u1[:, :], in1=u2[:, :], op=OP.add))
        o.append(lambda e: e.reciprocal(out=inv[:, :], in_=u1[:, :]))
        o.append(lambda e: e.tensor_tensor(out=dre[:, :], in0=dre[:, :], in1=inv[:, :], op=OP.mult))
        o.append(lambda e: e.tensor_tensor(out=dim_[:, :], in0=dim_[:, :], in1=inv[:, :], op=OP.mult))
        quads = [("k00re", "k00im", "v00re", "v00im", "v00ren"),
                 ("k01re", "k01im", "v01re", "v01im", "v01ren"),
                 ("k10re", "k10im", "v10re", "v10im", "v10ren"),
                 ("k11re", "k11im", "v11re", "v11im", "v11ren")]
        for kre, kim, vre, vim, vren in quads:
            o.append(lambda e, kre=kre, vre=vre: e.scalar_tensor_tensor(out=V[kre][:, :], in0=dre[:, :], scalar=V[vre][:, n:n+1], in1=V[kre][:, :], op0=OP.mult, op1=OP.add))
            o.append(lambda e, kim=kim, vim=vim: e.scalar_tensor_tensor(out=V[kim][:, :], in0=dre[:, :], scalar=V[vim][:, n:n+1], in1=V[kim][:, :], op0=OP.mult, op1=OP.add))
        for kre, kim, vre, vim, vren in quads:
            o.append(lambda e, kre=kre, vim=vim: e.scalar_tensor_tensor(out=V[kre][:, :], in0=dim_[:, :], scalar=V[vim][:, n:n+1], in1=V[kre][:, :], op0=OP.mult, op1=OP.add))
            o.append(lambda e, kim=kim, vren=vren: e.scalar_tensor_tensor(out=V[kim][:, :], in0=dim_[:, :], scalar=V[vren][:, n:n+1], in1=V[kim][:, :], op0=OP.mult, op1=OP.add))
        return o
    for pair in range(NSTATE // 2):
        oa = cauchy_half(None, 2 * pair, bufs["A"])
        ob = cauchy_half(None, 2 * pair + 1, bufs["B"])
        for fa, fb in zip(oa, ob):
            p.run("vector", fa)
            p.run("vector", fb)
    R = lambda i: V["r%02d" % i]
    seq = [
        lambda e: e.tensor_tensor(out=R(1)[:, :], in0=V["k01re"][:, :], in1=V["k10re"][:, :], op=OP.mult),
        lambda e: e.tensor_tensor(out=R(2)[:, :], in0=V["k01im"][:, :], in1=V["k10im"][:, :], op=OP.mult),
        lambda e: e.tensor_tensor(out=R(3)[:, :], in0=V["k01re"][:, :], in1=V["k10im"][:, :], op=OP.mult),
        lambda e: e.tensor_tensor(out=R(4)[:, :], in0=V["k01im"][:, :], in1=V["k10re"][:, :], op=OP.mult),
        lambda e: e.tensor_scalar(out=R(5)[:, :], in0=V["k11re"][:, :], scalar1=1.0, scalar2=None, op0=OP.add),
        lambda e: e.tensor_tensor(out=V["wre"][:, :], in0=R(1)[:, :], in1=R(2)[:, :], op=OP.subtract),
        lambda e: e.tensor_tensor(out=V["wim"][:, :], in0=R(3)[:, :], in1=R(4)[:, :], op=OP.add),
        lambda e: e.tensor_tensor(out=R(6)[:, :], in0=R(5)[:, :], in1=R(5)[:, :], op=OP.mult),
        lambda e: e.tensor_tensor(out=R(7)[:, :], in0=V["k11im"][:, :], in1=V["k11im"][:, :], op=OP.mult),
        lambda e: e.tensor_tensor(out=R(1)[:, :], in0=V["wre"][:, :], in1=R(5)[:, :], op=OP.mult),
        lambda e: e.tensor_tensor(out=R(8)[:, :], in0=R(6)[:, :], in1=R(7)[:, :], op=OP.add),
        lambda e: e.tensor_tensor(out=R(2)[:, :], in0=V["wim"][:, :], in1=V["k11im"][:, :], op=OP.mult),
        lambda e: e.reciprocal(out=V["inv" + "A"][:, :], in_=R(8)[:, :]),
        lambda e: e.tensor_tensor(out=R(3)[:, :], in0=V["wim"][:, :], in1=R(5)[:, :], op=OP.mult),
        lambda e: e.tensor_tensor(out=R(4)[:, :], in0=V["wre"][:, :], in1=V["k11im"][:, :], op=OP.mult),
        lambda e: e.tensor_tensor(out=R(6)[:, :], in0=R(1)[:, :], in1=R(2)[:, :], op=OP.add),
        lambda e: e.tensor_tensor(out=R(7)[:, :], in0=R(3)[:, :], in1=R(4)[:, :], op=OP.subtract),
        lambda e: e.tensor_tensor(out=V["ure"][:, :], in0=R(6)[:, :], in1=V["invA"][:, :], op=OP.mult),
        lambda e: e.tensor_tensor(out=V["uim"][:, :], in0=R(7)[:, :], in1=V["invA"][:, :], op=OP.mult),
        lambda e: e.tensor_tensor(out=R(1)[:, :], in0=V["k00re"][:, :], in1=V["ure"][:, :], op=OP.subtract),
        lambda e: e.tensor_tensor(out=R(2)[:, :], in0=V["k00im"][:, :], in1=V["uim"][:, :], op=OP.subtract),
        lambda e: e.tensor_tensor(out=R(3)[:, :], in0=V["c_re"][:, :], in1=R(1)[:, :], op=OP.mult),
        lambda e: e.tensor_tensor(out=R(4)[:, :], in0=V["c_im"][:, :], in1=R(2)[:, :], op=OP.mult),
        lambda e: e.tensor_tensor(out=R(5)[:, :], in0=V["c_re"][:, :], in1=R(2)[:, :], op=OP.mult),
        lambda e: e.tensor_tensor(out=R(6)[:, :], in0=V["c_im"][:, :], in1=R(1)[:, :], op=OP.mult),
        lambda e: e.tensor_tensor(out=V["atre"][:, :], in0=R(3)[:, :], in1=R(4)[:, :], op=OP.subtract),
        lambda e: e.tensor_tensor(out=V["atim"][:, :], in0=R(5)[:, :], in1=R(6)[:, :], op=OP.add),
    ]
    for i, f in enumerate(seq):
        p.run("vector", f, vm if i == len(seq) - 1 else None)

    psGre = ps[0][:, 0:32]
    psGim = ps[1][:, 0:32]
    p.wait("tensor", vm)
    p.run("tensor", lambda e: e.matmul(psGre, sb["W1re"][:, :], V["atre"][:, :], start=True, stop=False))
    p.run("tensor", lambda e: e.matmul(psGre, sb["W1imneg"][:, :], V["atim"][:, :], start=False, stop=True))
    p.run("tensor", lambda e: e.matmul(psGim, sb["W1im"][:, :], V["atre"][:, :], start=True, stop=False))
    p.run("tensor", lambda e: e.matmul(psGim, sb["W1re"][:, :], V["atim"][:, :], start=False, stop=True), tm)
    p.wait("scalar", tm)
    p.run("scalar", lambda e: e.copy(out=V["Gre"][:, :], in_=psGre), sm)
    p.run("scalar", lambda e: e.copy(out=V["Gim"][:, :], in_=psGim), sm)
    p.wait("vector", sm)
    p.run("vector", lambda e: e.tensor_tensor(out=R(1)[:, :], in0=V["Gre"][:, :], in1=V["twre"][:, :], op=OP.mult))
    p.run("vector", lambda e: e.tensor_tensor(out=R(2)[:, :], in0=V["Gim"][:, :], in1=V["twim"][:, :], op=OP.mult))
    p.run("vector", lambda e: e.tensor_tensor(out=R(3)[:, :], in0=V["Gre"][:, :], in1=V["twim"][:, :], op=OP.mult))
    p.run("vector", lambda e: e.tensor_tensor(out=R(4)[:, :], in0=V["Gim"][:, :], in1=V["twre"][:, :], op=OP.mult))
    p.run("vector", lambda e: e.tensor_tensor(out=V["Gpre"][:, :], in0=R(1)[:, :], in1=R(2)[:, :], op=OP.subtract))
    p.run("vector", lambda e: e.tensor_tensor(out=V["Gpim"][:, :], in0=R(3)[:, :], in1=R(4)[:, :], op=OP.add), vm)
    p.wait("tensor", vm)
    p.run("tensor", lambda e: e.transpose(ps[2][0:32, 0:128], V["Gpre"][:, :], sb["identf"][:, :]))
    p.run("tensor", lambda e: e.transpose(ps[3][0:32, 0:128], V["Gpim"][:, :], sb["identf"][:, :]), tm)
    p.wait("scalar", tm)
    p.run("scalar", lambda e: e.copy(out=V["GTre"][0:32, :], in_=ps[2][0:32, 0:128]), sm)
    p.run("scalar", lambda e: e.copy(out=V["GTim"][0:32, :], in_=ps[3][0:32, 0:128]), sm)
    psK = ps[4][0:32, 0:128]
    p.wait("tensor", sm)
    p.run("tensor", lambda e: e.matmul(psK, sb["W2re"][0:32, :], V["GTre"][0:32, :], start=True, stop=False))
    p.run("tensor", lambda e: e.matmul(psK, sb["W2imneg"][0:32, :], V["GTim"][0:32, :], start=False, stop=True), tm)
    p.wait("scalar", tm)
    p.run("scalar", lambda e: e.copy(out=V["Ksb"][0:32, :], in_=psK), sm)
    p.wait("vector", sm)
    p.run("vector", lambda e: e.tensor_tensor(out=V["Ksb"][0:1, 0:1], in0=V["Ksb"][0:1, 0:1], in1=sb["dD"][0:1, 0:1], op=OP.add))
    p.run("vector", lambda e: e.memset(V["z16"][0:1, :], 0.0))
    p.run("vector", lambda e: e.tensor_copy(out=V["K16"][0:32, :], in_=V["Ksb"][0:32, :]), vm)
    p.wait("gpsimd", vm)
    p.dma("gpsimd", t["Kpad"][0:1, 0:127], V["z16"][0:1, 0:127], qg)
    p.dma("gpsimd", t["Kpad"][0:1, 127:4223].rearrange("one (c p) -> (one c) p", p=128), V["K16"][0:32, :], qg)
    p.wait("sync", qg)
    if "Kdump" in t:
        p.dma("sync", t["Kdump"][:, :], t["Kpad"][:, :], qs)
    for d in range(32):
        ap_in = AP(t["Kpad_t"], 128 * d, [[1, 128], [1, 128]])
        p.dma("sync", sb["w0sb"][:, 128*d:128*(d+1)], ap_in, qs)
    p.wait("tensor", qs)
    for d in range(32):
        pb = (ps[6] if d % 2 == 0 else ps[7])[:, 0:128]
        if d >= 2:
            p.wait("tensor", sm, sm.c - 1)
        p.run("tensor", lambda e, d=d, pb=pb: e.matmul(pb, sb["J16"][:, :], sb["w0sb"][:, 128*d:128*(d+1)], start=True, stop=True), tm)
        p.wait("scalar", tm)
        p.run("scalar", lambda e, d=d, pb=pb: e.copy(out=sb["Ttiles"][:, 128*d:128*(d+1)], in_=pb), sm)

    if DEV_DUMP:
        p.wait("gpsimd", vm)
        p.wait("gpsimd", sm)
        p.dma("gpsimd", t["dbg_sm64"][:, :], sb["sm64"][0:64, :], qg)
        p.dma("gpsimd", t["dbg_w0"][:, :], t["w0cache"][5, :, :], qg)
        p.dma("gpsimd", t["dbg_lamre"][:, :], sb["lam_re"][:, :], qg)
        p.dma("gpsimd", t["dbg_v00re"][:, :], sb["v00re"][:, :], qg)
        p.dma("gpsimd", t["dbg_gre"][:, :], sb["g_re"][:, :], qg)
        p.dma("gpsimd", t["dbg_gim"][:, :], sb["g_im"][:, :], qg)
        p.dma("gpsimd", t["dbg_k00re"][:, :], sb["k00re"][:, :], qg)
        p.dma("gpsimd", t["dbg_dre"][:, :], sb["dreB"][:, :], qg)
        p.dma("gpsimd", t["dbg_dim"][:, :], sb["dimB"][:, :], qg)
        p.dma("gpsimd", t["dbg_inv"][:, :], sb["invB"][:, :], qg)
        p.dma("gpsimd", t["dbg_k00im"][:, :], sb["k00im"][:, :], qg)
        p.dma("gpsimd", t["dbg_k11re"][:, :], sb["k11re"][:, :], qg)
        p.dma("gpsimd", t["dbg_atre"][:, :], sb["atre"][:, :], qg)
        p.dma("gpsimd", t["dbg_atim"][:, :], sb["atim"][:, :], qg)
        p.dma("gpsimd", t["dbg_Gre"][:, :], sb["Gre"][:, :], qg)
        p.dma("gpsimd", t["dbg_Ksb"][:, :], sb["Ksb"][0:32, :], qg)
    p.wait("tensor", sm)
    for k in range(NFC):
        dqf = sb["big1"]
        wout = sb["w0out"]
        if k >= 1:
            p.wait("gpsimd", vm)
            p.wait("sync", tm)
        p.dma("gpsimd", dqf[:, :], t["qwT"][128*k:128*(k+1), :], qg, max_dma_last_dim=8192)
        gq = qg.c
        p.wait("vector", qg, gq)
        for j in range(8):
            pb6 = ps[6][:, 0:512]
            pb7 = ps[7][:, 0:512]
            p.wait("sync", tm)
            p.dma("sync", sb["scrow"][0:1, 0:512], t["scalesT"][k:k+1, 512*j:512*(j+1)], qs)
            p.dma("sync", sb["zrrow"][0:1, 0:512], t["zerosT"][k:k+1, 512*j:512*(j+1)], qs)
            p.wait("tensor", qs, qs.c)
            if k > 0 or j > 0:
                p.wait("tensor", vm, vm.c)
            p.run("tensor", lambda e, j=j, pb6=pb6: e.matmul(pb6, sb["ones1f"][0:1, 0:128], sb["scrow"][0:1, 0:512], start=True, stop=True))
            p.run("tensor", lambda e, j=j, pb7=pb7: e.matmul(pb7, sb["ones1f"][0:1, 0:128], sb["zrrow"][0:1, 0:512], start=True, stop=True), tm)
            p.wait("vector", tm)
            p.run("vector", lambda e, j=j, dqf=dqf, pb6=pb6: e.tensor_tensor(out=sb["dqt"][:, :], in0=dqf[:, 512*j:512*(j+1)], in1=pb6, op=OP.mult))
            p.run("vector", lambda e, j=j, wout=wout, pb7=pb7: e.tensor_tensor(out=wout[:, 512*j:512*(j+1)], in0=sb["dqt"][:, :], in1=pb7, op=OP.add), vm)
        p.wait("gpsimd", vm)
        p.dma("gpsimd", t["w0cache"][k, :, :], wout[:, :], qg)

def emit_L0(p, t, sb, ps):
    V = sb
    qs = p.sem("qs"); qg = p.sem("qg"); tm = p.sem("tm"); vm = p.sem("vm"); sm = p.sem("sm")
    p.dma("sync", sb["bias_sb"][:, :], t["bias_dev"][:, :], qs)
    p.wait("sync", sm)
    p.wait("sync", vm)

    inv_dm = 1.0 / DM
    for rb in range(NBLK):
        r0 = rb * BLK
        p.wait("sync", vm); p.wait("sync", qg)
        for j in range(4):
            p.dma("sync", sb["ur"][j][:, :], t["u_dev"][r0+128*j:r0+128*(j+1), :], qs)
        uld = qs.c
        p.wait("scalar", qs, uld)
        p.wait("vector", qs, uld)
        for j in range(4):
            p.run("scalar", lambda e, j=j: e.activation(out=sb["big1"][:, :], in_=sb["ur"][j][:, :], func=AF.Square, accum_out=sb["ssq"][:, j:j+1]), sm)
            p.run("vector", lambda e, j=j: e.reduce_sum(out=sb["ssum"][:, j:j+1], in_=sb["ur"][j][:, :], axis=AX.X), vm)
        p.wait("vector", sm)
        ME, EX, VA = sb["mcol"], sb["ecol"], sb["vcol"]
        for j in range(4):
            p.run("vector", lambda e, j=j: e.tensor_scalar(out=ME[:, j:j+1], in0=sb["ssum"][:, j:j+1], scalar1=inv_dm, scalar2=None, op0=OP.mult))
        for j in range(4):
            p.run("vector", lambda e, j=j: e.tensor_scalar(out=EX[:, j:j+1], in0=sb["ssq"][:, j:j+1], scalar1=inv_dm, scalar2=None, op0=OP.mult))
        for j in range(4):
            p.run("vector", lambda e, j=j: e.tensor_tensor(out=VA[:, j:j+1], in0=ME[:, j:j+1], in1=ME[:, j:j+1], op=OP.mult))
        for j in range(4):
            p.run("vector", lambda e, j=j: e.tensor_tensor(out=VA[:, j:j+1], in0=EX[:, j:j+1], in1=VA[:, j:j+1], op=OP.subtract))
        for j in range(4):
            p.run("vector", lambda e, j=j: e.tensor_scalar(out=VA[:, j:j+1], in0=VA[:, j:j+1], scalar1=EPS, scalar2=None, op0=OP.add), vm)
        p.wait("scalar", vm)
        for j in range(4):
            p.run("scalar", lambda e, j=j: e.activation(out=sb["scol"][:, j:j+1], in_=sb["vcol"][:, j:j+1], func=AF.Sqrt), sm)
        p.wait("vector", sm)
        for j in range(4):
            p.run("vector", lambda e, j=j: e.reciprocal(out=sb["rstd"][:, j:j+1], in_=sb["scol"][:, j:j+1]))
        for j in range(4):
            p.run("vector", lambda e, j=j: e.scalar_tensor_tensor(out=sb["bneg"][:, j:j+1], in0=sb["mcol"][:, j:j+1], scalar=-1.0, in1=sb["rstd"][:, j:j+1], op0=OP.mult, op1=OP.mult), vm)
        sm_tp = None
        for j in range(4):
            p.run("vector", lambda e, j=j: e.tensor_scalar(out=sb["xn16"][:, :], in0=sb["ur"][j][:, :], scalar1=sb["rstd"][:, j:j+1], scalar2=sb["bneg"][:, j:j+1], op0=OP.mult, op1=OP.add), vm)
            if DEV_DUMP and rb == 0 and j == 2:
                p.wait("gpsimd", vm)
                p.dma("gpsimd", t["dbg_xn"][:, :], sb["xn16"][:, :], qg)
                p.wait("vector", qg)
            p.wait("tensor", vm)
            for fc in range(NFC):
                pb = (ps[5] if fc % 2 == 0 else ps[6]).bitcast(BF16)[:, 0:128]
                if fc >= 2 or j > 0:
                    p.wait("tensor", sm, sm.c - 1)
                p.run("tensor", lambda e, fc=fc, pb=pb: e.transpose(pb, sb["xn16"][:, 128*fc:128*(fc+1)], sb["ident16"][:, :]), tm)
                p.wait("scalar", tm)
                p.run("scalar", lambda e, fc=fc, j=j, pb=pb: e.copy(out=sb["xnT"][:, 512*fc+128*j:512*fc+128*(j+1)], in_=pb), sm)
            p.wait("vector", sm)
        for c in range(NFC):
            psA = ps[0] if c % 2 == 0 else ps[1]
            psB_ = ps[2] if c % 2 == 0 else ps[3]
            w0b = sb["w0sb"] if c % 2 == 0 else sb["w0sb2"]
            if c >= 2:
                p.wait("sync", tm, tm.c)
            p.dma("sync", w0b[:].rearrange("p (k n) -> p k n", k=NFC),
                  t["w0cache"][:, :, 128*c:128*(c+1)].transpose([1, 0, 2]), qs)
            wld = qs.c
            if c >= 2:
                p.wait("tensor", vm, vm.c)
            for ci in range(c + 1):
                d = c - ci
                if ci < c:
                    p.run("tensor", lambda e, d=d, ci=ci, psA=psA, c=c: e.matmul(psA[:, :], sb["Ttiles"][:, 128*d:128*(d+1)], sb["xnT"][:, 512*ci:512*(ci+1)], start=(ci == 0), stop=False))
                else:
                    p.run("tensor", lambda e, d=d, ci=ci, psA=psA, c=c: e.matmul(psA[:, :], sb["Ttiles"][:, 128*d:128*(d+1)], sb["xnT"][:, 512*ci:512*(ci+1)], start=(ci == 0), stop=True), tm)
            p.wait("tensor", qs, wld)
            for k in range(NFC):
                if k < NFC - 1:
                    p.run("tensor", lambda e, k=k, psB_=psB_, w0b=w0b: e.matmul(psB_[:, :], w0b[:, 128*k:128*(k+1)], sb["xnT"][:, 512*k:512*(k+1)], start=(k == 0), stop=False))
                else:
                    p.run("tensor", lambda e, k=k, psB_=psB_, w0b=w0b: e.matmul(psB_[:, :], w0b[:, 128*k:128*(k+1)], sb["xnT"][:, 512*k:512*(k+1)], start=(k == 0), stop=True), tm)
            p.wait("scalar", tm)
            p.run("scalar", lambda e, c=c, psB_=psB_: e.activation(out=sb["gt"][:, :], in_=psB_[:, :], func=AF.Silu, bias=sb["bias_sb"][:, c:c+1], scale=1.0), sm)
            p.wait("vector", sm)
            if DEV_DUMP and rb == 0 and c == 3:
                p.run("scalar", lambda e, psA=psA: e.copy(out=sb["big1"][:, 0:512], in_=psA[:, :]), sm)
                p.wait("gpsimd", sm)
                p.dma("gpsimd", t["dbg_st"][:, :], sb["big1"][:, 0:512], qg)
                p.dma("gpsimd", t["dbg_gt"][:, :], sb["gt"][:, :], qg)
            p.run("vector", lambda e, psA=psA: e.tensor_tensor(out=sb["sgt"][:, :], in0=sb["gt"][:, :], in1=psA[:, :], op=OP.mult), vm)
            p.wait("tensor", vm)
            for j in range(4):
                if j < 3:
                    p.run("tensor", lambda e, j=j: e.transpose(ps[4][:, 128*j:128*(j+1)], sb["sgt"][:, 128*j:128*(j+1)], sb["identf"][:, :]))
                else:
                    p.run("tensor", lambda e, j=j: e.transpose(ps[4][:, 128*j:128*(j+1)], sb["sgt"][:, 128*j:128*(j+1)], sb["identf"][:, :]), tm)
            p.wait("vector", tm)
            for j in range(4):
                p.run("vector", lambda e, c=c, j=j: e.tensor_tensor(out=sb["ur"][j][:, 128*c:128*(c+1)], in0=sb["ur"][j][:, 128*c:128*(c+1)], in1=ps[4][:, 128*j:128*(j+1)], op=OP.add), vm)
        p.wait("gpsimd", vm)
        for j in range(4):
            p.dma("gpsimd", t["x1_dev"][r0+128*j:r0+128*(j+1), :], sb["ur"][j][:, :], qg)

def build_A(consts):
    nc = bass.Bass("TRN2", target_bir_lowering=False)
    es = ExitStack()
    t = {}
    def din(name, shape, dt):
        h = nc.dram_tensor(name, shape, dt, kind="ExternalInput")
        t[name] = h.ap()
        return h
    def dout(name, shape, dt):
        t[name] = nc.dram_tensor(name, shape, dt, kind="ExternalOutput").ap()
    def dint(name, shape, dt):
        h = nc.dram_tensor(name, shape, dt, kind="Internal")
        t[name] = h.ap()
        return h
    for nm, arr in consts.items():
        din(nm, list(arr.shape), BF16 if arr.dtype == BF else F32)
    for nm in ["lamRe", "lamIm", "CtRe", "CtIm", "pRe", "pIm", "qRe", "qIm"]:
        din(nm, [64, 1], F32)
    din("VcReT", [64, 64], F32); din("VcImT", [64, 64], F32)
    din("Bp", [64, 1], F32); din("dD", [1, 1], F32); din("lstep", [1, 1], F32)
    din("u_dev", [RPAD, DM], F32)
    din("qwT", [DM, DM], I32)
    din("scalesT", [32, DM], F32); din("zerosT", [32, DM], F32)
    din("bias_dev", [128, 32], F32)
    dout("x1_dev", [RPAD, DM], F32)
    dout("Kdump", [1, 4352], BF16)
    if DEV_DUMP:
        dout("dbg_sm64", [64, 20], F32); dout("dbg_lamre", [128, 64], F32)
        dout("dbg_v00re", [128, 64], F32)
        dout("dbg_gre", [128, 32], F32); dout("dbg_gim", [128, 32], F32)
        dout("dbg_k00re", [128, 32], F32); dout("dbg_k00im", [128, 32], F32)
        dout("dbg_dre", [128, 32], F32); dout("dbg_dim", [128, 32], F32); dout("dbg_inv", [128, 32], F32)
        dout("dbg_k11re", [128, 32], F32)
        dout("dbg_atre", [128, 32], F32); dout("dbg_atim", [128, 32], F32)
        dout("dbg_Gre", [128, 32], F32); dout("dbg_Ksb", [32, 128], F32)
        dout("dbg_xn", [128, 4096], BF16); dout("dbg_gt", [128, 512], F32); dout("dbg_st", [128, 512], F32)
        dout("dbg_w0", [128, 4096], BF16)
    kp = dint("Kpad", [1, 4352], BF16); t["Kpad_t"] = kp
    dint("smTd", [18, 64], F32); dint("stepd", [1, 1], F32)
    dint("w0cache", [NFC, 128, DM], BF16)

    sb = {}
    def asb(name, shape, dt):
        sb[name] = es.enter_context(nc.sbuf_tensor("s_" + name, shape, dt))
    for nm in ["itan_re", "itan_im", "c_re", "c_im"]:
        asb(nm, [128, 32], F32)
    asb("identf", [128, 128], F32); asb("ident16", [128, 128], BF16); asb("J16", [128, 128], BF16)
    asb("W1re", [128, 128], F32); asb("W1imneg", [128, 128], F32); asb("W1im", [128, 128], F32)
    asb("twre", [128, 32], F32); asb("twim", [128, 32], F32)
    asb("W2re", [32, 32], F32); asb("W2imneg", [32, 32], F32)
    asb("VcReT", [64, 64], F32); asb("VcImT", [64, 64], F32)
    asb("Bp", [64, 1], F32); asb("dD", [1, 1], F32); asb("lstep", [1, 1], F32)
    asb("ones1f", [1, 512], F32)
    asb("sm64", [64, 20], F32); asb("sm64t", [64, 16], F32); asb("smT", [18, 64], F32)
    asb("vjunk", [128, 32], F32)
    for nm in ["lam_re", "lam_im", "v00re", "v00im", "v01re", "v01im", "v10re", "v10im",
               "v11re", "v11im", "v00ren", "v01ren", "v10ren", "v11ren"]:
        asb(nm, [128, 64], F32)
    asb("step", [1, 4], F32); asb("r2bc", [128, 1], F32)
    for nm in ["k00re", "k00im", "k01re", "k01im", "k10re", "k10im", "k11re", "k11im",
               "g_re", "g_im", "dreA", "dimA", "u1A", "u2A", "invA",
               "dreB", "dimB", "u1B", "u2B", "invB", "wre", "wim",
               "ure", "uim", "atre", "atim", "Gre", "Gim", "Gpre", "Gpim",
               "r01", "r02", "r03", "r04", "r05", "r06", "r07", "r08"]:
        asb(nm, [128, 32], F32)
    asb("GTre", [32, 128], F32); asb("GTim", [32, 128], F32)
    asb("Ksb", [32, 128], F32); asb("K16", [32, 128], BF16); asb("z16", [1, 128], BF16)
    asb("Ttiles", [128, 4096], BF16)
    asb("big1", [128, 4096], F32)
    asb("w0out", [128, 4096], BF16)
    asb("scrow", [1, 512], F32); asb("zrrow", [1, 512], F32)
    asb("dqt", [128, 512], F32)
    sb["ur"] = [es.enter_context(nc.sbuf_tensor(f"ur{j}", [128, DM], F32)) for j in range(4)]
    asb("ssq", [128, 4], F32); asb("ssum", [128, 4], F32)
    asb("mcol", [128, 4], F32); asb("ecol", [128, 4], F32); asb("vcol", [128, 4], F32)
    asb("scol", [128, 4], F32); asb("rstd", [128, 4], F32); asb("bneg", [128, 4], F32)
    asb("xn16", [128, DM], BF16)
    asb("xnT", [128, NFC * BLK], BF16)
    asb("w0sb", [128, 4096], BF16); asb("w0sb2", [128, 4096], BF16)
    asb("bias_sb", [128, 32], F32)
    asb("gt", [128, 512], F32); asb("sgt", [128, 512], F32)

    ps_h = [es.enter_context(nc.psum_tensor(f"pb{i}", [128, 512], F32)) for i in range(8)]

    p = Prog(nc, es)
    sba = {k: (v if isinstance(v, list) else v[:, :]) for k, v in sb.items()}
    sba["ur"] = [x[:, :] for x in sb["ur"]]
    psa = [x[:, :] for x in ps_h]
    emit_P0(p, t, sba, psa)
    emit_L0(p, t, sba, psa)
    qg = p.sem("qg"); qs = p.sem("qs")
    p.wait("sync", qg)
    p.wait("sync", qs)
    p.emit()
    return nc, es

def _prep_A_inputs(inputs, consts):
    u = np.asarray(inputs["L0_attn_u"])
    base = dict(consts)
    lam = np.asarray(inputs["L0_attn_Lambda_ri"])
    base["lamRe"] = np.ascontiguousarray(lam[:, 0:1]); base["lamIm"] = np.ascontiguousarray(lam[:, 1:2])
    ct = np.asarray(inputs["L0_attn_Ct_ri"]).reshape(64, 2)
    base["CtRe"] = np.ascontiguousarray(ct[:, 0:1]); base["CtIm"] = np.ascontiguousarray(ct[:, 1:2])
    pp = np.asarray(inputs["L0_attn_p_ri"])
    base["pRe"] = np.ascontiguousarray(pp[:, 0:1]); base["pIm"] = np.ascontiguousarray(pp[:, 1:2])
    qq = np.asarray(inputs["L0_attn_q_ri"])
    base["qRe"] = np.ascontiguousarray(qq[:, 0:1]); base["qIm"] = np.ascontiguousarray(qq[:, 1:2])
    vc = np.asarray(inputs["L0_attn_Vc_ri"])
    base["VcReT"] = np.ascontiguousarray(vc[:, :, 0].T); base["VcImT"] = np.ascontiguousarray(vc[:, :, 1].T)
    base["Bp"] = np.ascontiguousarray(np.asarray(inputs["L0_attn_Bp"]))
    base["dD"] = np.asarray(inputs["L0_attn_D"]).reshape(1, 1).astype(np.float32)
    base["lstep"] = np.asarray(inputs["L0_attn_log_step"]).reshape(1, 1).astype(np.float32)
    qw = np.asarray(inputs["L0_ffn_qweight"]).reshape(4096, 4096)
    base["qwT"] = np.ascontiguousarray(qw.T)
    base["scalesT"] = np.ascontiguousarray(np.asarray(inputs["L0_ffn_scales"])[:, :, 0].T)
    base["zerosT"] = np.ascontiguousarray(np.asarray(inputs["L0_ffn_scaled_zeros"])[:, :, 0].T)
    base["bias_dev"] = np.ascontiguousarray(np.asarray(inputs["L0_ffn_bias"]).reshape(32, 128).T)
    in_maps = []
    for core in range(8):
        b = core // 2
        q0 = 2048 * (core % 2)
        uc = np.zeros((RPAD, DM), dtype=np.float32)
        s_lo = q0 - 256
        a0 = max(0, s_lo); a1 = min(SEQ, q0 + 2048 + 256)
        uc[a0 - s_lo: a1 - s_lo, :] = u[b, a0:a1, :]
        m = dict(base)
        m["u_dev"] = uc
        in_maps.append(m)
    return in_maps

_CONSTS = None
def _get_consts():
    global _CONSTS
    if _CONSTS is None:
        _CONSTS = _host_consts()
    return _CONSTS

def run_A(inputs):
    consts = _get_consts()
    in_maps = _prep_A_inputs(inputs, consts)
    nc, es = build_A(consts)
    res = run_bass_kernel_spmd(nc, in_maps, core_ids=list(range(8)))
    es.close()
    if DEV_DUMP:
        return [r["x1_dev"] for r in res.results], [r["Kdump"] for r in res.results], res.results[0]
    return [r["x1_dev"] for r in res.results], [r["Kdump"] for r in res.results]


def emit_B(p, t, sb, ps):
    V = sb
    qs = p.sem("qs"); qg = p.sem("qg"); tm = p.sem("tm"); vm = p.sem("vm"); sm = p.sem("sm")
    NKC = RPAD // 128

    for nm in ["ident16", "identf", "J2", "ones16", "ones1f", "maskb", "alphas",
               "cosQ", "sinQ", "cosK", "sinK"]:
        p.dma("sync", sb[nm][:, :], t[nm][:, :], qs)
    p.dma("gpsimd", sb["wk_sb"][:].rearrange("p (k d) -> p k d", k=NFC),
          t["WkT"].rearrange("(k p) d -> p k d", p=128), qg, max_dma_last_dim=512)
    p.dma("gpsimd", sb["wv_sb"][:].rearrange("p (k d) -> p k d", k=NFC),
          t["WvT"].rearrange("(k p) d -> p k d", p=128), qg, max_dma_last_dim=512)
    p.dma("gpsimd", t["wq_bf"][:, :], t["WqT"][:, :], qg, max_dma_last_dim=8192)
    p.dma("gpsimd", t["wo_bf"][:, :], t["WoT"][:, :], qg, max_dma_last_dim=8192)

    inv_dm = 1.0 / DM
    bigbuf = sb["bigbuf"]
    urv = [bigbuf[:, 4096*j:4096*(j+1)] for j in range(4)]
    big16 = sb["bigbuf_bf"]
    Qr = big16[:, 0:16384]
    AO = big16[:, 16384:32768]

    for rb in range(NBLK):
        r0 = rb * BLK
        p.wait("sync", vm); p.wait("sync", qg); p.wait("sync", tm)
        for j in range(4):
            p.dma("sync", urv[j], t["x1_dev"][r0+128*j:r0+128*(j+1), :], qs)
        uld = qs.c
        p.wait("scalar", qs, uld)
        p.wait("vector", qs, uld)
        for j in range(4):
            p.run("scalar", lambda e, j=j: e.activation(out=sb["xn16"][:, :], in_=urv[j], func=AF.Square, accum_out=sb["ssq"][:, j:j+1]), sm)
            p.run("vector", lambda e, j=j: e.reduce_sum(out=sb["ssum"][:, j:j+1], in_=urv[j], axis=AX.X), vm)
        p.wait("vector", sm)
        ME, EX, VA = sb["mcol"], sb["ecol"], sb["vcol"]
        for j in range(4):
            p.run("vector", lambda e, j=j: e.tensor_scalar(out=ME[:, j:j+1], in0=sb["ssum"][:, j:j+1], scalar1=inv_dm, scalar2=None, op0=OP.mult))
        for j in range(4):
            p.run("vector", lambda e, j=j: e.tensor_scalar(out=EX[:, j:j+1], in0=sb["ssq"][:, j:j+1], scalar1=inv_dm, scalar2=None, op0=OP.mult))
        for j in range(4):
            p.run("vector", lambda e, j=j: e.tensor_tensor(out=VA[:, j:j+1], in0=ME[:, j:j+1], in1=ME[:, j:j+1], op=OP.mult))
        for j in range(4):
            p.run("vector", lambda e, j=j: e.tensor_tensor(out=VA[:, j:j+1], in0=EX[:, j:j+1], in1=VA[:, j:j+1], op=OP.subtract))
        for j in range(4):
            p.run("vector", lambda e, j=j: e.tensor_scalar(out=VA[:, j:j+1], in0=VA[:, j:j+1], scalar1=EPS, scalar2=None, op0=OP.add), vm)
        p.wait("scalar", vm)
        for j in range(4):
            p.run("scalar", lambda e, j=j: e.activation(out=sb["scol"][:, j:j+1], in_=sb["vcol"][:, j:j+1], func=AF.Sqrt), sm)
        p.wait("vector", sm)
        for j in range(4):
            p.run("vector", lambda e, j=j: e.reciprocal(out=sb["rstd"][:, j:j+1], in_=sb["scol"][:, j:j+1]))
        for j in range(4):
            p.run("vector", lambda e, j=j: e.scalar_tensor_tensor(out=sb["bneg"][:, j:j+1], in0=sb["mcol"][:, j:j+1], scalar=-1.0, in1=sb["rstd"][:, j:j+1], op0=OP.mult, op1=OP.mult), vm)
        for j in range(4):
            p.run("vector", lambda e, j=j: e.tensor_scalar(out=sb["xn16"][:, :], in0=urv[j], scalar1=sb["rstd"][:, j:j+1], scalar2=sb["bneg"][:, j:j+1], op0=OP.mult, op1=OP.add), vm)
            p.wait("tensor", vm)
            for fc in range(NFC):
                pb = (ps[5] if fc % 2 == 0 else ps[6])[:, :].bitcast(BF16)[:, 0:128]
                if fc >= 2 or j > 0:
                    p.wait("tensor", sm, sm.c - 1)
                p.run("tensor", lambda e, fc=fc, pb=pb: e.transpose(pb, sb["xn16"][:, 128*fc:128*(fc+1)], sb["ident16"][:, :]), tm)
                p.wait("scalar", tm)
                p.run("scalar", lambda e, fc=fc, j=j, pb=pb: e.copy(out=sb["xnTb"][:, 512*fc+128*j:512*fc+128*(j+1)], in_=pb), sm)
            p.wait("vector", sm)
        p.wait("gpsimd", sm)
        p.dma("gpsimd", t["xn1T"][:, :, r0:r0+BLK].transpose([1, 0, 2]),
              sb["xnTb"][:].rearrange("p (c n) -> p c n", c=NFC), qg)
        psK_ = ps[0]; psV_ = ps[1]; psJ = ps[2]
        p.wait("tensor", sm)
        for k in range(NFC):
            p.run("tensor", lambda e, k=k: e.matmul(psK_[:, :], sb["wk_sb"][:, 128*k:128*(k+1)], sb["xnTb"][:, 512*k:512*(k+1)], start=(k == 0), stop=(k == NFC-1)), tm if k == NFC-1 else None)
        for k in range(NFC):
            p.run("tensor", lambda e, k=k: e.matmul(psV_[:, :], sb["wv_sb"][:, 128*k:128*(k+1)], sb["xnTb"][:, 512*k:512*(k+1)], start=(k == 0), stop=(k == NFC-1)), tm if k == NFC-1 else None)
        p.wait("scalar", tm)
        p.wait("scalar", vm)
        p.run("scalar", lambda e: e.copy(out=sb["ktmp"][:, :], in_=psK_[:, :]), sm)
        p.run("scalar", lambda e: e.copy(out=sb["vtmp"][:, :], in_=psV_[:, :]), sm)
        p.wait("tensor", sm)
        p.run("tensor", lambda e: e.matmul(psJ[:, :].bitcast(BF16)[:, 0:512] if False else psJ[:, :], sb["J2f"][:, :], sb["ktmp32"][:, :], start=True, stop=True) if False else e.matmul(psJ[:, :], sb["J2"][:, :], sb["ktmp"][:, :], start=True, stop=True), tm)
        p.wait("scalar", tm)
        p.run("scalar", lambda e: e.copy(out=sb["kstmp"][:, :], in_=psJ[:, :]), sm)
        p.wait("vector", sm)
        p.run("vector", lambda e, r0=r0: e.tensor_tensor(out=sb["q1t"][:, :], in0=sb["ktmp"][:, :], in1=sb["cosK"][:, r0:r0+BLK], op=OP.mult))
        p.run("vector", lambda e, r0=r0: e.tensor_tensor(out=sb["q2t"][:, :], in0=sb["kstmp"][:, :], in1=sb["sinK"][:, r0:r0+BLK], op=OP.mult))
        p.run("vector", lambda e, r0=r0: e.tensor_tensor(out=sb["K_sb"][:, r0:r0+BLK], in0=sb["q1t"][:, :], in1=sb["q2t"][:, :], op=OP.add), vm)
        p.wait("tensor", sm)
        for j in range(4):
            pb = (ps[3] if j % 2 == 0 else ps[4])[:, :].bitcast(BF16)[:, 0:128]
            if j >= 2:
                p.wait("tensor", sm, sm.c - 1)
            p.run("tensor", lambda e, j=j, pb=pb: e.transpose(pb, sb["vtmp"][:, 128*j:128*(j+1)], sb["ident16"][:, :]), tm)
            p.wait("scalar", tm)
            p.run("scalar", lambda e, j=j, pb=pb, rb=rb: e.copy(out=sb["V_sb"][:, 128*(4*rb+j):128*(4*rb+j+1)], in_=pb), sm)

    for qb in range(4):
        r0 = OWN0 + qb * BLK
        p.wait("sync", tm); p.wait("sync", vm); p.wait("sync", qg)
        p.dma("sync", sb["xnTb"][:].rearrange("p (c n) -> p c n", c=NFC),
              t["xn1T"][:, :, r0:r0+BLK].transpose([1, 0, 2]), qs)
        xld = qs.c
        for hc in range(NFC):
            wqb = sb["wq_sb"] if hc % 2 == 0 else sb["wq_sb2"]
            if hc >= 2:
                p.wait("sync", tm)
            p.dma("sync", wqb[:].rearrange("p (k n) -> p k n", k=NFC),
                  t["wq_bf"].rearrange("(k p) o -> k p o", p=128)[:, :, 128*hc:128*(hc+1)].transpose([1, 0, 2]), qs)
            wld = qs.c
            psQ = ps[0] if hc % 2 == 0 else ps[1]
            psJ = ps[2] if hc % 2 == 0 else ps[3]
            p.wait("tensor", qs, wld)
            if hc >= 2:
                p.wait("tensor", sm)
            for k in range(NFC):
                p.run("tensor", lambda e, k=k, psQ=psQ, wqb=wqb: e.matmul(psQ[:, :], wqb[:, 128*k:128*(k+1)], sb["xnTb"][:, 512*k:512*(k+1)], start=(k == 0), stop=(k == NFC-1)), tm if k == NFC-1 else None)
            p.wait("scalar", tm)
            p.wait("scalar", vm)
            p.run("scalar", lambda e, psQ=psQ: e.copy(out=sb["ktmp"][:, :], in_=psQ[:, :]), sm)
            p.wait("tensor", sm)
            p.run("tensor", lambda e, psJ=psJ: e.matmul(psJ[:, :], sb["J2"][:, :], sb["ktmp"][:, :], start=True, stop=True), tm)
            p.wait("scalar", tm)
            p.run("scalar", lambda e, psJ=psJ: e.copy(out=sb["kstmp"][:, :], in_=psJ[:, :]), sm)
            p.wait("vector", sm)
            p.run("vector", lambda e, r0=r0: e.tensor_tensor(out=sb["q1t"][:, :], in0=sb["ktmp"][:, :], in1=sb["cosQ"][:, r0:r0+BLK], op=OP.mult))
            p.run("vector", lambda e, r0=r0: e.tensor_tensor(out=sb["q2t"][:, :], in0=sb["kstmp"][:, :], in1=sb["sinQ"][:, r0:r0+BLK], op=OP.mult))
            p.run("vector", lambda e, hc=hc: e.tensor_tensor(out=Qr[:, 512*hc:512*(hc+1)], in0=sb["q1t"][:, :], in1=sb["q2t"][:, :], op=OP.add), vm)
        for hc in range(NFC):
            for m2 in range(2):
                tl = 2 * qb + m2
                lc0 = 2 * tl
                qcols = slice(512*hc + 256*m2, 512*hc + 256*(m2+1))
                psS = [ps[0], ps[1], ps[2]]
                psSum = ps[3]; psL = ps[4]; psR = ps[5]; psC = ps[6]
                p.wait("tensor", vm)
                for ck in range(6):
                    pb = psS[ck // 2][:, 256*(ck % 2):256*(ck % 2 + 1)]
                    p.run("tensor", lambda e, ck=ck, pb=pb, lc0=lc0, qcols=qcols: e.matmul(pb, sb["K_sb"][:, 128*(lc0+ck):128*(lc0+ck+1)], Qr[:, qcols], start=True, stop=True), tm if ck == 5 else None)
                p.wait("scalar", tm)
                for ck in range(6):
                    pb = psS[ck // 2][:, 256*(ck % 2):256*(ck % 2 + 1)]
                    p.run("scalar", lambda e, ck=ck, pb=pb, lc0=lc0: e.activation(out=sb["attnT"][:, 256*ck:256*(ck+1)], in_=pb, func=AF.Sigmoid, bias=sb["maskb"][:, lc0+ck:lc0+ck+1], scale=1.0), sm)
                p.wait("tensor", sm)
                for i, ck in enumerate([0, 1, 2, 3]):
                    p.run("tensor", lambda e, ck=ck, i=i: e.matmul(psSum[0:1, 0:256], sb["ones16"][:, 0:1], sb["attnT"][:, 256*ck:256*(ck+1)], start=(i == 0), stop=(i == 3)))
                for i, ck in enumerate([2, 3, 4, 5]):
                    p.run("tensor", lambda e, ck=ck, i=i: e.matmul(psSum[0:1, 256:512], sb["ones16"][:, 0:1], sb["attnT"][:, 256*ck:256*(ck+1)], start=(i == 0), stop=(i == 3)), tm if i == 3 else None)
                for i, ck in enumerate([0, 1, 2, 3]):
                    p.run("tensor", lambda e, ck=ck, i=i, lc0=lc0: e.matmul(psL[:, 0:256], sb["V_sb"][:, 128*(lc0+ck):128*(lc0+ck+1)], sb["attnT"][:, 256*ck:256*(ck+1)], start=(i == 0), stop=(i == 3)))
                for i, ck in enumerate([2, 3, 4, 5]):
                    p.run("tensor", lambda e, ck=ck, i=i, lc0=lc0: e.matmul(psR[:, 0:256], sb["V_sb"][:, 128*(lc0+ck):128*(lc0+ck+1)], sb["attnT"][:, 256*ck:256*(ck+1)], start=(i == 0), stop=(i == 3)), tm if i == 3 else None)
                p.wait("vector", tm, tm.c - 1)
                p.run("vector", lambda e: e.reciprocal(out=sb["rec"][0:1, 0:512], in_=psSum[0:1, 0:512]))
                p.run("vector", lambda e: e.tensor_copy(out=sb["vjunk"][:, :], in_=sb["vjunk"][:, :]))
                p.run("vector", lambda e, tl=tl: e.tensor_tensor(out=sb["cc"][0:1, 0:512], in0=sb["rec"][0:1, 0:512], in1=sb["alphas"][0:1, 512*tl:512*(tl+1)], op=OP.mult), vm)
                p.wait("tensor", vm)
                p.run("tensor", lambda e: e.matmul(psC[:, :], sb["ones1f"][0:1, 0:128], sb["cc"][0:1, 0:512], start=True, stop=True), tm)
                p.wait("scalar", tm)
                p.run("scalar", lambda e: e.copy(out=sb["oL"][:, :], in_=psL[:, 0:256]), sm)
                p.run("scalar", lambda e: e.copy(out=sb["oR"][:, :], in_=psR[:, 0:256]), sm)
                p.wait("vector", sm)
                p.run("vector", lambda e: e.tensor_tensor(out=sb["f1"][:, :], in0=sb["oL"][:, :], in1=psC[:, 0:256], op=OP.mult))
                p.run("vector", lambda e: e.tensor_tensor(out=sb["f2"][:, :], in0=sb["oR"][:, :], in1=psC[:, 256:512], op=OP.mult))
                p.run("vector", lambda e, qcols=qcols: e.tensor_tensor(out=AO[:, qcols], in0=sb["f1"][:, :], in1=sb["f2"][:, :], op=OP.add), vm)
        if DEV_DUMP and qb == 0:
            p.wait("gpsimd", vm)
            p.dma("gpsimd", t["dbg_Qr"][:, :], Qr, qg)
            p.dma("gpsimd", t["dbg_AO"][:, :], AO, qg)
            p.dma("gpsimd", t["dbg_K"][:, :], sb["K_sb"][:, :], qg)
            p.dma("gpsimd", t["dbg_V"][:, :], sb["V_sb"][:, :], qg)
            p.wait("sync", qg)
        for oc in range(NFC):
            wqb = sb["wq_sb"] if oc % 2 == 0 else sb["wq_sb2"]
            if True:
                p.wait("sync", tm)
            p.dma("sync", wqb[:].rearrange("p (k n) -> p k n", k=NFC),
                  t["wo_bf"].rearrange("(k p) o -> k p o", p=128)[:, :, 128*oc:128*(oc+1)].transpose([1, 0, 2]), qs)
            wld = qs.c
            psO = ps[0] if oc % 2 == 0 else ps[1]
            p.wait("tensor", qs, wld)
            if oc >= 2:
                p.wait("tensor", sm)
            for k in range(NFC):
                p.run("tensor", lambda e, k=k, psO=psO, wqb=wqb: e.matmul(psO[:, :], wqb[:, 128*k:128*(k+1)], AO[:, 512*k:512*(k+1)], start=(k == 0), stop=(k == NFC-1)), tm if k == NFC-1 else None)
            p.wait("scalar", tm)
            p.wait("scalar", qg)
            p.run("scalar", lambda e, psO=psO: e.copy(out=sb["s1chunk"][:, :], in_=psO[:, :]), sm)
            p.wait("gpsimd", sm)
            p.dma("gpsimd", t["s1T"][oc, :, 512*qb:512*(qb+1)], sb["s1chunk"][:, :], qg)

def build_B(consts):
    nc = bass.Bass("TRN2", target_bir_lowering=False)
    es = ExitStack()
    t = {}
    def din(name, shape, dt):
        h = nc.dram_tensor(name, shape, dt, kind="ExternalInput")
        t[name] = h.ap()
        return h
    def dout(name, shape, dt):
        t[name] = nc.dram_tensor(name, shape, dt, kind="ExternalOutput").ap()
    def dint(name, shape, dt):
        h = nc.dram_tensor(name, shape, dt, kind="Internal")
        t[name] = h.ap()
        return h
    for nm in ["ident16", "identf", "J2", "ones16"]:
        arr = consts[nm]
        din(nm, list(arr.shape), BF16 if arr.dtype == BF else F32)
    din("ones1f", [1, 512], F32)
    din("x1_dev", [RPAD, DM], F32)
    din("WqT", [DM, DM], F32); din("WoT", [DM, DM], F32)
    din("WkT", [DM, 128], F32); din("WvT", [DM, 128], F32)
    din("cosQ", [128, RPAD], BF16); din("sinQ", [128, RPAD], BF16)
    din("cosK", [128, RPAD], BF16); din("sinK", [128, RPAD], BF16)
    din("maskb", [128, 20], F32)
    din("alphas", [1, 16 * 256], F32)
    dout("s1T", [NFC, 128, NOWN], F32)
    dout("xn1T_out", [NFC, 128, RPAD], BF16)
    if DEV_DUMP:
        dout("dbg_Qr", [128, 16384], BF16); dout("dbg_AO", [128, 16384], BF16)
        dout("dbg_K", [128, RPAD], BF16); dout("dbg_V", [128, RPAD], BF16)
    t["xn1T"] = t["xn1T_out"]
    dint("wq_bf", [DM, DM], BF16); dint("wo_bf", [DM, DM], BF16)

    sb = {}
    def asb(name, shape, dt):
        sb[name] = es.enter_context(nc.sbuf_tensor("s_" + name, shape, dt))
    asb("ident16", [128, 128], BF16); asb("identf", [128, 128], F32)
    asb("J2", [128, 128], BF16); asb("ones16", [128, 1], BF16); asb("ones1f", [1, 512], F32)
    asb("maskb", [128, 20], F32); asb("alphas", [1, 16 * 256], F32)
    asb("cosQ", [128, RPAD], BF16); asb("sinQ", [128, RPAD], BF16)
    asb("cosK", [128, RPAD], BF16); asb("sinK", [128, RPAD], BF16)
    asb("wk_sb", [128, DM], BF16); asb("wv_sb", [128, DM], BF16)
    asb("bigbuf", [128, 16384], F32)
    asb("xnTb", [128, NFC * BLK], BF16)
    asb("xn16", [128, DM], BF16)
    asb("ssq", [128, 4], F32); asb("ssum", [128, 4], F32)
    asb("mcol", [128, 4], F32); asb("ecol", [128, 4], F32); asb("vcol", [128, 4], F32)
    asb("scol", [128, 4], F32); asb("rstd", [128, 4], F32); asb("bneg", [128, 4], F32)
    asb("ktmp", [128, BLK], BF16); asb("kstmp", [128, BLK], BF16); asb("vtmp", [128, BLK], BF16)
    asb("q1t", [128, BLK], BF16); asb("q2t", [128, BLK], BF16)
    asb("K_sb", [128, RPAD], BF16); asb("V_sb", [128, RPAD], BF16)
    asb("wq_sb", [128, DM], BF16); asb("wq_sb2", [128, DM], BF16)
    asb("attnT", [128, 6 * 256], BF16)
    asb("rec", [1, 512], F32); asb("cc", [1, 512], F32)
    asb("oL", [128, 256], F32); asb("oR", [128, 256], F32)
    asb("f1", [128, 256], F32); asb("f2", [128, 256], F32)
    asb("s1chunk", [128, BLK], F32)
    asb("vjunk", [128, 32], F32)
    ps_h = [es.enter_context(nc.psum_tensor(f"pb{i}", [128, 512], F32)) for i in range(8)]

    p = Prog(nc, es)
    sba = {k: v[:, :] for k, v in sb.items()}
    sba["bigbuf_bf"] = sb["bigbuf"][:, :].bitcast(BF16)
    psa = [x[:, :] for x in ps_h]
    emit_B(p, t, sba, psa)
    qg = p.sem("qg"); qs = p.sem("qs")
    p.wait("sync", qg)
    p.wait("sync", qs)
    p.emit()
    return nc, es

def _rope_tables():
    inv_freq = 1.0 / (10000.0 ** (np.arange(0, 128, 2, dtype=np.float32) / 128.0))
    tt = np.arange(SEQ, dtype=np.float32)
    f = np.outer(tt, inv_freq)
    cos = np.zeros((SEQ, 128), dtype=np.float32)
    sin = np.zeros((SEQ, 128), dtype=np.float32)
    cos[:, 0::2] = np.cos(f); cos[:, 1::2] = np.cos(f)
    sin[:, 0::2] = np.sin(f); sin[:, 1::2] = np.sin(f)
    sign = np.where(np.arange(128) < 64, -1.0, 1.0).astype(np.float32)
    return cos.T.copy(), (sin * sign[None, :]).T.copy()

def _prep_B_inputs(inputs, consts, x1_list):
    SCALE = 1.0 / np.sqrt(128.0)
    cosT, sinT = _rope_tables()
    base = {nm: consts[nm] for nm in ["ident16", "identf", "J2", "ones16"]}
    base["ones1f"] = consts["ones1f"]
    base["WqT"] = np.ascontiguousarray(np.asarray(inputs["L1_attn_Wq"]).T)
    base["WoT"] = np.ascontiguousarray(np.asarray(inputs["L1_attn_Wo"]).T)
    base["WkT"] = np.ascontiguousarray(np.asarray(inputs["L1_attn_Wk"]).T)
    base["WvT"] = np.ascontiguousarray(np.asarray(inputs["L1_attn_Wv"]).T)
    alpha = (np.arange(256, dtype=np.float32) / 255.0)
    in_maps = []
    for core in range(8):
        q0 = 2048 * (core % 2)
        s_lo = q0 - 256
        idx = np.clip(np.arange(s_lo, s_lo + RPAD), 0, SEQ - 1)
        m = dict(base)
        m["cosQ"] = np.ascontiguousarray(cosT[:, idx] * SCALE).astype(BF)
        m["sinQ"] = np.ascontiguousarray(sinT[:, idx] * SCALE).astype(BF)
        m["cosK"] = np.ascontiguousarray(cosT[:, idx]).astype(BF)
        m["sinK"] = np.ascontiguousarray(sinT[:, idx]).astype(BF)
        mb = np.zeros((128, 20), dtype=np.float32)
        if core % 2 == 0:
            mb[:, 0:2] = -30000.0
        else:
            mb[:, 18:20] = -30000.0
        m["maskb"] = mb
        al = np.zeros((16, 256), dtype=np.float32)
        for tl in range(8):
            mq = 8 * (core % 2) + tl
            if mq == 0:
                al[2*tl, :] = 0.0; al[2*tl+1, :] = 1.0
            else:
                al[2*tl, :] = 1.0 - alpha; al[2*tl+1, :] = alpha
        m["alphas"] = al.reshape(1, -1)
        m["x1_dev"] = x1_list[core]
        in_maps.append(m)
    return in_maps

def run_B(inputs, x1_list):
    consts = _get_consts()
    in_maps = _prep_B_inputs(inputs, consts, x1_list)
    nc, es = build_B(consts)
    res = run_bass_kernel_spmd(nc, in_maps, core_ids=list(range(8)))
    es.close()
    if DEV_DUMP:
        return ([r["s1T"] for r in res.results], [r["xn1T_out"] for r in res.results], res.results[0])
    return ([r["s1T"] for r in res.results], [r["xn1T_out"] for r in res.results])


def emit_C(p, t, sb, ps):
    qs = p.sem("qs"); qg = p.sem("qg"); tm = p.sem("tm"); vm = p.sem("vm"); sm = p.sem("sm")

    p.dma("sync", sb["shifts"][:, :], t["shifts"][:, :], qs)
    p.dma("sync", sb["ones1f"][:, :], t["ones1f"][:, :], qs)
    for blk in range(4):
        c0 = 688 * blk
        f0 = 5504 * blk
        p.wait("sync", vm); p.wait("sync", qg)
        p.dma("sync", sb["qzg"][0:32, :], t["qzeros_gu"][:, c0:c0+688], qs)
        p.dma("sync", sb["scg"][0:32, :], t["scales_gu"][:, f0:f0+5504], qs)
        p.wait("vector", qs)
        for kk in range(8):
            p.run("vector", lambda e, kk=kk: e.tensor_scalar(out=sb["zqg"][0:32, kk::8], in0=sb["qzg"][0:32, :], scalar1=4*kk, scalar2=15, op0=OP.logical_shift_right, op1=OP.bitwise_and), vm)
        p.wait("scalar", vm)
        p.run("scalar", lambda e: e.copy(out=sb["zqgf"][0:32, :], in_=sb["zqg"][0:32, :]), sm)
        p.wait("vector", sm)
        p.run("vector", lambda e: e.scalar_tensor_tensor(out=sb["zqgf"][0:32, :], in0=sb["zqgf"][0:32, :], scalar=1.0, in1=sb["scg"][0:32, :], op0=OP.add, op1=OP.mult), vm)
        p.wait("gpsimd", vm)
        p.dma("gpsimd", t["zb_gu"][:, f0:f0+5504], sb["zqgf"][0:32, :], qg)
    for g0, gn in [(0, 32), (32, 32), (64, 22)]:
        p.wait("sync", vm); p.wait("sync", qg)
        p.dma("sync", sb["qzd"][0:gn, :], t["qzeros_down"][g0:g0+gn, :], qs)
        p.dma("sync", sb["scd"][0:gn, 0:4096], t["scales_down"][g0:g0+gn, :], qs)
        p.wait("vector", qs)
        for kk in range(8):
            p.run("vector", lambda e, kk=kk, gn=gn: e.tensor_scalar(out=sb["zqd"][0:gn, kk::8], in0=sb["qzd"][0:gn, :], scalar1=4*kk, scalar2=15, op0=OP.logical_shift_right, op1=OP.bitwise_and), vm)
        p.wait("scalar", vm)
        p.run("scalar", lambda e, gn=gn: e.copy(out=sb["zqdf"][0:gn, 0:4096], in_=sb["zqd"][0:gn, :]), sm)
        p.wait("vector", sm)
        p.run("vector", lambda e, gn=gn: e.scalar_tensor_tensor(out=sb["zqdf"][0:gn, 0:4096], in0=sb["zqdf"][0:gn, 0:4096], scalar=1.0, in1=sb["scd"][0:gn, 0:4096], op0=OP.add, op1=OP.mult), vm)
        p.wait("gpsimd", vm)
        p.dma("gpsimd", t["zb_d"][g0:g0+gn, :], sb["zqdf"][0:gn, 0:4096], qg)

    if DEV_DUMP:
        p.wait("sync", qg)
        p.dma("sync", t["dbg_zbd"][:, :], t["zb_d"][:, :], qs)
        p.dma("sync", t["dbg_zbgu"][:, :], t["zb_gu"][:, :], qs)
    def deq_slice(qw_t, row0, col0, w2, sc_t, zb_t, grow, dst):
        p.wait("gpsimd", vm)
        p.wait("gpsimd", sm)
        p.dma("gpsimd", sb["dqu"][:, 0:w2], qw_t[row0:row0+128, col0:col0+w2], qg)
        p.wait("sync", tm)
        p.dma("sync", sb["scrow"][0:1, 0:w2], sc_t[grow:grow+1, col0:col0+w2], qs)
        p.dma("sync", sb["zbrow"][0:1, 0:w2], zb_t[grow:grow+1, col0:col0+w2], qs)
        gq = qg.c; sq_ = qs.c
        p.wait("vector", qg, gq)
        p.run("vector", lambda e: e.tensor_scalar(out=sb["dqu"][:, 0:w2], in0=sb["dqu"][:, 0:w2], scalar1=sb["shifts"][:, 0:1], scalar2=15, op0=OP.logical_shift_right, op1=OP.bitwise_and), vm)
        p.wait("scalar", vm)
        p.run("scalar", lambda e: e.copy(out=sb["dqf"][:, 0:w2], in_=sb["dqu"][:, 0:w2]), sm)
        pb6 = ps[6][:, 0:w2]; pb7 = ps[7][:, 0:w2]
        p.wait("tensor", qs, sq_)
        p.wait("tensor", vm, vm.c)
        p.wait("tensor", sm)
        p.run("tensor", lambda e, pb6=pb6: e.matmul(pb6, sb["ones1f"][0:1, 0:128], sb["scrow"][0:1, 0:w2], start=True, stop=True))
        p.run("tensor", lambda e, pb7=pb7: e.matmul(pb7, sb["ones1f"][0:1, 0:128], sb["zbrow"][0:1, 0:w2], start=True, stop=True), tm)
        p.wait("vector", tm)
        p.wait("vector", sm)
        p.run("vector", lambda e, pb6=pb6: e.tensor_tensor(out=sb["dqt"][:, 0:w2], in0=sb["dqf"][:, 0:w2], in1=pb6, op=OP.mult))
        p.run("vector", lambda e, pb7=pb7: e.tensor_tensor(out=dst, in0=sb["dqt"][:, 0:w2], in1=pb7, op=OP.subtract), vm)

    p.wait("sync", qg)
    for f in range(NGF):
        for j in range(8):
            deq_slice(t["qweight_down"], 128*f, 512*j, 512, t["scales_down"], t["zb_d"], f,
                      sb["wdout"][:, 512*j:512*(j+1)])
        p.wait("gpsimd", vm)
        p.dma("gpsimd", t["wd_bf"][f, :, :], sb["wdout"][:, :], qg)

    p.wait("sync", qg); p.wait("sync", vm)
    p.dma("sync", sb["xq"][:].rearrange("p (c n) -> p c n", c=NFC),
          t["xn1T"][:, :, OWN0:OWN0+NOWN].transpose([1, 0, 2]), qs)
    xld = qs.c

    for i in range(43):
        f0g = 256 * i
        p.wait("vector", tm)
        for k in range(NFC):
            deq_slice(t["qweight_gu"], 128*k, f0g, 256, t["scales_gu"], t["zb_gu"], k,
                      sb["wg_sb"][:, 256*k:256*(k+1)])
            deq_slice(t["qweight_gu"], 128*k, INTER + f0g, 256, t["scales_gu"], t["zb_gu"], k,
                      sb["wu_sb"][:, 256*k:256*(k+1)])
        p.wait("tensor", qs, xld)
        p.wait("tensor", vm)
        for fc in range(2):
            for rt in range(4):
                psG = ps[0] if rt % 2 == 0 else ps[1]
                psU = ps[2] if rt % 2 == 0 else ps[3]
                p.wait("tensor", vm, vm.c)
                for k in range(NFC):
                    p.run("tensor", lambda e, k=k, fc=fc, rt=rt, psG=psG: e.matmul(psG[:, :], sb["wg_sb"][:, 256*k+128*fc:256*k+128*(fc+1)], sb["xq"][:, 2048*k+512*rt:2048*k+512*(rt+1)], start=(k == 0), stop=(k == NFC-1)), tm if k == NFC-1 else None)
                p.wait("scalar", tm)
                p.wait("scalar", vm)
                p.run("scalar", lambda e, psG=psG: e.activation(out=sb["gsil"][:, :], in_=psG[:, :], func=AF.Silu), sm)
                for k in range(NFC):
                    p.run("tensor", lambda e, k=k, fc=fc, rt=rt, psU=psU: e.matmul(psU[:, :], sb["wu_sb"][:, 256*k+128*fc:256*k+128*(fc+1)], sb["xq"][:, 2048*k+512*rt:2048*k+512*(rt+1)], start=(k == 0), stop=(k == NFC-1)), tm if k == NFC-1 else None)
                p.wait("vector", tm)
                p.wait("vector", sm)
                p.wait("vector", qg)
                p.run("vector", lambda e, psU=psU: e.tensor_tensor(out=sb["h16"][:, :], in0=sb["gsil"][:, :], in1=psU[:, :], op=OP.mult), vm)
                p.wait("gpsimd", vm)
                fci = 2 * i + fc
                p.dma("gpsimd", t["h_scr"][fci, :, 512*rt:512*(rt+1)], sb["h16"][:, :], qg)

def build_C(consts):
    nc = bass.Bass("TRN2", target_bir_lowering=False)
    es = ExitStack()
    t = {}
    def din(name, shape, dt):
        h = nc.dram_tensor(name, shape, dt, kind="ExternalInput")
        t[name] = h.ap()
        return h
    def dout(name, shape, dt):
        t[name] = nc.dram_tensor(name, shape, dt, kind="ExternalOutput").ap()
    def dint(name, shape, dt):
        h = nc.dram_tensor(name, shape, dt, kind="Internal")
        t[name] = h.ap()
        return h
    din("ones1f", [1, 512], F32)
    din("shifts", [128, 1], U32)
    din("xn1T", [NFC, 128, RPAD], BF16)
    din("qweight_gu", [4096, 2*INTER], U32)
    din("qzeros_gu", [32, 2752], U32)
    din("scales_gu", [32, 2*INTER], F32)
    din("qweight_down", [11008, DM], U32)
    din("qzeros_down", [NGF, 512], U32)
    din("scales_down", [NGF, DM], F32)
    dout("h_scr", [NGF, 128, NOWN], BF16)
    dout("wd_bf", [NGF, 128, DM], BF16)
    dint("zb_gu", [32, 2*INTER], F32)
    dint("zb_d", [NGF, DM], F32)
    if DEV_DUMP:
        dout("dbg_zbd", [NGF, DM], F32)
        dout("dbg_zbgu", [32, 2*INTER], F32)
    sb = {}
    def asb(name, shape, dt):
        sb[name] = es.enter_context(nc.sbuf_tensor("s_" + name, shape, dt))
    asb("ones1f", [1, 512], F32); asb("shifts", [128, 1], U32)
    asb("dqu", [128, 512], U32); asb("dqf", [128, 512], F32)
    asb("scrow", [1, 4096], F32); asb("zbrow", [1, 4096], F32)
    asb("dqt", [128, 512], F32)
    asb("wg_sb", [128, NFC * 256], BF16); asb("wu_sb", [128, NFC * 256], BF16)
    asb("xq", [128, NFC * 2048], BF16)
    asb("gsil", [128, 512], F32); asb("h16", [128, 512], BF16)
    asb("vjunk", [128, 32], F32)
    ps_h = [es.enter_context(nc.psum_tensor(f"pb{i}", [128, 512], F32)) for i in range(8)]
    p = Prog(nc, es)
    sba = {k: v[:, :] for k, v in sb.items()}
    xq_u = sb["xq"][:, :].bitcast(U32)
    xq_f = sb["xq"][:, :].bitcast(F32)
    sba["qzg"] = xq_u[:, 0:688]
    sba["zqg"] = xq_u[:, 1024:1024+5504]
    sba["zqgf"] = xq_f[:, 8192:8192+5504]
    sba["scg"] = xq_f[:, 16384:16384+5504]
    sba["qzd"] = xq_u[:, 0:512]
    sba["zqd"] = xq_u[:, 1024:1024+4096]
    sba["zqdf"] = xq_f[:, 8192:8192+4096]
    sba["scd"] = xq_f[:, 16384:16384+4096]
    sba["wdout"] = sb["wg_sb"][:, :].bitcast(BF16)[:, 0:4096]
    psa = [x[:, :] for x in ps_h]
    emit_C(p, t, sba, psa)
    qg = p.sem("qg"); qs = p.sem("qs")
    p.wait("sync", qg); p.wait("sync", qs)
    p.emit()
    return nc, es


def run_C(inputs, xn1T_list):
    consts = _get_consts()
    gsel_gu = np.asarray(inputs["L1_ffn_g_idx_gu"]).reshape(-1, 128)[:, 0]
    gsel_d = np.asarray(inputs["L1_ffn_g_idx_down"]).reshape(-1, 128)[:, 0]
    base = {
        "ones1f": consts["ones1f"],
        "shifts": (4 * (np.arange(128, dtype=np.uint32) % 8)).reshape(128, 1),
        "qweight_gu": np.repeat(np.asarray(inputs["L1_ffn_qweight_gu"]), 8, axis=0),
        "qzeros_gu": np.ascontiguousarray(np.asarray(inputs["L1_ffn_qzeros_gu"])[gsel_gu]),
        "scales_gu": np.ascontiguousarray(np.asarray(inputs["L1_ffn_scales_gu"])[gsel_gu]),
        "qweight_down": np.repeat(np.asarray(inputs["L1_ffn_qweight_down"]), 8, axis=0),
        "qzeros_down": np.ascontiguousarray(np.asarray(inputs["L1_ffn_qzeros_down"])[gsel_d]),
        "scales_down": np.ascontiguousarray(np.asarray(inputs["L1_ffn_scales_down"])[gsel_d]),
    }
    in_maps = []
    for core in range(8):
        m = dict(base)
        m["xn1T"] = xn1T_list[core]
        in_maps.append(m)
    nc, es = build_C(consts)
    res = run_bass_kernel_spmd(nc, in_maps, core_ids=list(range(8)))
    es.close()
    if DEV_DUMP:
        return ([r["h_scr"] for r in res.results], [r["wd_bf"] for r in res.results], res.results[0])
    return ([r["h_scr"] for r in res.results], [r["wd_bf"] for r in res.results])

def emit_D(p, t, sb, ps):
    qs = p.sem("qs"); qg = p.sem("qg"); tm = p.sem("tm"); vm = p.sem("vm"); sm = p.sem("sm")
    p.dma("sync", sb["identf"][:, :], t["identf"][:, :], qs)
    for rt in range(4):
        r0 = OWN0 + 512 * rt
        p.wait("sync", vm); p.wait("sync", qg); p.wait("sync", tm)
        p.dma("sync", sb["h_sb"][:].rearrange("p (c n) -> p c n", c=NGF),
              t["h_scr"][:, :, 512*rt:512*(rt+1)].transpose([1, 0, 2]), qs)
        for j in range(4):
            p.dma("sync", sb["xr"][j][:, :], t["x1_dev"][r0+128*j:r0+128*(j+1), :], qs)
        hld = qs.c
        for oc in range(NFC):
            wdb = sb["wd_sb"] if oc % 2 == 0 else sb["wd_sb2"]
            if oc >= 1 or rt >= 1:
                p.wait("sync", tm)
            p.dma("sync", wdb[:].rearrange("p (c n) -> p c n", c=NGF),
                  t["wd_bf"][:, :, 128*oc:128*(oc+1)].transpose([1, 0, 2]), qs)
            p.wait("gpsimd", vm)
            p.dma("gpsimd", sb["s1c"][:, :] if oc % 2 == 0 else sb["s1c2"][:, :], t["s1T"][oc, :, 512*rt:512*(rt+1)], qg)
            wld = qs.c; sld = qg.c
            psD = ps[0] if oc % 2 == 0 else ps[1]
            p.wait("tensor", qs, wld)
            if oc >= 2:
                p.wait("tensor", vm, vm.c)
            for f in range(NGF):
                p.run("tensor", lambda e, f=f, psD=psD, wdb=wdb: e.matmul(psD[:, :], wdb[:, 128*f:128*(f+1)], sb["h_sb"][:, 512*f:512*(f+1)], start=(f == 0), stop=(f == NGF-1)), tm if f == NGF-1 else None)
            p.wait("scalar", tm)
            p.wait("scalar", vm)
            p.run("scalar", lambda e, psD=psD: e.activation(out=sb["gd"][:, :], in_=psD[:, :], func=AF.Silu), sm)
            p.wait("vector", sm)
            p.wait("vector", qg, sld)
            p.run("vector", lambda e, oc=oc: e.tensor_tensor(out=sb["sgt"][:, :], in0=sb["gd"][:, :], in1=sb["s1c"][:, :] if oc % 2 == 0 else sb["s1c2"][:, :], op=OP.mult), vm)
            p.wait("tensor", vm)
            for j in range(4):
                p.run("tensor", lambda e, j=j: e.transpose(ps[4][:, 128*j:128*(j+1)], sb["sgt"][:, 128*j:128*(j+1)], sb["identf"][:, :]), tm if j == 3 else None)
            p.wait("vector", tm)
            for j in range(4):
                p.run("vector", lambda e, oc=oc, j=j: e.tensor_tensor(out=sb["xr"][j][:, 128*oc:128*(oc+1)], in0=sb["xr"][j][:, 128*oc:128*(oc+1)], in1=ps[4][:, 128*j:128*(j+1)], op=OP.add), vm)
        p.wait("gpsimd", vm)
        for j in range(4):
            p.dma("gpsimd", t["out_rows"][512*rt+128*j:512*rt+128*(j+1), :], sb["xr"][j][:, :], qg)

def build_D(consts):
    nc = bass.Bass("TRN2", target_bir_lowering=False)
    es = ExitStack()
    t = {}
    def din(name, shape, dt):
        h = nc.dram_tensor(name, shape, dt, kind="ExternalInput")
        t[name] = h.ap()
        return h
    def dout(name, shape, dt):
        t[name] = nc.dram_tensor(name, shape, dt, kind="ExternalOutput").ap()
    din("identf", [128, 128], F32)
    din("h_scr", [NGF, 128, NOWN], BF16)
    din("wd_bf", [NGF, 128, DM], BF16)
    din("s1T", [NFC, 128, NOWN], F32)
    din("x1_dev", [RPAD, DM], F32)
    dout("out_rows", [NOWN, DM], F32)
    sb = {}
    def asb(name, shape, dt):
        sb[name] = es.enter_context(nc.sbuf_tensor("s_" + name, shape, dt))
    asb("identf", [128, 128], F32)
    asb("h_sb", [128, NGF * 512], BF16)
    sb["xr"] = [es.enter_context(nc.sbuf_tensor(f"s_xr{j}", [128, DM], F32)) for j in range(4)]
    asb("wd_sb", [128, NGF * 128], BF16); asb("wd_sb2", [128, NGF * 128], BF16)
    asb("s1c", [128, 512], F32); asb("s1c2", [128, 512], F32)
    asb("gd", [128, 512], F32); asb("sgt", [128, 512], F32)
    asb("vjunk", [128, 32], F32)
    ps_h = [es.enter_context(nc.psum_tensor(f"pb{i}", [128, 512], F32)) for i in range(8)]
    p = Prog(nc, es)
    sba = {k: (v if isinstance(v, list) else v[:, :]) for k, v in sb.items()}
    sba["xr"] = [x[:, :] for x in sb["xr"]]
    psa = [x[:, :] for x in ps_h]
    emit_D(p, t, sba, psa)
    qg = p.sem("qg"); qs = p.sem("qs")
    p.wait("sync", qg); p.wait("sync", qs)
    p.emit()
    return nc, es

def run_D(inputs, x1_list, s1_list, h_list, wd_list):
    consts = _get_consts()
    in_maps = []
    for core in range(8):
        m = {"identf": consts["identf"], "h_scr": h_list[core], "wd_bf": wd_list[core],
             "s1T": s1_list[core], "x1_dev": x1_list[core]}
        in_maps.append(m)
    nc, es = build_D(consts)
    res = run_bass_kernel_spmd(nc, in_maps, core_ids=list(range(8)))
    es.close()
    return [r["out_rows"] for r in res.results]


_PHASE_ALLOCS = {
 "A": [("identf",[128,128],"F32"),("ident16",[128,128],"BF16"),("J16",[128,128],"BF16"),
  ("W1re",[128,128],"F32"),("W1imneg",[128,128],"F32"),("W1im",[128,128],"F32"),
  ("twre",[128,32],"F32"),("twim",[128,32],"F32"),("W2re",[32,32],"F32"),("W2imneg",[32,32],"F32"),
  ("VcReT",[64,64],"F32"),("VcImT",[64,64],"F32"),("Bp",[64,1],"F32"),("dD",[1,1],"F32"),("lstep",[1,1],"F32"),
  ("ones1f",[1,512],"F32"),("sm64",[64,20],"F32"),("sm64t",[64,16],"F32"),("smT",[18,64],"F32"),
  ("step",[1,4],"F32"),("r2bc",[128,1],"F32"),
  ("itan_re",[128,32],"F32"),("itan_im",[128,32],"F32"),("c_re",[128,32],"F32"),("c_im",[128,32],"F32"),
  ("lam_re",[128,64],"F32"),("lam_im",[128,64],"F32"),
  ("v00re",[128,64],"F32"),("v00im",[128,64],"F32"),("v01re",[128,64],"F32"),("v01im",[128,64],"F32"),
  ("v10re",[128,64],"F32"),("v10im",[128,64],"F32"),("v11re",[128,64],"F32"),("v11im",[128,64],"F32"),
  ("v00ren",[128,64],"F32"),("v01ren",[128,64],"F32"),("v10ren",[128,64],"F32"),("v11ren",[128,64],"F32"),
  ("k00re",[128,32],"F32"),("k00im",[128,32],"F32"),("k01re",[128,32],"F32"),("k01im",[128,32],"F32"),
  ("k10re",[128,32],"F32"),("k10im",[128,32],"F32"),("k11re",[128,32],"F32"),("k11im",[128,32],"F32"),
  ("g_re",[128,32],"F32"),("g_im",[128,32],"F32"),
  ("dreA",[128,32],"F32"),("dimA",[128,32],"F32"),("u1A",[128,32],"F32"),("u2A",[128,32],"F32"),("invA",[128,32],"F32"),
  ("dreB",[128,32],"F32"),("dimB",[128,32],"F32"),("u1B",[128,32],"F32"),("u2B",[128,32],"F32"),("invB",[128,32],"F32"),
  ("wre",[128,32],"F32"),("wim",[128,32],"F32"),("ure",[128,32],"F32"),("uim",[128,32],"F32"),
  ("atre",[128,32],"F32"),("atim",[128,32],"F32"),("Gre",[128,32],"F32"),("Gim",[128,32],"F32"),
  ("Gpre",[128,32],"F32"),("Gpim",[128,32],"F32"),
  ("r01",[128,32],"F32"),("r02",[128,32],"F32"),("r03",[128,32],"F32"),("r04",[128,32],"F32"),
  ("r05",[128,32],"F32"),("r06",[128,32],"F32"),("r07",[128,32],"F32"),("r08",[128,32],"F32"),
  ("GTre",[32,128],"F32"),("GTim",[32,128],"F32"),("Ksb",[32,128],"F32"),("K16",[32,128],"BF16"),("z16",[1,128],"BF16"),
  ("Ttiles",[128,4096],"BF16"),("big1",[128,4096],"F32"),("w0out",[128,4096],"BF16"),
  ("scrow",[1,512],"F32"),("zrrow",[1,512],"F32"),("dqt",[128,512],"F32"),
  ("ssq",[128,4],"F32"),("ssum",[128,4],"F32"),("mcol",[128,4],"F32"),("ecol",[128,4],"F32"),
  ("vcol",[128,4],"F32"),("scol",[128,4],"F32"),("rstd",[128,4],"F32"),("bneg",[128,4],"F32"),
  ("xn16",[128,4096],"BF16"),("xnT",[128,16384],"BF16"),
  ("w0sb",[128,4096],"BF16"),("w0sb2",[128,4096],"BF16"),("bias_sb",[128,32],"F32"),
  ("gt",[128,512],"F32"),("sgt",[128,512],"F32"),
  ("ur0",[128,4096],"F32"),("ur1",[128,4096],"F32"),("ur2",[128,4096],"F32"),("ur3",[128,4096],"F32")],
 "B": [("ident16",[128,128],"BF16"),("identf",[128,128],"F32"),("J2",[128,128],"BF16"),
  ("ones16",[128,1],"BF16"),("ones1f",[1,512],"F32"),("maskb",[128,20],"F32"),("alphas",[1,4096],"F32"),
  ("cosQ",[128,2560],"BF16"),("sinQ",[128,2560],"BF16"),("cosK",[128,2560],"BF16"),("sinK",[128,2560],"BF16"),
  ("wk_sb",[128,4096],"BF16"),("wv_sb",[128,4096],"BF16"),
  ("bigbuf",[128,16384],"F32"),("xnTb",[128,16384],"BF16"),("xn16",[128,4096],"BF16"),
  ("ssq",[128,4],"F32"),("ssum",[128,4],"F32"),("mcol",[128,4],"F32"),("ecol",[128,4],"F32"),
  ("vcol",[128,4],"F32"),("scol",[128,4],"F32"),("rstd",[128,4],"F32"),("bneg",[128,4],"F32"),
  ("ktmp",[128,512],"BF16"),("kstmp",[128,512],"BF16"),("vtmp",[128,512],"BF16"),
  ("q1t",[128,512],"BF16"),("q2t",[128,512],"BF16"),
  ("K_sb",[128,2560],"BF16"),("V_sb",[128,2560],"BF16"),
  ("wq_sb",[128,4096],"BF16"),("wq_sb2",[128,4096],"BF16"),
  ("attnT",[128,1536],"BF16"),("rec",[1,512],"F32"),("cc",[1,512],"F32"),
  ("oL",[128,256],"F32"),("oR",[128,256],"F32"),("f1",[128,256],"F32"),("f2",[128,256],"F32"),
  ("s1chunk",[128,512],"F32")],
 "C": [("ones1f",[1,512],"F32"),("shifts",[128,1],"U32"),
  ("dqu",[128,512],"U32"),("dqf",[128,512],"F32"),
  ("scrow",[1,512],"F32"),("zbrow",[1,512],"F32"),("dqt",[128,512],"F32"),
  ("wg_sb",[128,8192],"BF16"),("wu_sb",[128,8192],"BF16"),
  ("xq",[128,65536],"BF16"),("gsil",[128,512],"F32"),("h16",[128,512],"BF16")],
 "D": [("identf",[128,128],"F32"),("h_sb",[128,44032],"BF16"),
  ("wd_sb",[128,11008],"BF16"),
  ("s1c",[128,512],"F32"),("s1c2",[128,512],"F32"),("gd",[128,512],"F32"),("sgt",[128,512],"F32"),
  ("xr0",[128,4096],"F32"),("xr1",[128,4096],"F32"),("xr2",[128,4096],"F32"),("xr3",[128,4096],"F32")],
}
_DT = {"F32": (F32, 4), "BF16": (BF16, 2), "U32": (U32, 4)}

def _pool_views(pool_ap, allocs):
    out = {}
    off = 0
    for name, shape, dts in allocs:
        dt, sz = _DT[dts]
        nb = shape[1] * sz
        off = (off + 31) // 32 * 32
        if dts == "F32":
            v = pool_ap.bitcast(F32)
        elif dts == "U32":
            v = pool_ap.bitcast(U32)
        else:
            v = pool_ap.bitcast(BF16)
        el = off // sz
        out[name] = v[0:shape[0], el:el + shape[1]]
        off += nb
    assert off <= 204 * 1024, f"pool overflow {off}"
    return out

def _phase_barrier(p):
    for eng in Prog.ENGS:
        for s in list(p.sems.values()):
            p.wait(eng, s)

def build_FUSED(consts):
    nc = bass.Bass("TRN2", target_bir_lowering=False)
    es = ExitStack()
    t = {}
    def din(name, shape, dt):
        h = nc.dram_tensor(name, shape, dt, kind="ExternalInput")
        t[name] = h.ap()
        return h
    def dout(name, shape, dt):
        t[name] = nc.dram_tensor(name, shape, dt, kind="ExternalOutput").ap()
    def dint(name, shape, dt):
        h = nc.dram_tensor(name, shape, dt, kind="Internal")
        t[name] = h.ap()
        return h
    for nm, arr in consts.items():
        din(nm, list(arr.shape), BF16 if arr.dtype == BF else F32)
    for nm in ["lamRe", "lamIm", "CtRe", "CtIm", "pRe", "pIm", "qRe", "qIm"]:
        din(nm, [64, 1], F32)
    din("VcReT", [64, 64], F32); din("VcImT", [64, 64], F32)
    din("Bp", [64, 1], F32); din("dD", [1, 1], F32); din("lstep", [1, 1], F32)
    din("shifts", [128, 1], U32)
    din("u_dev", [RPAD, DM], F32)
    din("qwT", [DM, DM], I32)
    din("scalesT", [32, DM], F32); din("zerosT", [32, DM], F32)
    din("bias_dev", [128, 32], F32)
    din("WqT", [DM, DM], F32); din("WoT", [DM, DM], F32)
    din("WkT", [DM, 128], F32); din("WvT", [DM, 128], F32)
    din("cosQ", [128, RPAD], BF16); din("sinQ", [128, RPAD], BF16)
    din("cosK", [128, RPAD], BF16); din("sinK", [128, RPAD], BF16)
    din("maskb", [128, 20], F32); din("alphas", [1, 16 * 256], F32)
    din("qweight_gu", [4096, 2*INTER], U32)
    din("qzeros_gu", [32, 2752], U32)
    din("scales_gu", [32, 2*INTER], F32)
    din("qweight_down", [11008, DM], U32)
    din("qzeros_down", [NGF, 512], U32)
    din("scales_down", [NGF, DM], F32)
    kp = dint("Kpad", [1, 4352], BF16); t["Kpad_t"] = kp
    dint("smTd", [18, 64], F32); dint("stepd", [1, 1], F32)
    dint("w0cache", [NFC, 128, DM], BF16)
    dint("x1_dev", [RPAD, DM], F32)
    dint("xn1T", [NFC, 128, RPAD], BF16)
    dint("wq_bf", [DM, DM], BF16); dint("wo_bf", [DM, DM], BF16)
    dint("s1T", [NFC, 128, NOWN], F32)
    dint("h_scr", [NGF, 128, NOWN], BF16)
    dint("wd_bf", [NGF, 128, DM], BF16)
    dint("zb_gu", [32, 2*INTER], F32)
    dint("zb_d", [NGF, DM], F32)
    dout("out_rows", [NOWN, DM], F32)

    pool = es.enter_context(nc.sbuf_tensor("pool", [128, 52224], F32))
    pool_ap = pool[:, :]
    vjunk = es.enter_context(nc.sbuf_tensor("s_vjunk", [128, 32], F32))
    ps_h = [es.enter_context(nc.psum_tensor(f"pb{i}", [128, 512], F32)) for i in range(8)]
    psa = [x[:, :] for x in ps_h]

    p = Prog(nc, es)
    sbA = _pool_views(pool_ap, _PHASE_ALLOCS["A"])
    sbA["vjunk"] = vjunk[:, :]
    sbA["ur"] = [sbA.pop("ur%d" % j) for j in range(4)]
    emit_P0(p, t, sbA, psa)
    emit_L0(p, t, sbA, psa)
    _phase_barrier(p)
    sbB = _pool_views(pool_ap, _PHASE_ALLOCS["B"])
    sbB["vjunk"] = vjunk[:, :]
    sbB["bigbuf"] = sbB["bigbuf"]
    sbB["bigbuf_bf"] = sbB["bigbuf"].bitcast(BF16)
    emit_B(p, t, sbB, psa)
    _phase_barrier(p)
    sbC = _pool_views(pool_ap, _PHASE_ALLOCS["C"])
    sbC["vjunk"] = vjunk[:, :]
    xq_u = sbC["xq"].bitcast(U32)
    xq_f = sbC["xq"].bitcast(F32)
    sbC["qzg"] = xq_u[:, 0:688]
    sbC["zqg"] = xq_u[:, 1024:1024+5504]
    sbC["zqgf"] = xq_f[:, 8192:8192+5504]
    sbC["scg"] = xq_f[:, 16384:16384+5504]
    sbC["qzd"] = xq_u[:, 0:512]
    sbC["zqd"] = xq_u[:, 1024:1024+4096]
    sbC["zqdf"] = xq_f[:, 8192:8192+4096]
    sbC["scd"] = xq_f[:, 16384:16384+4096]
    sbC["wdout"] = sbC["wg_sb"].bitcast(BF16)[:, 0:4096]
    emit_C(p, t, sbC, psa)
    _phase_barrier(p)
    sbD = _pool_views(pool_ap, _PHASE_ALLOCS["D"])
    sbD["vjunk"] = vjunk[:, :]
    sbD["wd_sb2"] = sbD["wd_sb"]
    sbD["xr"] = [sbD.pop("xr%d" % j) for j in range(4)]
    emit_D(p, t, sbD, psa)
    qg = p.sem("qg"); qs = p.sem("qs")
    p.wait("sync", qg)
    p.wait("sync", qs)
    p.emit()
    return nc, es

def kernel(**inputs):
    consts = _get_consts()
    in_maps_A = _prep_A_inputs(inputs, consts)
    in_maps_B = _prep_B_inputs(inputs, consts, [None] * 8)
    gsel_gu = np.asarray(inputs["L1_ffn_g_idx_gu"]).reshape(-1, 128)[:, 0]
    gsel_d = np.asarray(inputs["L1_ffn_g_idx_down"]).reshape(-1, 128)[:, 0]
    cbase = {
        "shifts": (4 * (np.arange(128, dtype=np.uint32) % 8)).reshape(128, 1),
        "qweight_gu": np.repeat(np.asarray(inputs["L1_ffn_qweight_gu"]), 8, axis=0),
        "qzeros_gu": np.ascontiguousarray(np.asarray(inputs["L1_ffn_qzeros_gu"])[gsel_gu]),
        "scales_gu": np.ascontiguousarray(np.asarray(inputs["L1_ffn_scales_gu"])[gsel_gu]),
        "qweight_down": np.repeat(np.asarray(inputs["L1_ffn_qweight_down"]), 8, axis=0),
        "qzeros_down": np.ascontiguousarray(np.asarray(inputs["L1_ffn_qzeros_down"])[gsel_d]),
        "scales_down": np.ascontiguousarray(np.asarray(inputs["L1_ffn_scales_down"])[gsel_d]),
    }
    in_maps = []
    for core in range(8):
        m = dict(in_maps_A[core])
        mb = in_maps_B[core]
        for k, v in mb.items():
            if k != "x1_dev":
                m[k] = v
        m.update(cbase)
        in_maps.append(m)
    nc, es = build_FUSED(consts)
    res = run_bass_kernel_spmd(nc, in_maps, core_ids=list(range(8)))
    es.close()
    out = np.zeros((4, SEQ, DM), dtype=np.float32)
    for core in range(8):
        b = core // 2
        q0 = 2048 * (core % 2)
        out[b, q0:q0+2048, :] = res.results[core]["out_rows"]
    return out


# version 08 (final)
# speedup vs baseline: 1.6878x; 1.6878x over previous
import numpy as np
import ml_dtypes
from contextlib import ExitStack
import concourse.bass as bass
import concourse.mybir as mybir
from concourse.bass_utils import run_bass_kernel_spmd
from concourse.ap import AP

F32 = mybir.dt.float32
BF16 = mybir.dt.bfloat16
U32 = mybir.dt.uint32
I32 = mybir.dt.int32
OP = mybir.AluOpType
AF = mybir.ActivationFunctionType
AX = mybir.AxisListType
BF = ml_dtypes.bfloat16

DM = 4096
SEQ = 4096
RPAD = 2560
OWN0 = 256
NOWN = 2048
BLK = 512
NBLK = RPAD // BLK
NFC = DM // 128
NSTATE = 64
EPS = 1e-5
INTER = 11008
NGF = INTER // 128
DEV_DUMP = False
DEV_DUMP2 = False

def _host_consts():
    c = {}
    c["ident16"] = np.eye(128, dtype=np.float32).astype(BF)
    c["identf"] = np.eye(128, dtype=np.float32)
    c["J16"] = np.eye(128, dtype=np.float32)[::-1].copy().astype(BF)
    rot = np.zeros((128, 128), dtype=np.float32)
    for d in range(128):
        rot[(d + 64) % 128, d] = 1.0
    c["J2"] = rot.astype(BF)
    c["ones1f"] = np.ones((1, 512), dtype=np.float32)
    c["ones16"] = np.ones((128, 1), dtype=BF)
    eg = np.zeros((128, 1024), dtype=np.float32)
    for g in range(32):
        eg[:, 32 * g + g] = 1.0
    c["EG"] = eg.astype(BF)
    L = DM
    kk = np.arange(L, dtype=np.float32) / np.float32(L)
    th = (np.float32(-2.0 * np.pi) * kk).astype(np.float32)
    Om = (np.cos(th) + 1j * np.sin(th)).astype(np.complex64)
    itan = ((1.0 - Om) / (1.0 + Om)).astype(np.complex64)
    cc = (2.0 / (1.0 + Om)).astype(np.complex64)
    bq = lambda v: np.ascontiguousarray(v.reshape(128, 32))
    c["itan_re"] = bq(itan.real.astype(np.float32))
    c["itan_im"] = bq(itan.imag.astype(np.float32))
    c["c_re"] = bq(cc.real.astype(np.float32))
    c["c_im"] = bq(cc.imag.astype(np.float32))
    b_ = np.arange(128); p_ = np.arange(128)
    W1 = np.exp(2j * np.pi * np.outer(b_, p_) / 128.0)
    c["W1re"] = W1.real.astype(np.float32)
    c["W1imneg"] = (-W1.imag).astype(np.float32)
    c["W1im"] = W1.imag.astype(np.float32)
    q_ = np.arange(32)
    tw = np.exp(2j * np.pi * np.outer(p_, q_) / 4096.0)
    c["twre"] = tw.real.astype(np.float32)
    c["twim"] = tw.imag.astype(np.float32)
    cix = np.arange(32)
    W2 = np.exp(2j * np.pi * np.outer(q_, cix) / 32.0) / 4096.0
    c["W2re"] = W2.real.astype(np.float32)
    c["W2imneg"] = (-W2.imag).astype(np.float32)
    return c

class Sem:
    __slots__ = ("h", "c")
    def __init__(self, h):
        self.h = h
        self.c = 0

class Prog:
    ENGS = ("sync", "scalar", "vector", "tensor", "gpsimd")
    def __init__(self, nc, es):
        self.nc = nc
        self.es = es
        self.q = {e: [] for e in self.ENGS}
        self.sems = {}
    def sem(self, name):
        if name not in self.sems:
            self.sems[name] = Sem(self.es.enter_context(self.nc.semaphore(name)))
        return self.sems[name]
    def on(self, eng, fn):
        self.q[eng].append(fn)
    def wait(self, eng, sem, val=None):
        v = sem.c if val is None else val
        if v > 0:
            self.on(eng, lambda e, sem=sem, v=v: e.wait_ge(sem.h, v))
    def dma(self, eng, out, in_, sem, **kw):
        sem.c += 16
        self.on(eng, lambda e, out=out, in_=in_, sem=sem, kw=kw:
                e.dma_start(out=out, in_=in_, **kw).then_inc(sem.h, 16))
    def run(self, eng, fn, sem=None, n=1):
        if sem is not None:
            sem.c += n
            self.on(eng, lambda e, fn=fn, sem=sem, n=n: fn(e).then_inc(sem.h, n))
        else:
            self.on(eng, fn)
    def emit(self):
        with self.nc.Block() as blk:
            for name in self.ENGS:
                dec = getattr(blk, name)
                thunks = self.q[name]
                def body(e, thunks=thunks):
                    for th in thunks:
                        th(e)
                dec(body)

def emit_P0(p, t, sb, ps):
    V = sb
    qs = p.sem("qs"); qg = p.sem("qg"); tm = p.sem("tm"); vm = p.sem("vm"); sm = p.sem("sm")

    for nm in ["identf", "ident16", "J16", "itan_re", "itan_im", "c_re", "c_im",
               "W1re", "W1imneg", "W1im", "twre", "twim", "W2re", "W2imneg",
               "VcReT", "VcImT", "Bp", "dD", "lstep", "ones1f"]:
        p.dma("sync", sb[nm][:, :], t[nm][:, :], qs)
    for i, nm in enumerate(["lamRe", "lamIm", "CtRe", "CtIm", "pRe", "pIm", "qRe", "qIm"]):
        p.dma("sync", sb["sm64"][0:64, i:i+1], t[nm][:, :], qs)
    ld_all = qs.c

    psB = ps[5][0:64, 0:1]
    psB2 = ps[5][0:64, 4:5]
    p.wait("tensor", qs)
    p.run("tensor", lambda e: e.matmul(psB, sb["VcReT"][:, :], sb["Bp"][:, :], start=True, stop=True))
    p.run("tensor", lambda e: e.matmul(psB2, sb["VcImT"][:, :], sb["Bp"][:, :], start=True, stop=True), tm)
    p.wait("scalar", tm)
    p.run("scalar", lambda e: e.copy(out=sb["sm64"][0:64, 8:9], in_=psB), sm)
    p.run("scalar", lambda e: e.copy(out=sb["sm64"][0:64, 9:10], in_=psB2), sm)

    S = sb["sm64"]
    def col(i): return S[0:64, i:i+1]
    TMP = sb["sm64t"]
    def tc(i): return TMP[0:64, i:i+1]
    p.wait("vector", sm)
    p.wait("vector", qs, ld_all)
    prods = [
        (10, 2, 8, 3, 9, OP.add), (11, 2, 9, 3, 8, OP.subtract),
        (12, 2, 4, 3, 5, OP.add), (13, 2, 5, 3, 4, OP.subtract),
        (14, 6, 8, 7, 9, OP.add), (15, 6, 9, 7, 8, OP.subtract),
        (16, 6, 4, 7, 5, OP.add), (17, 6, 5, 7, 4, OP.subtract),
    ]
    for i, (dst, a1, b1, a2, b2, op2) in enumerate(prods):
        p.run("vector", lambda e, a=col(a1), b=col(b1), i=i: e.tensor_tensor(out=tc(2*i), in0=a, in1=b, op=OP.mult))
        p.run("vector", lambda e, a=col(a2), b=col(b2), i=i: e.tensor_tensor(out=tc(2*i+1), in0=a, in1=b, op=OP.mult))
    for i, (dst, a1, b1, a2, b2, op2) in enumerate(prods):
        p.run("vector", lambda e, d=col(dst), op2=op2, i=i: e.tensor_tensor(out=d, in0=tc(2*i), in1=tc(2*i+1), op=op2), vm)

    p.wait("tensor", vm)
    p.run("tensor", lambda e: e.transpose(ps[2][0:18, 0:64], S[0:64, 0:18], sb["identf"][0:64, 0:64]), tm)
    p.wait("scalar", tm)
    p.run("scalar", lambda e: e.copy(out=sb["smT"][0:18, :], in_=ps[2][0:18, 0:64]), sm)
    p.wait("gpsimd", sm)
    p.dma("gpsimd", t["smTd"][:, :], sb["smT"][0:18, :], qg)
    p.wait("sync", qg)
    bc_rows = {"lam_re": 0, "lam_im": 1, "v00re": 10, "v00im": 11, "v01re": 12, "v01im": 13,
               "v10re": 14, "v10im": 15, "v11re": 16, "v11im": 17}
    for nm, row in bc_rows.items():
        p.dma("sync", sb[nm][:, :], t["smTd"][row:row+1, :].partition_broadcast(128), qs)
    p.wait("vector", qs)
    for nm in ["v00re", "v01re", "v10re", "v11re"]:
        p.run("vector", lambda e, nm=nm: e.tensor_scalar(out=sb[nm + "n"][:, :], in0=sb[nm][:, :], scalar1=-1.0, scalar2=None, op0=OP.mult))

    p.wait("scalar", qs, ld_all)
    p.run("scalar", lambda e: e.activation(out=sb["step"][0:1, 0:1], in_=sb["lstep"][0:1, 0:1], func=AF.Exp), sm)
    p.wait("vector", sm)
    p.run("vector", lambda e: e.reciprocal(out=sb["step"][0:1, 1:2], in_=sb["step"][0:1, 0:1]))
    p.run("vector", lambda e: e.tensor_copy(out=sb["vjunk"][:, :], in_=sb["vjunk"][:, :]))
    p.run("vector", lambda e: e.tensor_scalar(out=sb["step"][0:1, 2:3], in0=sb["step"][0:1, 1:2], scalar1=2.0, scalar2=None, op0=OP.mult), vm)
    p.wait("gpsimd", vm)
    p.dma("gpsimd", t["stepd"][:, :], sb["step"][0:1, 2:3], qg)
    p.wait("sync", qg)
    p.dma("sync", sb["r2bc"][:, :], t["stepd"][0:1, 0:1].partition_broadcast(128), qs)

    p.wait("vector", qs)
    for nm in ["k00re", "k00im", "k01re", "k01im", "k10re", "k10im", "k11re", "k11im"]:
        p.run("vector", lambda e, nm=nm: e.memset(V[nm][:, :], 0.0))
    p.run("vector", lambda e: e.tensor_scalar(out=V["g_re"][:, :], in0=V["itan_re"][:, :], scalar1=V["r2bc"][:, 0:1], scalar2=None, op0=OP.mult))
    p.run("vector", lambda e: e.tensor_scalar(out=V["g_im"][:, :], in0=V["itan_im"][:, :], scalar1=V["r2bc"][:, 0:1], scalar2=None, op0=OP.mult))
    bufs = {}
    for side in "AB":
        bufs[side] = {nm: V[nm + side] for nm in ["dre", "dim", "u1", "u2", "inv"]}
    def cauchy_half(e_ops, n, B):
        dre, dim_, u1, u2, inv = B["dre"], B["dim"], B["u1"], B["u2"], B["inv"]
        o = []
        o.append(lambda e: e.tensor_scalar(out=dre[:, :], in0=V["g_re"][:, :], scalar1=V["lam_re"][:, n:n+1], scalar2=None, op0=OP.subtract))
        o.append(lambda e: e.tensor_scalar(out=dim_[:, :], in0=V["g_im"][:, :], scalar1=V["lam_im"][:, n:n+1], scalar2=None, op0=OP.subtract))
        o.append(lambda e: e.tensor_tensor(out=u1[:, :], in0=dre[:, :], in1=dre[:, :], op=OP.mult))
        o.append(lambda e: e.tensor_tensor(out=u2[:, :], in0=dim_[:, :], in1=dim_[:, :], op=OP.mult))
        o.append(lambda e: e.tensor_tensor(out=u1[:, :], in0=u1[:, :], in1=u2[:, :], op=OP.add))
        o.append(lambda e: e.reciprocal(out=inv[:, :], in_=u1[:, :]))
        o.append(lambda e: e.tensor_tensor(out=dre[:, :], in0=dre[:, :], in1=inv[:, :], op=OP.mult))
        o.append(lambda e: e.tensor_tensor(out=dim_[:, :], in0=dim_[:, :], in1=inv[:, :], op=OP.mult))
        quads = [("k00re", "k00im", "v00re", "v00im", "v00ren"),
                 ("k01re", "k01im", "v01re", "v01im", "v01ren"),
                 ("k10re", "k10im", "v10re", "v10im", "v10ren"),
                 ("k11re", "k11im", "v11re", "v11im", "v11ren")]
        for kre, kim, vre, vim, vren in quads:
            o.append(lambda e, kre=kre, vre=vre: e.scalar_tensor_tensor(out=V[kre][:, :], in0=dre[:, :], scalar=V[vre][:, n:n+1], in1=V[kre][:, :], op0=OP.mult, op1=OP.add))
            o.append(lambda e, kim=kim, vim=vim: e.scalar_tensor_tensor(out=V[kim][:, :], in0=dre[:, :], scalar=V[vim][:, n:n+1], in1=V[kim][:, :], op0=OP.mult, op1=OP.add))
        for kre, kim, vre, vim, vren in quads:
            o.append(lambda e, kre=kre, vim=vim: e.scalar_tensor_tensor(out=V[kre][:, :], in0=dim_[:, :], scalar=V[vim][:, n:n+1], in1=V[kre][:, :], op0=OP.mult, op1=OP.add))
            o.append(lambda e, kim=kim, vren=vren: e.scalar_tensor_tensor(out=V[kim][:, :], in0=dim_[:, :], scalar=V[vren][:, n:n+1], in1=V[kim][:, :], op0=OP.mult, op1=OP.add))
        return o
    for pair in range(NSTATE // 2):
        oa = cauchy_half(None, 2 * pair, bufs["A"])
        ob = cauchy_half(None, 2 * pair + 1, bufs["B"])
        for fa, fb in zip(oa, ob):
            p.run("vector", fa)
            p.run("vector", fb)
    R = lambda i: V["r%02d" % i]
    seq = [
        lambda e: e.tensor_tensor(out=R(1)[:, :], in0=V["k01re"][:, :], in1=V["k10re"][:, :], op=OP.mult),
        lambda e: e.tensor_tensor(out=R(2)[:, :], in0=V["k01im"][:, :], in1=V["k10im"][:, :], op=OP.mult),
        lambda e: e.tensor_tensor(out=R(3)[:, :], in0=V["k01re"][:, :], in1=V["k10im"][:, :], op=OP.mult),
        lambda e: e.tensor_tensor(out=R(4)[:, :], in0=V["k01im"][:, :], in1=V["k10re"][:, :], op=OP.mult),
        lambda e: e.tensor_scalar(out=R(5)[:, :], in0=V["k11re"][:, :], scalar1=1.0, scalar2=None, op0=OP.add),
        lambda e: e.tensor_tensor(out=V["wre"][:, :], in0=R(1)[:, :], in1=R(2)[:, :], op=OP.subtract),
        lambda e: e.tensor_tensor(out=V["wim"][:, :], in0=R(3)[:, :], in1=R(4)[:, :], op=OP.add),
        lambda e: e.tensor_tensor(out=R(6)[:, :], in0=R(5)[:, :], in1=R(5)[:, :], op=OP.mult),
        lambda e: e.tensor_tensor(out=R(7)[:, :], in0=V["k11im"][:, :], in1=V["k11im"][:, :], op=OP.mult),
        lambda e: e.tensor_tensor(out=R(1)[:, :], in0=V["wre"][:, :], in1=R(5)[:, :], op=OP.mult),
        lambda e: e.tensor_tensor(out=R(8)[:, :], in0=R(6)[:, :], in1=R(7)[:, :], op=OP.add),
        lambda e: e.tensor_tensor(out=R(2)[:, :], in0=V["wim"][:, :], in1=V["k11im"][:, :], op=OP.mult),
        lambda e: e.reciprocal(out=V["inv" + "A"][:, :], in_=R(8)[:, :]),
        lambda e: e.tensor_tensor(out=R(3)[:, :], in0=V["wim"][:, :], in1=R(5)[:, :], op=OP.mult),
        lambda e: e.tensor_tensor(out=R(4)[:, :], in0=V["wre"][:, :], in1=V["k11im"][:, :], op=OP.mult),
        lambda e: e.tensor_tensor(out=R(6)[:, :], in0=R(1)[:, :], in1=R(2)[:, :], op=OP.add),
        lambda e: e.tensor_tensor(out=R(7)[:, :], in0=R(3)[:, :], in1=R(4)[:, :], op=OP.subtract),
        lambda e: e.tensor_tensor(out=V["ure"][:, :], in0=R(6)[:, :], in1=V["invA"][:, :], op=OP.mult),
        lambda e: e.tensor_tensor(out=V["uim"][:, :], in0=R(7)[:, :], in1=V["invA"][:, :], op=OP.mult),
        lambda e: e.tensor_tensor(out=R(1)[:, :], in0=V["k00re"][:, :], in1=V["ure"][:, :], op=OP.subtract),
        lambda e: e.tensor_tensor(out=R(2)[:, :], in0=V["k00im"][:, :], in1=V["uim"][:, :], op=OP.subtract),
        lambda e: e.tensor_tensor(out=R(3)[:, :], in0=V["c_re"][:, :], in1=R(1)[:, :], op=OP.mult),
        lambda e: e.tensor_tensor(out=R(4)[:, :], in0=V["c_im"][:, :], in1=R(2)[:, :], op=OP.mult),
        lambda e: e.tensor_tensor(out=R(5)[:, :], in0=V["c_re"][:, :], in1=R(2)[:, :], op=OP.mult),
        lambda e: e.tensor_tensor(out=R(6)[:, :], in0=V["c_im"][:, :], in1=R(1)[:, :], op=OP.mult),
        lambda e: e.tensor_tensor(out=V["atre"][:, :], in0=R(3)[:, :], in1=R(4)[:, :], op=OP.subtract),
        lambda e: e.tensor_tensor(out=V["atim"][:, :], in0=R(5)[:, :], in1=R(6)[:, :], op=OP.add),
    ]
    for i, f in enumerate(seq):
        p.run("vector", f, vm if i == len(seq) - 1 else None)

    psGre = ps[0][:, 0:32]
    psGim = ps[1][:, 0:32]
    p.wait("tensor", vm)
    p.run("tensor", lambda e: e.matmul(psGre, sb["W1re"][:, :], V["atre"][:, :], start=True, stop=False))
    p.run("tensor", lambda e: e.matmul(psGre, sb["W1imneg"][:, :], V["atim"][:, :], start=False, stop=True))
    p.run("tensor", lambda e: e.matmul(psGim, sb["W1im"][:, :], V["atre"][:, :], start=True, stop=False))
    p.run("tensor", lambda e: e.matmul(psGim, sb["W1re"][:, :], V["atim"][:, :], start=False, stop=True), tm)
    p.wait("scalar", tm)
    p.run("scalar", lambda e: e.copy(out=V["Gre"][:, :], in_=psGre), sm)
    p.run("scalar", lambda e: e.copy(out=V["Gim"][:, :], in_=psGim), sm)
    p.wait("vector", sm)
    p.run("vector", lambda e: e.tensor_tensor(out=R(1)[:, :], in0=V["Gre"][:, :], in1=V["twre"][:, :], op=OP.mult))
    p.run("vector", lambda e: e.tensor_tensor(out=R(2)[:, :], in0=V["Gim"][:, :], in1=V["twim"][:, :], op=OP.mult))
    p.run("vector", lambda e: e.tensor_tensor(out=R(3)[:, :], in0=V["Gre"][:, :], in1=V["twim"][:, :], op=OP.mult))
    p.run("vector", lambda e: e.tensor_tensor(out=R(4)[:, :], in0=V["Gim"][:, :], in1=V["twre"][:, :], op=OP.mult))
    p.run("vector", lambda e: e.tensor_tensor(out=V["Gpre"][:, :], in0=R(1)[:, :], in1=R(2)[:, :], op=OP.subtract))
    p.run("vector", lambda e: e.tensor_tensor(out=V["Gpim"][:, :], in0=R(3)[:, :], in1=R(4)[:, :], op=OP.add), vm)
    p.wait("tensor", vm)
    p.run("tensor", lambda e: e.transpose(ps[2][0:32, 0:128], V["Gpre"][:, :], sb["identf"][:, :]))
    p.run("tensor", lambda e: e.transpose(ps[3][0:32, 0:128], V["Gpim"][:, :], sb["identf"][:, :]), tm)
    p.wait("scalar", tm)
    p.run("scalar", lambda e: e.copy(out=V["GTre"][0:32, :], in_=ps[2][0:32, 0:128]), sm)
    p.run("scalar", lambda e: e.copy(out=V["GTim"][0:32, :], in_=ps[3][0:32, 0:128]), sm)
    psK = ps[4][0:32, 0:128]
    p.wait("tensor", sm)
    p.run("tensor", lambda e: e.matmul(psK, sb["W2re"][0:32, :], V["GTre"][0:32, :], start=True, stop=False))
    p.run("tensor", lambda e: e.matmul(psK, sb["W2imneg"][0:32, :], V["GTim"][0:32, :], start=False, stop=True), tm)
    p.wait("scalar", tm)
    p.run("scalar", lambda e: e.copy(out=V["Ksb"][0:32, :], in_=psK), sm)
    p.wait("vector", sm)
    p.run("vector", lambda e: e.tensor_tensor(out=V["Ksb"][0:1, 0:1], in0=V["Ksb"][0:1, 0:1], in1=sb["dD"][0:1, 0:1], op=OP.add))
    p.run("vector", lambda e: e.memset(V["z16"][0:1, :], 0.0))
    p.run("vector", lambda e: e.tensor_copy(out=V["K16"][0:32, :], in_=V["Ksb"][0:32, :]), vm)
    p.wait("gpsimd", vm)
    p.dma("gpsimd", t["Kpad"][0:1, 0:127], V["z16"][0:1, 0:127], qg)
    p.dma("gpsimd", t["Kpad"][0:1, 127:4223].rearrange("one (c p) -> (one c) p", p=128), V["K16"][0:32, :], qg)
    p.wait("sync", qg)
    if "Kdump" in t:
        p.dma("sync", t["Kdump"][:, :], t["Kpad"][:, :], qs)
    for d in range(32):
        ap_in = AP(t["Kpad_t"], 128 * d, [[1, 128], [1, 128]])
        p.dma("sync", sb["w0sb"][:, 128*d:128*(d+1)], ap_in, qs)
    p.wait("tensor", qs)
    for d in range(32):
        pb = (ps[6] if d % 2 == 0 else ps[7])[:, 0:128]
        if d >= 2:
            p.wait("tensor", sm, sm.c - 1)
        p.run("tensor", lambda e, d=d, pb=pb: e.matmul(pb, sb["J16"][:, :], sb["w0sb"][:, 128*d:128*(d+1)], start=True, stop=True), tm)
        p.wait("scalar", tm)
        p.run("scalar", lambda e, d=d, pb=pb: e.copy(out=sb["Ttiles"][:, 128*d:128*(d+1)], in_=pb), sm)

    if DEV_DUMP:
        p.wait("gpsimd", vm)
        p.wait("gpsimd", sm)
        p.dma("gpsimd", t["dbg_sm64"][:, :], sb["sm64"][0:64, :], qg)
        p.dma("gpsimd", t["dbg_w0"][:, :], t["w0cache"][5, :, :], qg)
        p.dma("gpsimd", t["dbg_lamre"][:, :], sb["lam_re"][:, :], qg)
        p.dma("gpsimd", t["dbg_v00re"][:, :], sb["v00re"][:, :], qg)
        p.dma("gpsimd", t["dbg_gre"][:, :], sb["g_re"][:, :], qg)
        p.dma("gpsimd", t["dbg_gim"][:, :], sb["g_im"][:, :], qg)
        p.dma("gpsimd", t["dbg_k00re"][:, :], sb["k00re"][:, :], qg)
        p.dma("gpsimd", t["dbg_dre"][:, :], sb["dreB"][:, :], qg)
        p.dma("gpsimd", t["dbg_dim"][:, :], sb["dimB"][:, :], qg)
        p.dma("gpsimd", t["dbg_inv"][:, :], sb["invB"][:, :], qg)
        p.dma("gpsimd", t["dbg_k00im"][:, :], sb["k00im"][:, :], qg)
        p.dma("gpsimd", t["dbg_k11re"][:, :], sb["k11re"][:, :], qg)
        p.dma("gpsimd", t["dbg_atre"][:, :], sb["atre"][:, :], qg)
        p.dma("gpsimd", t["dbg_atim"][:, :], sb["atim"][:, :], qg)
        p.dma("gpsimd", t["dbg_Gre"][:, :], sb["Gre"][:, :], qg)
        p.dma("gpsimd", t["dbg_Ksb"][:, :], sb["Ksb"][0:32, :], qg)
    p.wait("tensor", sm)
    for k in range(NFC):
        dqf = sb["big1"]
        wout = sb["w0out"]
        if k >= 1:
            p.wait("gpsimd", vm)
            p.wait("sync", tm)
        p.dma("gpsimd", dqf[:, :], t["qwT"][128*k:128*(k+1), :], qg, max_dma_last_dim=8192)
        gq = qg.c
        p.wait("vector", qg, gq)
        for j in range(8):
            pb6 = ps[6][:, 0:512]
            pb7 = ps[7][:, 0:512]
            p.wait("sync", tm)
            p.dma("sync", sb["scrow"][0:1, 0:512], t["scalesT"][k:k+1, 512*j:512*(j+1)], qs)
            p.dma("sync", sb["zrrow"][0:1, 0:512], t["zerosT"][k:k+1, 512*j:512*(j+1)], qs)
            p.wait("tensor", qs, qs.c)
            if k > 0 or j > 0:
                p.wait("tensor", vm, vm.c)
            p.run("tensor", lambda e, j=j, pb6=pb6: e.matmul(pb6, sb["ones1f"][0:1, 0:128], sb["scrow"][0:1, 0:512], start=True, stop=True))
            p.run("tensor", lambda e, j=j, pb7=pb7: e.matmul(pb7, sb["ones1f"][0:1, 0:128], sb["zrrow"][0:1, 0:512], start=True, stop=True), tm)
            p.wait("vector", tm)
            p.run("vector", lambda e, j=j, dqf=dqf, pb6=pb6: e.tensor_tensor(out=sb["dqt"][:, :], in0=dqf[:, 512*j:512*(j+1)], in1=pb6, op=OP.mult))
            p.run("vector", lambda e, j=j, wout=wout, pb7=pb7: e.tensor_tensor(out=wout[:, 512*j:512*(j+1)], in0=sb["dqt"][:, :], in1=pb7, op=OP.add), vm)
        p.wait("gpsimd", vm)
        p.dma("gpsimd", t["w0cache"][k, :, :], wout[:, :], qg)

def emit_L0(p, t, sb, ps):
    V = sb
    qs = p.sem("qs"); qg = p.sem("qg"); tm = p.sem("tm"); vm = p.sem("vm"); sm = p.sem("sm")
    p.dma("sync", sb["bias_sb"][:, :], t["bias_dev"][:, :], qs)
    p.wait("sync", sm)
    p.wait("sync", vm)

    inv_dm = 1.0 / DM
    for rb in range(NBLK):
        r0 = rb * BLK
        p.wait("sync", vm); p.wait("sync", qg)
        for j in range(4):
            p.dma("sync", sb["ur"][j][:, :], t["u_dev"][r0+128*j:r0+128*(j+1), :], qs)
        uld = qs.c
        p.wait("scalar", qs, uld)
        p.wait("vector", qs, uld)
        for j in range(4):
            p.run("scalar", lambda e, j=j: e.activation(out=sb["big1"][:, :], in_=sb["ur"][j][:, :], func=AF.Square, accum_out=sb["ssq"][:, j:j+1]), sm)
            p.run("vector", lambda e, j=j: e.reduce_sum(out=sb["ssum"][:, j:j+1], in_=sb["ur"][j][:, :], axis=AX.X), vm)
        p.wait("vector", sm)
        ME, EX, VA = sb["mcol"], sb["ecol"], sb["vcol"]
        for j in range(4):
            p.run("vector", lambda e, j=j: e.tensor_scalar(out=ME[:, j:j+1], in0=sb["ssum"][:, j:j+1], scalar1=inv_dm, scalar2=None, op0=OP.mult))
        for j in range(4):
            p.run("vector", lambda e, j=j: e.tensor_scalar(out=EX[:, j:j+1], in0=sb["ssq"][:, j:j+1], scalar1=inv_dm, scalar2=None, op0=OP.mult))
        for j in range(4):
            p.run("vector", lambda e, j=j: e.tensor_tensor(out=VA[:, j:j+1], in0=ME[:, j:j+1], in1=ME[:, j:j+1], op=OP.mult))
        for j in range(4):
            p.run("vector", lambda e, j=j: e.tensor_tensor(out=VA[:, j:j+1], in0=EX[:, j:j+1], in1=VA[:, j:j+1], op=OP.subtract))
        for j in range(4):
            p.run("vector", lambda e, j=j: e.tensor_scalar(out=VA[:, j:j+1], in0=VA[:, j:j+1], scalar1=EPS, scalar2=None, op0=OP.add), vm)
        p.wait("scalar", vm)
        for j in range(4):
            p.run("scalar", lambda e, j=j: e.activation(out=sb["scol"][:, j:j+1], in_=sb["vcol"][:, j:j+1], func=AF.Sqrt), sm)
        p.wait("vector", sm)
        for j in range(4):
            p.run("vector", lambda e, j=j: e.reciprocal(out=sb["rstd"][:, j:j+1], in_=sb["scol"][:, j:j+1]))
        for j in range(4):
            p.run("vector", lambda e, j=j: e.scalar_tensor_tensor(out=sb["bneg"][:, j:j+1], in0=sb["mcol"][:, j:j+1], scalar=-1.0, in1=sb["rstd"][:, j:j+1], op0=OP.mult, op1=OP.mult), vm)
        sm_tp = None
        for j in range(4):
            p.run("vector", lambda e, j=j: e.tensor_scalar(out=sb["xn16"][:, :], in0=sb["ur"][j][:, :], scalar1=sb["rstd"][:, j:j+1], scalar2=sb["bneg"][:, j:j+1], op0=OP.mult, op1=OP.add), vm)
            if DEV_DUMP and rb == 0 and j == 2:
                p.wait("gpsimd", vm)
                p.dma("gpsimd", t["dbg_xn"][:, :], sb["xn16"][:, :], qg)
                p.wait("vector", qg)
            p.wait("tensor", vm)
            for fc in range(NFC):
                pb = (ps[5] if fc % 2 == 0 else ps[6]).bitcast(BF16)[:, 0:128]
                if fc >= 2 or j > 0:
                    p.wait("tensor", sm, sm.c - 1)
                p.run("tensor", lambda e, fc=fc, pb=pb: e.transpose(pb, sb["xn16"][:, 128*fc:128*(fc+1)], sb["ident16"][:, :]), tm)
                p.wait("scalar", tm)
                p.run("scalar", lambda e, fc=fc, j=j, pb=pb: e.copy(out=sb["xnT"][:, 512*fc+128*j:512*fc+128*(j+1)], in_=pb), sm)
            p.wait("vector", sm)
        for c in range(NFC):
            psA = ps[0] if c % 2 == 0 else ps[1]
            psB_ = ps[2] if c % 2 == 0 else ps[3]
            w0b = sb["w0sb"] if c % 2 == 0 else sb["w0sb2"]
            if c >= 2:
                p.wait("sync", tm, tm.c)
            p.dma("sync", w0b[:].rearrange("p (k n) -> p k n", k=NFC),
                  t["w0cache"][:, :, 128*c:128*(c+1)].transpose([1, 0, 2]), qs)
            wld = qs.c
            if c >= 2:
                p.wait("tensor", vm, vm.c)
            for ci in range(c + 1):
                d = c - ci
                if ci < c:
                    p.run("tensor", lambda e, d=d, ci=ci, psA=psA, c=c: e.matmul(psA[:, :], sb["Ttiles"][:, 128*d:128*(d+1)], sb["xnT"][:, 512*ci:512*(ci+1)], start=(ci == 0), stop=False))
                else:
                    p.run("tensor", lambda e, d=d, ci=ci, psA=psA, c=c: e.matmul(psA[:, :], sb["Ttiles"][:, 128*d:128*(d+1)], sb["xnT"][:, 512*ci:512*(ci+1)], start=(ci == 0), stop=True), tm)
            p.wait("tensor", qs, wld)
            for k in range(NFC):
                if k < NFC - 1:
                    p.run("tensor", lambda e, k=k, psB_=psB_, w0b=w0b: e.matmul(psB_[:, :], w0b[:, 128*k:128*(k+1)], sb["xnT"][:, 512*k:512*(k+1)], start=(k == 0), stop=False))
                else:
                    p.run("tensor", lambda e, k=k, psB_=psB_, w0b=w0b: e.matmul(psB_[:, :], w0b[:, 128*k:128*(k+1)], sb["xnT"][:, 512*k:512*(k+1)], start=(k == 0), stop=True), tm)
            p.wait("scalar", tm)
            p.run("scalar", lambda e, c=c, psB_=psB_: e.activation(out=sb["gt"][:, :], in_=psB_[:, :], func=AF.Silu, bias=sb["bias_sb"][:, c:c+1], scale=1.0), sm)
            p.wait("vector", sm)
            if DEV_DUMP and rb == 0 and c == 3:
                p.run("scalar", lambda e, psA=psA: e.copy(out=sb["big1"][:, 0:512], in_=psA[:, :]), sm)
                p.wait("gpsimd", sm)
                p.dma("gpsimd", t["dbg_st"][:, :], sb["big1"][:, 0:512], qg)
                p.dma("gpsimd", t["dbg_gt"][:, :], sb["gt"][:, :], qg)
            p.run("vector", lambda e, psA=psA: e.tensor_tensor(out=sb["sgt"][:, :], in0=sb["gt"][:, :], in1=psA[:, :], op=OP.mult), vm)
            p.wait("tensor", vm)
            for j in range(4):
                if j < 3:
                    p.run("tensor", lambda e, j=j: e.transpose(ps[4][:, 128*j:128*(j+1)], sb["sgt"][:, 128*j:128*(j+1)], sb["identf"][:, :]))
                else:
                    p.run("tensor", lambda e, j=j: e.transpose(ps[4][:, 128*j:128*(j+1)], sb["sgt"][:, 128*j:128*(j+1)], sb["identf"][:, :]), tm)
            p.wait("vector", tm)
            for j in range(4):
                p.run("vector", lambda e, c=c, j=j: e.tensor_tensor(out=sb["ur"][j][:, 128*c:128*(c+1)], in0=sb["ur"][j][:, 128*c:128*(c+1)], in1=ps[4][:, 128*j:128*(j+1)], op=OP.add), vm)
        p.wait("gpsimd", vm)
        for j in range(4):
            p.dma("gpsimd", t["x1_dev"][r0+128*j:r0+128*(j+1), :], sb["ur"][j][:, :], qg)

def build_A(consts):
    nc = bass.Bass("TRN2", target_bir_lowering=False)
    es = ExitStack()
    t = {}
    def din(name, shape, dt):
        h = nc.dram_tensor(name, shape, dt, kind="ExternalInput")
        t[name] = h.ap()
        return h
    def dout(name, shape, dt):
        t[name] = nc.dram_tensor(name, shape, dt, kind="ExternalOutput").ap()
    def dint(name, shape, dt):
        h = nc.dram_tensor(name, shape, dt, kind="Internal")
        t[name] = h.ap()
        return h
    for nm, arr in consts.items():
        din(nm, list(arr.shape), BF16 if arr.dtype == BF else F32)
    for nm in ["lamRe", "lamIm", "CtRe", "CtIm", "pRe", "pIm", "qRe", "qIm"]:
        din(nm, [64, 1], F32)
    din("VcReT", [64, 64], F32); din("VcImT", [64, 64], F32)
    din("Bp", [64, 1], F32); din("dD", [1, 1], F32); din("lstep", [1, 1], F32)
    din("u_dev", [RPAD, DM], F32)
    din("qwT", [DM, DM], I32)
    din("scalesT", [32, DM], F32); din("zerosT", [32, DM], F32)
    din("bias_dev", [128, 32], F32)
    dout("x1_dev", [RPAD, DM], F32)
    dout("Kdump", [1, 4352], BF16)
    if DEV_DUMP:
        dout("dbg_sm64", [64, 20], F32); dout("dbg_lamre", [128, 64], F32)
        dout("dbg_v00re", [128, 64], F32)
        dout("dbg_gre", [128, 32], F32); dout("dbg_gim", [128, 32], F32)
        dout("dbg_k00re", [128, 32], F32); dout("dbg_k00im", [128, 32], F32)
        dout("dbg_dre", [128, 32], F32); dout("dbg_dim", [128, 32], F32); dout("dbg_inv", [128, 32], F32)
        dout("dbg_k11re", [128, 32], F32)
        dout("dbg_atre", [128, 32], F32); dout("dbg_atim", [128, 32], F32)
        dout("dbg_Gre", [128, 32], F32); dout("dbg_Ksb", [32, 128], F32)
        dout("dbg_xn", [128, 4096], BF16); dout("dbg_gt", [128, 512], F32); dout("dbg_st", [128, 512], F32)
        dout("dbg_w0", [128, 4096], BF16)
    kp = dint("Kpad", [1, 4352], BF16); t["Kpad_t"] = kp
    dint("smTd", [18, 64], F32); dint("stepd", [1, 1], F32)
    dint("w0cache", [NFC, 128, DM], BF16)

    sb = {}
    def asb(name, shape, dt):
        sb[name] = es.enter_context(nc.sbuf_tensor("s_" + name, shape, dt))
    for nm in ["itan_re", "itan_im", "c_re", "c_im"]:
        asb(nm, [128, 32], F32)
    asb("identf", [128, 128], F32); asb("ident16", [128, 128], BF16); asb("J16", [128, 128], BF16)
    asb("W1re", [128, 128], F32); asb("W1imneg", [128, 128], F32); asb("W1im", [128, 128], F32)
    asb("twre", [128, 32], F32); asb("twim", [128, 32], F32)
    asb("W2re", [32, 32], F32); asb("W2imneg", [32, 32], F32)
    asb("VcReT", [64, 64], F32); asb("VcImT", [64, 64], F32)
    asb("Bp", [64, 1], F32); asb("dD", [1, 1], F32); asb("lstep", [1, 1], F32)
    asb("ones1f", [1, 512], F32)
    asb("sm64", [64, 20], F32); asb("sm64t", [64, 16], F32); asb("smT", [18, 64], F32)
    asb("vjunk", [128, 32], F32)
    for nm in ["lam_re", "lam_im", "v00re", "v00im", "v01re", "v01im", "v10re", "v10im",
               "v11re", "v11im", "v00ren", "v01ren", "v10ren", "v11ren"]:
        asb(nm, [128, 64], F32)
    asb("step", [1, 4], F32); asb("r2bc", [128, 1], F32)
    for nm in ["k00re", "k00im", "k01re", "k01im", "k10re", "k10im", "k11re", "k11im",
               "g_re", "g_im", "dreA", "dimA", "u1A", "u2A", "invA",
               "dreB", "dimB", "u1B", "u2B", "invB", "wre", "wim",
               "ure", "uim", "atre", "atim", "Gre", "Gim", "Gpre", "Gpim",
               "r01", "r02", "r03", "r04", "r05", "r06", "r07", "r08"]:
        asb(nm, [128, 32], F32)
    asb("GTre", [32, 128], F32); asb("GTim", [32, 128], F32)
    asb("Ksb", [32, 128], F32); asb("K16", [32, 128], BF16); asb("z16", [1, 128], BF16)
    asb("Ttiles", [128, 4096], BF16)
    asb("big1", [128, 4096], F32)
    asb("w0out", [128, 4096], BF16)
    asb("scrow", [1, 512], F32); asb("zrrow", [1, 512], F32)
    asb("dqt", [128, 512], F32)
    sb["ur"] = [es.enter_context(nc.sbuf_tensor(f"ur{j}", [128, DM], F32)) for j in range(4)]
    asb("ssq", [128, 4], F32); asb("ssum", [128, 4], F32)
    asb("mcol", [128, 4], F32); asb("ecol", [128, 4], F32); asb("vcol", [128, 4], F32)
    asb("scol", [128, 4], F32); asb("rstd", [128, 4], F32); asb("bneg", [128, 4], F32)
    asb("xn16", [128, DM], BF16)
    asb("xnT", [128, NFC * BLK], BF16)
    asb("w0sb", [128, 4096], BF16); asb("w0sb2", [128, 4096], BF16)
    asb("bias_sb", [128, 32], F32)
    asb("gt", [128, 512], F32); asb("sgt", [128, 512], F32)

    ps_h = [es.enter_context(nc.psum_tensor(f"pb{i}", [128, 512], F32)) for i in range(8)]

    p = Prog(nc, es)
    sba = {k: (v if isinstance(v, list) else v[:, :]) for k, v in sb.items()}
    sba["ur"] = [x[:, :] for x in sb["ur"]]
    psa = [x[:, :] for x in ps_h]
    emit_P0(p, t, sba, psa)
    emit_L0(p, t, sba, psa)
    qg = p.sem("qg"); qs = p.sem("qs")
    p.wait("sync", qg)
    p.wait("sync", qs)
    p.emit()
    return nc, es

def _prep_A_inputs(inputs, consts):
    u = np.asarray(inputs["L0_attn_u"])
    base = dict(consts)
    lam = np.asarray(inputs["L0_attn_Lambda_ri"])
    base["lamRe"] = np.ascontiguousarray(lam[:, 0:1]); base["lamIm"] = np.ascontiguousarray(lam[:, 1:2])
    ct = np.asarray(inputs["L0_attn_Ct_ri"]).reshape(64, 2)
    base["CtRe"] = np.ascontiguousarray(ct[:, 0:1]); base["CtIm"] = np.ascontiguousarray(ct[:, 1:2])
    pp = np.asarray(inputs["L0_attn_p_ri"])
    base["pRe"] = np.ascontiguousarray(pp[:, 0:1]); base["pIm"] = np.ascontiguousarray(pp[:, 1:2])
    qq = np.asarray(inputs["L0_attn_q_ri"])
    base["qRe"] = np.ascontiguousarray(qq[:, 0:1]); base["qIm"] = np.ascontiguousarray(qq[:, 1:2])
    vc = np.asarray(inputs["L0_attn_Vc_ri"])
    base["VcReT"] = np.ascontiguousarray(vc[:, :, 0].T); base["VcImT"] = np.ascontiguousarray(vc[:, :, 1].T)
    base["Bp"] = np.ascontiguousarray(np.asarray(inputs["L0_attn_Bp"]))
    base["dD"] = np.asarray(inputs["L0_attn_D"]).reshape(1, 1).astype(np.float32)
    base["lstep"] = np.asarray(inputs["L0_attn_log_step"]).reshape(1, 1).astype(np.float32)
    qw = np.asarray(inputs["L0_ffn_qweight"]).reshape(4096, 4096)
    base["qwT"] = np.ascontiguousarray(qw.T)
    base["scalesT"] = np.ascontiguousarray(np.asarray(inputs["L0_ffn_scales"])[:, :, 0].T)
    base["zerosT"] = np.ascontiguousarray(np.asarray(inputs["L0_ffn_scaled_zeros"])[:, :, 0].T)
    base["bias_dev"] = np.ascontiguousarray(np.asarray(inputs["L0_ffn_bias"]).reshape(32, 128).T)
    in_maps = []
    for core in range(8):
        b = core // 2
        q0 = 2048 * (core % 2)
        uc = np.zeros((RPAD, DM), dtype=np.float32)
        s_lo = q0 - 256
        a0 = max(0, s_lo); a1 = min(SEQ, q0 + 2048 + 256)
        uc[a0 - s_lo: a1 - s_lo, :] = u[b, a0:a1, :]
        m = dict(base)
        m["u_dev"] = uc
        in_maps.append(m)
    return in_maps

_CONSTS = None
def _get_consts():
    global _CONSTS
    if _CONSTS is None:
        _CONSTS = _host_consts()
    return _CONSTS

def run_A(inputs):
    consts = _get_consts()
    in_maps = _prep_A_inputs(inputs, consts)
    nc, es = build_A(consts)
    res = run_bass_kernel_spmd(nc, in_maps, core_ids=list(range(8)))
    es.close()
    if DEV_DUMP:
        return [r["x1_dev"] for r in res.results], [r["Kdump"] for r in res.results], res.results[0]
    return [r["x1_dev"] for r in res.results], [r["Kdump"] for r in res.results]


def emit_B(p, t, sb, ps):
    V = sb
    qs = p.sem("qs"); qg = p.sem("qg"); tm = p.sem("tm"); vm = p.sem("vm"); sm = p.sem("sm")
    NKC = RPAD // 128

    for nm in ["ident16", "identf", "J2", "ones16", "ones1f", "maskb", "alphas",
               "cosQ", "sinQ", "cosK", "sinK"]:
        p.dma("sync", sb[nm][:, :], t[nm][:, :], qs)
    p.dma("gpsimd", sb["wk_sb"][:].rearrange("p (k d) -> p k d", k=NFC),
          t["WkT"].rearrange("(k p) d -> p k d", p=128), qg, max_dma_last_dim=512)
    p.dma("gpsimd", sb["wv_sb"][:].rearrange("p (k d) -> p k d", k=NFC),
          t["WvT"].rearrange("(k p) d -> p k d", p=128), qg, max_dma_last_dim=512)
    p.dma("gpsimd", t["wq_bf"][:, :], t["WqT"][:, :], qg, max_dma_last_dim=8192)
    p.dma("gpsimd", t["wo_bf"][:, :], t["WoT"][:, :], qg, max_dma_last_dim=8192)

    inv_dm = 1.0 / DM
    bigbuf = sb["bigbuf"]
    urv = [bigbuf[:, 4096*j:4096*(j+1)] for j in range(4)]
    big16 = sb["bigbuf_bf"]
    Qr = big16[:, 0:16384]
    AO = big16[:, 16384:32768]

    for rb in range(NBLK):
        r0 = rb * BLK
        p.wait("sync", vm); p.wait("sync", qg); p.wait("sync", tm)
        for j in range(4):
            p.dma("sync", urv[j], t["x1_dev"][r0+128*j:r0+128*(j+1), :], qs)
        uld = qs.c
        p.wait("scalar", qs, uld)
        p.wait("vector", qs, uld)
        for j in range(4):
            p.run("scalar", lambda e, j=j: e.activation(out=sb["xn16"][:, :], in_=urv[j], func=AF.Square, accum_out=sb["ssq"][:, j:j+1]), sm)
            p.run("vector", lambda e, j=j: e.reduce_sum(out=sb["ssum"][:, j:j+1], in_=urv[j], axis=AX.X), vm)
        p.wait("vector", sm)
        ME, EX, VA = sb["mcol"], sb["ecol"], sb["vcol"]
        for j in range(4):
            p.run("vector", lambda e, j=j: e.tensor_scalar(out=ME[:, j:j+1], in0=sb["ssum"][:, j:j+1], scalar1=inv_dm, scalar2=None, op0=OP.mult))
        for j in range(4):
            p.run("vector", lambda e, j=j: e.tensor_scalar(out=EX[:, j:j+1], in0=sb["ssq"][:, j:j+1], scalar1=inv_dm, scalar2=None, op0=OP.mult))
        for j in range(4):
            p.run("vector", lambda e, j=j: e.tensor_tensor(out=VA[:, j:j+1], in0=ME[:, j:j+1], in1=ME[:, j:j+1], op=OP.mult))
        for j in range(4):
            p.run("vector", lambda e, j=j: e.tensor_tensor(out=VA[:, j:j+1], in0=EX[:, j:j+1], in1=VA[:, j:j+1], op=OP.subtract))
        for j in range(4):
            p.run("vector", lambda e, j=j: e.tensor_scalar(out=VA[:, j:j+1], in0=VA[:, j:j+1], scalar1=EPS, scalar2=None, op0=OP.add), vm)
        p.wait("scalar", vm)
        for j in range(4):
            p.run("scalar", lambda e, j=j: e.activation(out=sb["scol"][:, j:j+1], in_=sb["vcol"][:, j:j+1], func=AF.Sqrt), sm)
        p.wait("vector", sm)
        for j in range(4):
            p.run("vector", lambda e, j=j: e.reciprocal(out=sb["rstd"][:, j:j+1], in_=sb["scol"][:, j:j+1]))
        for j in range(4):
            p.run("vector", lambda e, j=j: e.scalar_tensor_tensor(out=sb["bneg"][:, j:j+1], in0=sb["mcol"][:, j:j+1], scalar=-1.0, in1=sb["rstd"][:, j:j+1], op0=OP.mult, op1=OP.mult), vm)
        for j in range(4):
            p.run("vector", lambda e, j=j: e.tensor_scalar(out=sb["xn16"][:, :], in0=urv[j], scalar1=sb["rstd"][:, j:j+1], scalar2=sb["bneg"][:, j:j+1], op0=OP.mult, op1=OP.add), vm)
            p.wait("tensor", vm)
            for fc in range(NFC):
                pb = (ps[5] if fc % 2 == 0 else ps[6])[:, :].bitcast(BF16)[:, 0:128]
                if fc >= 2 or j > 0:
                    p.wait("tensor", sm, sm.c - 1)
                p.run("tensor", lambda e, fc=fc, pb=pb: e.transpose(pb, sb["xn16"][:, 128*fc:128*(fc+1)], sb["ident16"][:, :]), tm)
                p.wait("scalar", tm)
                p.run("scalar", lambda e, fc=fc, j=j, pb=pb: e.copy(out=sb["xnTb"][:, 512*fc+128*j:512*fc+128*(j+1)], in_=pb), sm)
            p.wait("vector", sm)
        p.wait("gpsimd", sm)
        p.dma("gpsimd", t["xn1T"][:, :, r0:r0+BLK].transpose([1, 0, 2]),
              sb["xnTb"][:].rearrange("p (c n) -> p c n", c=NFC), qg)
        psK_ = ps[0]; psV_ = ps[1]; psJ = ps[2]
        p.wait("tensor", sm)
        for k in range(NFC):
            p.run("tensor", lambda e, k=k: e.matmul(psK_[:, :], sb["wk_sb"][:, 128*k:128*(k+1)], sb["xnTb"][:, 512*k:512*(k+1)], start=(k == 0), stop=(k == NFC-1)), tm if k == NFC-1 else None)
        for k in range(NFC):
            p.run("tensor", lambda e, k=k: e.matmul(psV_[:, :], sb["wv_sb"][:, 128*k:128*(k+1)], sb["xnTb"][:, 512*k:512*(k+1)], start=(k == 0), stop=(k == NFC-1)), tm if k == NFC-1 else None)
        p.wait("scalar", tm)
        p.wait("scalar", vm)
        p.run("scalar", lambda e: e.copy(out=sb["ktmp"][:, :], in_=psK_[:, :]), sm)
        p.run("scalar", lambda e: e.copy(out=sb["vtmp"][:, :], in_=psV_[:, :]), sm)
        p.wait("tensor", sm)
        p.run("tensor", lambda e: e.matmul(psJ[:, :].bitcast(BF16)[:, 0:512] if False else psJ[:, :], sb["J2f"][:, :], sb["ktmp32"][:, :], start=True, stop=True) if False else e.matmul(psJ[:, :], sb["J2"][:, :], sb["ktmp"][:, :], start=True, stop=True), tm)
        p.wait("scalar", tm)
        p.run("scalar", lambda e: e.copy(out=sb["kstmp"][:, :], in_=psJ[:, :]), sm)
        p.wait("vector", sm)
        p.run("vector", lambda e, r0=r0: e.tensor_tensor(out=sb["q1t"][:, :], in0=sb["ktmp"][:, :], in1=sb["cosK"][:, r0:r0+BLK], op=OP.mult))
        p.run("vector", lambda e, r0=r0: e.tensor_tensor(out=sb["q2t"][:, :], in0=sb["kstmp"][:, :], in1=sb["sinK"][:, r0:r0+BLK], op=OP.mult))
        p.run("vector", lambda e, r0=r0: e.tensor_tensor(out=sb["K_sb"][:, r0:r0+BLK], in0=sb["q1t"][:, :], in1=sb["q2t"][:, :], op=OP.add), vm)
        p.wait("tensor", sm)
        for j in range(4):
            pb = (ps[3] if j % 2 == 0 else ps[4])[:, :].bitcast(BF16)[:, 0:128]
            if j >= 2:
                p.wait("tensor", sm, sm.c - 1)
            p.run("tensor", lambda e, j=j, pb=pb: e.transpose(pb, sb["vtmp"][:, 128*j:128*(j+1)], sb["ident16"][:, :]), tm)
            p.wait("scalar", tm)
            p.run("scalar", lambda e, j=j, pb=pb, rb=rb: e.copy(out=sb["V_sb"][:, 128*(4*rb+j):128*(4*rb+j+1)], in_=pb), sm)

    for qb in range(4):
        r0 = OWN0 + qb * BLK
        p.wait("sync", tm); p.wait("sync", vm); p.wait("sync", qg)
        p.dma("sync", sb["xnTb"][:].rearrange("p (c n) -> p c n", c=NFC),
              t["xn1T"][:, :, r0:r0+BLK].transpose([1, 0, 2]), qs)
        xld = qs.c
        for hc in range(NFC):
            wqb = sb["wq_sb"] if hc % 2 == 0 else sb["wq_sb2"]
            if hc >= 2:
                p.wait("sync", tm)
            p.dma("sync", wqb[:].rearrange("p (k n) -> p k n", k=NFC),
                  t["wq_bf"].rearrange("(k p) o -> k p o", p=128)[:, :, 128*hc:128*(hc+1)].transpose([1, 0, 2]), qs)
            wld = qs.c
            psQ = ps[0] if hc % 2 == 0 else ps[1]
            psJ = ps[2] if hc % 2 == 0 else ps[3]
            p.wait("tensor", qs, wld)
            if hc >= 2:
                p.wait("tensor", sm)
            for k in range(NFC):
                p.run("tensor", lambda e, k=k, psQ=psQ, wqb=wqb: e.matmul(psQ[:, :], wqb[:, 128*k:128*(k+1)], sb["xnTb"][:, 512*k:512*(k+1)], start=(k == 0), stop=(k == NFC-1)), tm if k == NFC-1 else None)
            p.wait("scalar", tm)
            p.wait("scalar", vm)
            p.run("scalar", lambda e, psQ=psQ: e.copy(out=sb["ktmp"][:, :], in_=psQ[:, :]), sm)
            p.wait("tensor", sm)
            p.run("tensor", lambda e, psJ=psJ: e.matmul(psJ[:, :], sb["J2"][:, :], sb["ktmp"][:, :], start=True, stop=True), tm)
            p.wait("scalar", tm)
            p.run("scalar", lambda e, psJ=psJ: e.copy(out=sb["kstmp"][:, :], in_=psJ[:, :]), sm)
            p.wait("vector", sm)
            p.run("vector", lambda e, r0=r0: e.tensor_tensor(out=sb["q1t"][:, :], in0=sb["ktmp"][:, :], in1=sb["cosQ"][:, r0:r0+BLK], op=OP.mult))
            p.run("vector", lambda e, r0=r0: e.tensor_tensor(out=sb["q2t"][:, :], in0=sb["kstmp"][:, :], in1=sb["sinQ"][:, r0:r0+BLK], op=OP.mult))
            p.run("vector", lambda e, hc=hc: e.tensor_tensor(out=Qr[:, 512*hc:512*(hc+1)], in0=sb["q1t"][:, :], in1=sb["q2t"][:, :], op=OP.add), vm)
        for hc in range(NFC):
            for m2 in range(2):
                tl = 2 * qb + m2
                lc0 = 2 * tl
                qcols = slice(512*hc + 256*m2, 512*hc + 256*(m2+1))
                psS = [ps[0], ps[1], ps[2]]
                psSum = ps[3]; psL = ps[4]; psR = ps[5]; psC = ps[6]
                p.wait("tensor", vm)
                for ck in range(6):
                    pb = psS[ck // 2][:, 256*(ck % 2):256*(ck % 2 + 1)]
                    p.run("tensor", lambda e, ck=ck, pb=pb, lc0=lc0, qcols=qcols: e.matmul(pb, sb["K_sb"][:, 128*(lc0+ck):128*(lc0+ck+1)], Qr[:, qcols], start=True, stop=True), tm if ck == 5 else None)
                p.wait("scalar", tm)
                for ck in range(6):
                    pb = psS[ck // 2][:, 256*(ck % 2):256*(ck % 2 + 1)]
                    p.run("scalar", lambda e, ck=ck, pb=pb, lc0=lc0: e.activation(out=sb["attnT"][:, 256*ck:256*(ck+1)], in_=pb, func=AF.Sigmoid, bias=sb["maskb"][:, lc0+ck:lc0+ck+1], scale=1.0), sm)
                p.wait("tensor", sm)
                for i, ck in enumerate([0, 1, 2, 3]):
                    p.run("tensor", lambda e, ck=ck, i=i: e.matmul(psSum[0:1, 0:256], sb["ones16"][:, 0:1], sb["attnT"][:, 256*ck:256*(ck+1)], start=(i == 0), stop=(i == 3)))
                for i, ck in enumerate([2, 3, 4, 5]):
                    p.run("tensor", lambda e, ck=ck, i=i: e.matmul(psSum[0:1, 256:512], sb["ones16"][:, 0:1], sb["attnT"][:, 256*ck:256*(ck+1)], start=(i == 0), stop=(i == 3)), tm if i == 3 else None)
                for i, ck in enumerate([0, 1, 2, 3]):
                    p.run("tensor", lambda e, ck=ck, i=i, lc0=lc0: e.matmul(psL[:, 0:256], sb["V_sb"][:, 128*(lc0+ck):128*(lc0+ck+1)], sb["attnT"][:, 256*ck:256*(ck+1)], start=(i == 0), stop=(i == 3)))
                for i, ck in enumerate([2, 3, 4, 5]):
                    p.run("tensor", lambda e, ck=ck, i=i, lc0=lc0: e.matmul(psR[:, 0:256], sb["V_sb"][:, 128*(lc0+ck):128*(lc0+ck+1)], sb["attnT"][:, 256*ck:256*(ck+1)], start=(i == 0), stop=(i == 3)), tm if i == 3 else None)
                p.wait("vector", tm, tm.c - 1)
                p.run("vector", lambda e: e.reciprocal(out=sb["rec"][0:1, 0:512], in_=psSum[0:1, 0:512]))
                p.run("vector", lambda e: e.tensor_copy(out=sb["vjunk"][:, :], in_=sb["vjunk"][:, :]))
                p.run("vector", lambda e, tl=tl: e.tensor_tensor(out=sb["cc"][0:1, 0:512], in0=sb["rec"][0:1, 0:512], in1=sb["alphas"][0:1, 512*tl:512*(tl+1)], op=OP.mult), vm)
                p.wait("tensor", vm)
                p.run("tensor", lambda e: e.matmul(psC[:, :], sb["ones1f"][0:1, 0:128], sb["cc"][0:1, 0:512], start=True, stop=True), tm)
                p.wait("scalar", tm)
                p.run("scalar", lambda e: e.copy(out=sb["oL"][:, :], in_=psL[:, 0:256]), sm)
                p.run("scalar", lambda e: e.copy(out=sb["oR"][:, :], in_=psR[:, 0:256]), sm)
                p.wait("vector", sm)
                p.run("vector", lambda e: e.tensor_tensor(out=sb["f1"][:, :], in0=sb["oL"][:, :], in1=psC[:, 0:256], op=OP.mult))
                p.run("vector", lambda e: e.tensor_tensor(out=sb["f2"][:, :], in0=sb["oR"][:, :], in1=psC[:, 256:512], op=OP.mult))
                p.run("vector", lambda e, qcols=qcols: e.tensor_tensor(out=AO[:, qcols], in0=sb["f1"][:, :], in1=sb["f2"][:, :], op=OP.add), vm)
        if DEV_DUMP and qb == 0:
            p.wait("gpsimd", vm)
            p.dma("gpsimd", t["dbg_Qr"][:, :], Qr, qg)
            p.dma("gpsimd", t["dbg_AO"][:, :], AO, qg)
            p.dma("gpsimd", t["dbg_K"][:, :], sb["K_sb"][:, :], qg)
            p.dma("gpsimd", t["dbg_V"][:, :], sb["V_sb"][:, :], qg)
            p.wait("sync", qg)
        for oc in range(NFC):
            wqb = sb["wq_sb"] if oc % 2 == 0 else sb["wq_sb2"]
            if True:
                p.wait("sync", tm)
            p.dma("sync", wqb[:].rearrange("p (k n) -> p k n", k=NFC),
                  t["wo_bf"].rearrange("(k p) o -> k p o", p=128)[:, :, 128*oc:128*(oc+1)].transpose([1, 0, 2]), qs)
            wld = qs.c
            psO = ps[0] if oc % 2 == 0 else ps[1]
            p.wait("tensor", qs, wld)
            if oc >= 2:
                p.wait("tensor", sm)
            for k in range(NFC):
                p.run("tensor", lambda e, k=k, psO=psO, wqb=wqb: e.matmul(psO[:, :], wqb[:, 128*k:128*(k+1)], AO[:, 512*k:512*(k+1)], start=(k == 0), stop=(k == NFC-1)), tm if k == NFC-1 else None)
            p.wait("scalar", tm)
            p.wait("scalar", qg)
            p.run("scalar", lambda e, psO=psO: e.copy(out=sb["s1chunk"][:, :], in_=psO[:, :]), sm)
            p.wait("gpsimd", sm)
            p.dma("gpsimd", t["s1T"][oc, :, 512*qb:512*(qb+1)], sb["s1chunk"][:, :], qg)

def build_B(consts):
    nc = bass.Bass("TRN2", target_bir_lowering=False)
    es = ExitStack()
    t = {}
    def din(name, shape, dt):
        h = nc.dram_tensor(name, shape, dt, kind="ExternalInput")
        t[name] = h.ap()
        return h
    def dout(name, shape, dt):
        t[name] = nc.dram_tensor(name, shape, dt, kind="ExternalOutput").ap()
    def dint(name, shape, dt):
        h = nc.dram_tensor(name, shape, dt, kind="Internal")
        t[name] = h.ap()
        return h
    for nm in ["ident16", "identf", "J2", "ones16"]:
        arr = consts[nm]
        din(nm, list(arr.shape), BF16 if arr.dtype == BF else F32)
    din("ones1f", [1, 512], F32)
    din("x1_dev", [RPAD, DM], F32)
    din("WqT", [DM, DM], F32); din("WoT", [DM, DM], F32)
    din("WkT", [DM, 128], F32); din("WvT", [DM, 128], F32)
    din("cosQ", [128, RPAD], BF16); din("sinQ", [128, RPAD], BF16)
    din("cosK", [128, RPAD], BF16); din("sinK", [128, RPAD], BF16)
    din("maskb", [128, 20], F32)
    din("alphas", [1, 16 * 256], F32)
    dout("s1T", [NFC, 128, NOWN], F32)
    dout("xn1T_out", [NFC, 128, RPAD], BF16)
    if DEV_DUMP:
        dout("dbg_Qr", [128, 16384], BF16); dout("dbg_AO", [128, 16384], BF16)
        dout("dbg_K", [128, RPAD], BF16); dout("dbg_V", [128, RPAD], BF16)
    t["xn1T"] = t["xn1T_out"]
    dint("wq_bf", [DM, DM], BF16); dint("wo_bf", [DM, DM], BF16)

    sb = {}
    def asb(name, shape, dt):
        sb[name] = es.enter_context(nc.sbuf_tensor("s_" + name, shape, dt))
    asb("ident16", [128, 128], BF16); asb("identf", [128, 128], F32)
    asb("J2", [128, 128], BF16); asb("ones16", [128, 1], BF16); asb("ones1f", [1, 512], F32)
    asb("maskb", [128, 20], F32); asb("alphas", [1, 16 * 256], F32)
    asb("cosQ", [128, RPAD], BF16); asb("sinQ", [128, RPAD], BF16)
    asb("cosK", [128, RPAD], BF16); asb("sinK", [128, RPAD], BF16)
    asb("wk_sb", [128, DM], BF16); asb("wv_sb", [128, DM], BF16)
    asb("bigbuf", [128, 16384], F32)
    asb("xnTb", [128, NFC * BLK], BF16)
    asb("xn16", [128, DM], BF16)
    asb("ssq", [128, 4], F32); asb("ssum", [128, 4], F32)
    asb("mcol", [128, 4], F32); asb("ecol", [128, 4], F32); asb("vcol", [128, 4], F32)
    asb("scol", [128, 4], F32); asb("rstd", [128, 4], F32); asb("bneg", [128, 4], F32)
    asb("ktmp", [128, BLK], BF16); asb("kstmp", [128, BLK], BF16); asb("vtmp", [128, BLK], BF16)
    asb("q1t", [128, BLK], BF16); asb("q2t", [128, BLK], BF16)
    asb("K_sb", [128, RPAD], BF16); asb("V_sb", [128, RPAD], BF16)
    asb("wq_sb", [128, DM], BF16); asb("wq_sb2", [128, DM], BF16)
    asb("attnT", [128, 6 * 256], BF16)
    asb("rec", [1, 512], F32); asb("cc", [1, 512], F32)
    asb("oL", [128, 256], F32); asb("oR", [128, 256], F32)
    asb("f1", [128, 256], F32); asb("f2", [128, 256], F32)
    asb("s1chunk", [128, BLK], F32)
    asb("vjunk", [128, 32], F32)
    ps_h = [es.enter_context(nc.psum_tensor(f"pb{i}", [128, 512], F32)) for i in range(8)]

    p = Prog(nc, es)
    sba = {k: v[:, :] for k, v in sb.items()}
    sba["bigbuf_bf"] = sb["bigbuf"][:, :].bitcast(BF16)
    psa = [x[:, :] for x in ps_h]
    emit_B(p, t, sba, psa)
    qg = p.sem("qg"); qs = p.sem("qs")
    p.wait("sync", qg)
    p.wait("sync", qs)
    p.emit()
    return nc, es

def _rope_tables():
    inv_freq = 1.0 / (10000.0 ** (np.arange(0, 128, 2, dtype=np.float32) / 128.0))
    tt = np.arange(SEQ, dtype=np.float32)
    f = np.outer(tt, inv_freq)
    cos = np.zeros((SEQ, 128), dtype=np.float32)
    sin = np.zeros((SEQ, 128), dtype=np.float32)
    cos[:, 0::2] = np.cos(f); cos[:, 1::2] = np.cos(f)
    sin[:, 0::2] = np.sin(f); sin[:, 1::2] = np.sin(f)
    sign = np.where(np.arange(128) < 64, -1.0, 1.0).astype(np.float32)
    return cos.T.copy(), (sin * sign[None, :]).T.copy()

def _prep_B_inputs(inputs, consts, x1_list):
    SCALE = 1.0 / np.sqrt(128.0)
    cosT, sinT = _rope_tables()
    base = {nm: consts[nm] for nm in ["ident16", "identf", "J2", "ones16"]}
    base["ones1f"] = consts["ones1f"]
    base["WqT"] = np.ascontiguousarray(np.asarray(inputs["L1_attn_Wq"]).T)
    base["WoT"] = np.ascontiguousarray(np.asarray(inputs["L1_attn_Wo"]).T)
    base["WkT"] = np.ascontiguousarray(np.asarray(inputs["L1_attn_Wk"]).T)
    base["WvT"] = np.ascontiguousarray(np.asarray(inputs["L1_attn_Wv"]).T)
    alpha = (np.arange(256, dtype=np.float32) / 255.0)
    in_maps = []
    for core in range(8):
        q0 = 2048 * (core % 2)
        s_lo = q0 - 256
        idx = np.clip(np.arange(s_lo, s_lo + RPAD), 0, SEQ - 1)
        m = dict(base)
        m["cosQ"] = np.ascontiguousarray(cosT[:, idx] * SCALE).astype(BF)
        m["sinQ"] = np.ascontiguousarray(sinT[:, idx] * SCALE).astype(BF)
        m["cosK"] = np.ascontiguousarray(cosT[:, idx]).astype(BF)
        m["sinK"] = np.ascontiguousarray(sinT[:, idx]).astype(BF)
        mb = np.zeros((128, 20), dtype=np.float32)
        if core % 2 == 0:
            mb[:, 0:2] = -30000.0
        else:
            mb[:, 18:20] = -30000.0
        m["maskb"] = mb
        al = np.zeros((16, 256), dtype=np.float32)
        for tl in range(8):
            mq = 8 * (core % 2) + tl
            if mq == 0:
                al[2*tl, :] = 0.0; al[2*tl+1, :] = 1.0
            else:
                al[2*tl, :] = 1.0 - alpha; al[2*tl+1, :] = alpha
        m["alphas"] = al.reshape(1, -1)
        m["x1_dev"] = x1_list[core]
        in_maps.append(m)
    return in_maps

def run_B(inputs, x1_list):
    consts = _get_consts()
    in_maps = _prep_B_inputs(inputs, consts, x1_list)
    nc, es = build_B(consts)
    res = run_bass_kernel_spmd(nc, in_maps, core_ids=list(range(8)))
    es.close()
    if DEV_DUMP:
        return ([r["s1T"] for r in res.results], [r["xn1T_out"] for r in res.results], res.results[0])
    return ([r["s1T"] for r in res.results], [r["xn1T_out"] for r in res.results])


def emit_C(p, t, sb, ps):
    qs = p.sem("qs"); qg = p.sem("qg"); tm = p.sem("tm"); vm = p.sem("vm"); sm = p.sem("sm")
    qa = p.sem("qa"); qv = p.sem("qv")

    p.dma("sync", sb["shifts"][:, :], t["shifts"][:, :], qs)
    p.dma("sync", sb["ones1f"][0:1, 0:128], t["ones1f"][0:1, 0:128], qs)
    p.dma("sync", sb["EG"][:, :], t["EG"][:, :], qs)
    p.dma("sync", sb["onescol"][:, :], t["ones16"][:, :], qs)
    for blk in range(4):
        c0 = 688 * blk
        f0 = 5504 * blk
        p.wait("sync", vm); p.wait("sync", qg)
        p.dma("sync", sb["qzg"][0:32, :], t["qzeros_gu"][:, c0:c0+688], qs)
        p.dma("sync", sb["scg"][0:32, :], t["scales_gu"][:, f0:f0+5504], qs)
        p.wait("vector", qs)
        for kk in range(8):
            p.run("vector", lambda e, kk=kk: e.tensor_scalar(out=sb["zqg"][0:32, kk::8], in0=sb["qzg"][0:32, :], scalar1=4*kk, scalar2=15, op0=OP.logical_shift_right, op1=OP.bitwise_and), vm)
        p.run("vector", lambda e: e.tensor_scalar(out=sb["scg"][0:32, :], in0=sb["scg"][0:32, :], scalar1=-1.0, scalar2=None, op0=OP.mult), vm)
        p.wait("scalar", vm)
        p.run("scalar", lambda e: e.copy(out=sb["zqgf"][0:32, :], in_=sb["zqg"][0:32, :]), sm)
        p.wait("vector", sm)
        p.run("vector", lambda e: e.scalar_tensor_tensor(out=sb["zqgf"][0:32, :], in0=sb["zqgf"][0:32, :], scalar=1.0, in1=sb["scg"][0:32, :], op0=OP.add, op1=OP.mult), vm)
        p.wait("gpsimd", vm)
        p.dma("gpsimd", t["zb_gu"][:, f0:f0+5504], sb["zqgf"][0:32, :], qg)
    for g0, gn in [(0, 32), (32, 32), (64, 22)]:
        p.wait("sync", vm); p.wait("sync", qg)
        p.dma("sync", sb["qzd"][0:gn, :], t["qzeros_down"][g0:g0+gn, :], qs)
        p.dma("sync", sb["scd"][0:gn, 0:4096], t["scales_down"][g0:g0+gn, :], qs)
        p.wait("vector", qs)
        for kk in range(8):
            p.run("vector", lambda e, kk=kk, gn=gn: e.tensor_scalar(out=sb["zqd"][0:gn, kk::8], in0=sb["qzd"][0:gn, :], scalar1=4*kk, scalar2=15, op0=OP.logical_shift_right, op1=OP.bitwise_and), vm)
        p.run("vector", lambda e, gn=gn: e.tensor_scalar(out=sb["scd"][0:gn, 0:4096], in0=sb["scd"][0:gn, 0:4096], scalar1=-1.0, scalar2=None, op0=OP.mult), vm)
        p.wait("scalar", vm)
        p.run("scalar", lambda e, gn=gn: e.copy(out=sb["zqdf"][0:gn, 0:4096], in_=sb["zqd"][0:gn, :]), sm)
        p.wait("vector", sm)
        p.run("vector", lambda e, gn=gn: e.scalar_tensor_tensor(out=sb["zqdf"][0:gn, 0:4096], in0=sb["zqdf"][0:gn, 0:4096], scalar=1.0, in1=sb["scd"][0:gn, 0:4096], op0=OP.add, op1=OP.mult), vm)
        p.wait("gpsimd", vm)
        p.dma("gpsimd", t["zb_d"][g0:g0+gn, :], sb["zqdf"][0:gn, 0:4096], qg)
    zbgu_store_c = qg.c

    qw_s = [p.sem("qw0"), p.sem("qw1")]
    sc_s = [p.sem("sc0"), p.sem("sc1")]
    st_s = [p.sem("st0"), p.sem("st1")]
    qz = p.sem("qz"); qh = p.sem("qh"); qsh = p.sem("qsh")
    qx1 = p.sem("qx1"); qx2 = p.sem("qx2")

    p.wait("sync", vm); p.wait("sync", qg)
    p.wait("scalar", vm); p.wait("scalar", qg)
    p.dma("sync", sb["xq"][:, 0:16*2048].rearrange("p (c n) -> p c n", c=16),
          t["xn1T"][0:16, :, OWN0:OWN0+NOWN].transpose([1, 0, 2]), qx1)
    xld_s = qx1.c
    p.dma("scalar", sb["xq"][:, 16*2048:32*2048].rearrange("p (c n) -> p c n", c=16),
          t["xn1T"][16:32, :, OWN0:OWN0+NOWN].transpose([1, 0, 2]), qx2)
    xld_a = qx2.c

    dqu = [sb["dqu0"], sb["dqu1"]]
    dqf = [sb["dqf0"], sb["dqf1"]]
    scb = [sb["scb0"], sb["scb1"]]
    wdout = [sb["wdout0"], sb["wdout1"]]
    pe = [ps[6], ps[7]]
    qwd_h = t["qweight_down_h"]
    qwgu_h = t["qweight_gu_h"]
    copy_c = [0, 0]; tt1_c = [0, 0]; qw_c = [0, 0]; bc_c = [0, 0]
    scb_c = [(None, 0), (None, 0)]
    st_c = [None, None]
    for s in range(NGF * 16):
        f, j = divmod(s, 16)
        sl = s % 2
        if copy_c[sl]:
            p.wait("gpsimd", sm, copy_c[sl])
        p.dma("gpsimd", dqu[sl][:, :], AP(qwd_h, (16*f)*4096 + 256*j, [[4096, 16], [0, 8], [1, 256]]), qw_s[sl])
        qw_c[sl] = qw_s[sl].c
        if bc_c[sl]:
            p.wait("sync", tm, bc_c[sl])
        p.dma("sync", scb[sl][0:1, 0:256], t["scales_down"][f:f+1, 256*j:256*(j+1)], sc_s[sl])
        sccnt = sc_s[sl].c
        p.wait("tensor", sc_s[sl], sccnt)
        if tt1_c[sl]:
            p.wait("tensor", vm, tt1_c[sl])
        p.run("tensor", lambda e, sl=sl: e.matmul(pe[sl][:, 0:256], sb["ones1f"][0:1, 0:128], scb[sl][0:1, 0:256], start=True, stop=True), tm)
        bc_c[sl] = tm.c
        p.wait("vector", qw_s[sl], qw_c[sl])
        p.run("vector", lambda e, sl=sl: e.tensor_scalar(out=dqu[sl][:, :], in0=dqu[sl][:, :], scalar1=sb["shifts"][:, 0:1], scalar2=15, op0=OP.logical_shift_right, op1=OP.bitwise_and), vm)
        sh_c = vm.c
        p.wait("scalar", vm, sh_c)
        p.run("scalar", lambda e, sl=sl: e.copy(out=dqf[sl][:, :], in_=dqu[sl][:, :]), sm)
        copy_c[sl] = sm.c
        p.wait("vector", sm, copy_c[sl])
        p.wait("vector", tm, bc_c[sl])
        if j == 0 and st_c[f % 2] is not None:
            p.wait("vector", st_c[f % 2][0], st_c[f % 2][1])
        p.run("vector", lambda e, sl=sl, fp=f % 2, j=j: e.tensor_tensor(out=wdout[fp][:, 256*j:256*(j+1)], in0=dqf[sl][:, :], in1=pe[sl][:, 0:256], op=OP.mult), vm)
        tt1_c[sl] = vm.c
        if j == 15:
            ring2 = "gpsimd" if f % 2 == 0 else "sync"
            sem2 = st_s[f % 2]
            p.wait(ring2, vm, tt1_c[sl])
            p.dma(ring2, t["wd_bf"][f, :, :], wdout[f % 2][:, :], sem2)
            st_c[f % 2] = (sem2, sem2.c)
    wd_last_stores = [st_c[0], st_c[1]]

    p.wait("tensor", qx1, xld_s)
    p.wait("tensor", qx2, xld_a)
    s_copy_c = 0
    for rt in range(4):
        pa = ps[4][0:32, 0:512]
        if s_copy_c:
            p.wait("tensor", sm, s_copy_c)
        for g in range(32):
            p.run("tensor", lambda e, g=g, rt=rt: e.matmul(pa, sb["EG"][:, 32*g:32*(g+1)], sb["xq"][:, 2048*g+512*rt:2048*g+512*(rt+1)], start=(g == 0), stop=(g == 31)), tm if g == 31 else None)
        p.wait("scalar", tm)
        p.run("scalar", lambda e, rt=rt: e.copy(out=sb["S_sb"][0:32, 512*rt:512*(rt+1)], in_=pa), sm)
        s_copy_c = sm.c
    s_ready = s_copy_c

    zb_gu_h = t["zb_gu_h"]
    wgu = [sb["wgu0"], sb["wgu1"]]
    wgu_free_c = [0, 0]
    zbt_c = [0]
    deq_end = {}
    slice_ctr = [0]

    def deq_pair_group(ip, g):
        for k in range(4*g, 4*g+4):
            for half in range(2):
                sl = slice_ctr[0] % 2
                slice_ctr[0] += 1
                col0 = 11008 * half + 256 * ip
                if copy_c[sl]:
                    p.wait("gpsimd", sm, copy_c[sl])
                p.dma("gpsimd", dqu[sl][:, 0:256], AP(qwgu_h, (16*k)*22016 + col0, [[22016, 16], [0, 8], [1, 256]]), qw_s[sl])
                qw_c[sl] = qw_s[sl].c
                ring = "sync" if slice_ctr[0] % 2 == 0 else "scalar"
                rsem = sc_s[sl]
                if tt1_c[sl]:
                    p.wait(ring, vm, tt1_c[sl])
                p.wait(ring, tm, bc_c[sl])
                p.dma(ring, scb[sl][:, 0:256], t["scales_gu"][k:k+1, col0:col0+256].partition_broadcast(128), rsem)
                scb_c[sl] = (rsem, rsem.c)
                p.wait("vector", qw_s[sl], qw_c[sl])
                p.run("vector", lambda e, sl=sl: e.tensor_scalar(out=dqu[sl][:, 0:256], in0=dqu[sl][:, 0:256], scalar1=sb["shifts"][:, 0:1], scalar2=15, op0=OP.logical_shift_right, op1=OP.bitwise_and), vm)
                sh_c = vm.c
                p.wait("scalar", vm, sh_c)
                p.run("scalar", lambda e, sl=sl: e.copy(out=dqf[sl][:, 0:256], in_=dqu[sl][:, 0:256]), sm)
                copy_c[sl] = sm.c
                p.wait("vector", sm, copy_c[sl])
                p.wait("vector", scb_c[sl][0], scb_c[sl][1])
                if k == 4*g and half == 0:
                    if ip == 0:
                        for stc in wd_last_stores:
                            if stc is not None:
                                p.wait("vector", stc[0], stc[1])
                    elif ip == 1:
                        p.wait("vector", sm, s_ready)
                    else:
                        p.wait("vector", tm, wgu_free_c[ip % 2])
                dcol = 512*k + 256*half
                p.run("vector", lambda e, sl=sl, ipb=ip % 2, dcol=dcol: e.tensor_tensor(out=wgu[ipb][:, dcol:dcol+256], in0=dqf[sl][:, 0:256], in1=scb[sl][:, 0:256], op=OP.mult), vm)
                tt1_c[sl] = vm.c

    def load_zbt(ip, free_c):
        p.wait("gpsimd", qg, zbgu_store_c)
        if free_c:
            p.wait("gpsimd", tm, free_c)
        p.dma("gpsimd", sb["zbt"][0:32, :], AP(zb_gu_h, 256*ip, [[22016, 32], [11008, 2], [1, 256]]), qz)
        zbt_c[0] = qz.c

    load_zbt(0, 0)
    for g in range(8):
        deq_pair_group(0, g)
    deq_end[0] = vm.c

    silu_slot = [0, 0]
    h16_slot = [0, 0]
    h_store_c = 0
    sh_pend = None
    sh_mm_c = [0, 0]
    sh_cp_c = [0, 0]
    sh_dma_c = 0
    psSh = [ps[4], ps[5]]

    def emit_sh_matmul(bank, h16c):
        p.wait("tensor", vm, h16c)
        if sh_cp_c[bank]:
            p.wait("tensor", sm, sh_cp_c[bank])
        p.run("tensor", lambda e, bank=bank: e.matmul(psSh[bank][0:1, 0:512], sb["onescol"][:, 0:1], sb["h16"][:, :], start=True, stop=True), tm)
        sh_mm_c[bank] = tm.c

    def emit_sh_evac(bank, fci, rt):
        p.wait("scalar", tm, sh_mm_c[bank])
        if sh_dma[0]:
            p.wait("scalar", qsh, sh_dma[0])
        p.run("scalar", lambda e, bank=bank: e.copy(out=sb["shrow"][0:1, :], in_=psSh[bank][0:1, 0:512]), sm)
        sh_cp_c[bank] = sm.c
        p.wait("sync", sm, sh_cp_c[bank])
        p.dma("sync", t["S_h"][fci:fci+1, 512*rt:512*(rt+1)], sb["shrow"][0:1, :], qsh)
        sh_dma[0] = qsh.c

    sh_dma = [0]
    gctr = 0
    for i in range(43):
        for g in range(8):
            fc, rt = divmod(g, 4)
            fci = 2 * i + fc
            sl2 = gctr % 2
            psG = ps[0 + sl2]
            psU = ps[2 + sl2]
            if g == 0:
                p.wait("tensor", vm, deq_end[i])
                p.wait("tensor", qz, zbt_c[0])
                if i == 0:
                    p.wait("tensor", sm, s_ready)
            if silu_slot[sl2]:
                p.wait("tensor", sm, silu_slot[sl2])
            for k in range(NFC):
                p.run("tensor", lambda e, k=k, fc=fc, rt=rt, psG=psG, ib=i % 2: e.matmul(psG[:, :], wgu[ib][:, 512*k+128*fc:512*k+128*(fc+1)], sb["xq"][:, 2048*k+512*rt:2048*k+512*(rt+1)], start=(k == 0), stop=False))
            p.run("tensor", lambda e, fc=fc, rt=rt, psG=psG: e.matmul(psG[:, :], sb["zbt"][0:32, 128*fc:128*(fc+1)], sb["S_sb"][0:32, 512*rt:512*(rt+1)], start=False, stop=True), tm)
            tmG = tm.c
            if h16_slot[sl2]:
                p.wait("tensor", vm, h16_slot[sl2])
            for k in range(NFC):
                p.run("tensor", lambda e, k=k, fc=fc, rt=rt, psU=psU, ib=i % 2: e.matmul(psU[:, :], wgu[ib][:, 512*k+256+128*fc:512*k+256+128*(fc+1)], sb["xq"][:, 2048*k+512*rt:2048*k+512*(rt+1)], start=(k == 0), stop=False))
            p.run("tensor", lambda e, fc=fc, rt=rt, psU=psU: e.matmul(psU[:, :], sb["zbt"][0:32, 256+128*fc:256+128*(fc+1)], sb["S_sb"][0:32, 512*rt:512*(rt+1)], start=False, stop=True), tm)
            tmU = tm.c
            if sh_pend is not None:
                pb, pf, pr, ph16 = sh_pend
                emit_sh_matmul(pb, ph16)
            p.wait("scalar", tm, tmG)
            if h16_slot[1 - sl2]:
                p.wait("scalar", vm, h16_slot[1 - sl2])
            p.run("scalar", lambda e, psG=psG: e.activation(out=sb["gsil"][:, :], in_=psG[:, :], func=AF.Silu), sm)
            silu_slot[sl2] = sm.c
            if sh_pend is not None:
                pb, pf, pr, ph16 = sh_pend
                emit_sh_evac(pb, pf, pr)
            p.wait("vector", sm, silu_slot[sl2])
            p.wait("vector", tm, tmU)
            if h_store_c:
                p.wait("vector", qh, h_store_c)
            if sh_pend is not None:
                p.wait("vector", tm, sh_mm_c[sh_pend[0]])
            p.run("vector", lambda e, psU=psU: e.tensor_tensor(out=sb["h16"][:, :], in0=sb["gsil"][:, :], in1=psU[:, :], op=OP.mult), vm)
            h16_slot[sl2] = vm.c
            p.wait("gpsimd", vm, h16_slot[sl2])
            p.dma("gpsimd", t["h_scr"][fci, :, 512*rt:512*(rt+1)], sb["h16"][:, :], qh)
            h_store_c = qh.c
            sh_pend = (gctr % 2, fci, rt, h16_slot[sl2])
            if i < 42:
                deq_pair_group(i + 1, g)
            gctr += 1
        if i < 42:
            deq_end[i + 1] = vm.c
        wgu_free_c[i % 2] = tm.c
        if i < 42:
            load_zbt(i + 1, tm.c)
    pb, pf, pr, ph16 = sh_pend
    emit_sh_matmul(pb, ph16)
    emit_sh_evac(pb, pf, pr)

def build_C(consts):
    nc = bass.Bass("TRN2", target_bir_lowering=False)
    es = ExitStack()
    t = {}
    def din(name, shape, dt):
        h = nc.dram_tensor(name, shape, dt, kind="ExternalInput")
        t[name] = h.ap()
        return h
    def dout(name, shape, dt):
        t[name] = nc.dram_tensor(name, shape, dt, kind="ExternalOutput").ap()
    def dint(name, shape, dt):
        h = nc.dram_tensor(name, shape, dt, kind="Internal")
        t[name] = h.ap()
        return h
    din("ones1f", [1, 512], F32)
    din("shifts", [128, 1], U32)
    din("xn1T", [NFC, 128, RPAD], BF16)
    din("qweight_gu", [4096, 2*INTER], U32)
    din("qzeros_gu", [32, 2752], U32)
    din("scales_gu", [32, 2*INTER], F32)
    din("qweight_down", [11008, DM], U32)
    din("qzeros_down", [NGF, 512], U32)
    din("scales_down", [NGF, DM], F32)
    dout("h_scr", [NGF, 128, NOWN], BF16)
    dout("wd_bf", [NGF, 128, DM], BF16)
    dint("zb_gu", [32, 2*INTER], F32)
    dint("zb_d", [NGF, DM], F32)
    if DEV_DUMP:
        dout("dbg_zbd", [NGF, DM], F32)
        dout("dbg_zbgu", [32, 2*INTER], F32)
    sb = {}
    def asb(name, shape, dt):
        sb[name] = es.enter_context(nc.sbuf_tensor("s_" + name, shape, dt))
    asb("ones1f", [1, 512], F32); asb("shifts", [128, 1], U32)
    asb("dqu", [128, 512], U32); asb("dqf", [128, 512], F32)
    asb("scrow", [1, 4096], F32); asb("zbrow", [1, 4096], F32)
    asb("dqt", [128, 512], F32)
    asb("wg_sb", [128, NFC * 256], BF16); asb("wu_sb", [128, NFC * 256], BF16)
    asb("xq", [128, NFC * 2048], BF16)
    asb("gsil", [128, 512], F32); asb("h16", [128, 512], BF16)
    asb("vjunk", [128, 32], F32)
    ps_h = [es.enter_context(nc.psum_tensor(f"pb{i}", [128, 512], F32)) for i in range(8)]
    p = Prog(nc, es)
    sba = {k: v[:, :] for k, v in sb.items()}
    xq_u = sb["xq"][:, :].bitcast(U32)
    xq_f = sb["xq"][:, :].bitcast(F32)
    sba["qzg"] = xq_u[:, 0:688]
    sba["zqg"] = xq_u[:, 1024:1024+5504]
    sba["zqgf"] = xq_f[:, 8192:8192+5504]
    sba["scg"] = xq_f[:, 16384:16384+5504]
    sba["qzd"] = xq_u[:, 0:512]
    sba["zqd"] = xq_u[:, 1024:1024+4096]
    sba["zqdf"] = xq_f[:, 8192:8192+4096]
    sba["scd"] = xq_f[:, 16384:16384+4096]
    sba["wdout"] = sb["wg_sb"][:, :].bitcast(BF16)[:, 0:4096]
    psa = [x[:, :] for x in ps_h]
    emit_C(p, t, sba, psa)
    qg = p.sem("qg"); qs = p.sem("qs")
    p.wait("sync", qg); p.wait("sync", qs)
    p.emit()
    return nc, es


def run_C(inputs, xn1T_list):
    consts = _get_consts()
    gsel_gu = np.asarray(inputs["L1_ffn_g_idx_gu"]).reshape(-1, 128)[:, 0]
    gsel_d = np.asarray(inputs["L1_ffn_g_idx_down"]).reshape(-1, 128)[:, 0]
    base = {
        "ones1f": consts["ones1f"],
        "shifts": (4 * (np.arange(128, dtype=np.uint32) % 8)).reshape(128, 1),
        "qweight_gu": np.repeat(np.asarray(inputs["L1_ffn_qweight_gu"]), 8, axis=0),
        "qzeros_gu": np.ascontiguousarray(np.asarray(inputs["L1_ffn_qzeros_gu"])[gsel_gu]),
        "scales_gu": np.ascontiguousarray(np.asarray(inputs["L1_ffn_scales_gu"])[gsel_gu]),
        "qweight_down": np.repeat(np.asarray(inputs["L1_ffn_qweight_down"]), 8, axis=0),
        "qzeros_down": np.ascontiguousarray(np.asarray(inputs["L1_ffn_qzeros_down"])[gsel_d]),
        "scales_down": np.ascontiguousarray(np.asarray(inputs["L1_ffn_scales_down"])[gsel_d]),
    }
    in_maps = []
    for core in range(8):
        m = dict(base)
        m["xn1T"] = xn1T_list[core]
        in_maps.append(m)
    nc, es = build_C(consts)
    res = run_bass_kernel_spmd(nc, in_maps, core_ids=list(range(8)))
    es.close()
    if DEV_DUMP:
        return ([r["h_scr"] for r in res.results], [r["wd_bf"] for r in res.results], res.results[0])
    return ([r["h_scr"] for r in res.results], [r["wd_bf"] for r in res.results])

def emit_D(p, t, sb, ps):
    qs = p.sem("qs"); qg = p.sem("qg"); tm = p.sem("tm"); vm = p.sem("vm"); sm = p.sem("sm")
    qa = p.sem("qa"); qv = p.sem("qv")
    p.dma("sync", sb["identf"][:, :], t["identf"][:, :], qs)
    HH = NGF // 2
    wdbs = [sb["wd_sb"], sb["wd_sb2"]]
    zbds = [sb["zbdt0"], sb["zbdt1"]]
    wl_s = [p.sem("wl0"), p.sem("wl1")]
    hd = p.sem("hd")
    s1_s = [p.sem("s10"), p.sem("s11")]
    wld = [None, None]

    def load_wd(oc):
        sl = oc % 2
        p.dma("sync", wdbs[sl][:, 0:128*HH].rearrange("p (c n) -> p c n", c=HH),
              t["wd_bf"][0:HH, :, 128*oc:128*(oc+1)].transpose([1, 0, 2]), wl_s[sl])
        p.dma("scalar", wdbs[sl][:, 128*HH:128*NGF].rearrange("p (c n) -> p c n", c=NGF-HH),
              t["wd_bf"][HH:NGF, :, 128*oc:128*(oc+1)].transpose([1, 0, 2]), wl_s[sl])
        p.dma("gpsimd", zbds[sl][0:NGF, :], t["zb_d"][:, 128*oc:128*(oc+1)], wl_s[sl])
        wld[sl] = wl_s[sl].c

    for rt in range(4):
        r0 = OWN0 + 512 * rt
        for eng in ("sync", "scalar", "gpsimd", "vector"):
            p.wait(eng, vm); p.wait(eng, tm); p.wait(eng, qg)
        p.dma("sync", sb["h_sb"][:, 0:512*HH].rearrange("p (c n) -> p c n", c=HH),
              t["h_scr"][0:HH, :, 512*rt:512*(rt+1)].transpose([1, 0, 2]), hd)
        p.dma("scalar", sb["h_sb"][:, 512*HH:512*NGF].rearrange("p (c n) -> p c n", c=NGF-HH),
              t["h_scr"][HH:NGF, :, 512*rt:512*(rt+1)].transpose([1, 0, 2]), hd)
        p.dma("gpsimd", sb["sht"][0:NGF, :], t["S_h"][:, 512*rt:512*(rt+1)], hd)
        for j in range(4):
            p.dma("gpsimd", sb["xr"][j][:, :], t["x1_dev"][r0+128*j:r0+128*(j+1), :], hd)
        hdld = hd.c
        load_wd(0)
        sgt_done = [0, 0]
        silu_c = [0, 0]
        for oc in range(NFC):
            sl = oc % 2
            wdb = wdbs[sl]
            mywld = wld[sl]
            if oc + 1 < NFC:
                if oc >= 1:
                    p.wait("sync", tm); p.wait("scalar", tm); p.wait("gpsimd", tm)
                load_wd(oc + 1)
            p.wait("gpsimd", vm)
            p.dma("gpsimd", sb["s1c"][:, :] if sl == 0 else sb["s1c2"][:, :], t["s1T"][oc, :, 512*rt:512*(rt+1)], s1_s[sl])
            sld = s1_s[sl].c
            psD = ps[0] if sl == 0 else ps[1]
            p.wait("tensor", wl_s[sl], mywld)
            if oc == 0:
                p.wait("tensor", hd, hdld)
            if silu_c[sl]:
                p.wait("tensor", sm, silu_c[sl])
            for f in range(NGF):
                p.run("tensor", lambda e, f=f, psD=psD, wdb=wdb: e.matmul(psD[:, :], wdb[:, 128*f:128*(f+1)], sb["h_sb"][:, 512*f:512*(f+1)], start=(f == 0), stop=False))
            p.run("tensor", lambda e, sl=sl, psD=psD: e.matmul(psD[:, :], zbds[sl][0:NGF, :], sb["sht"][0:NGF, :], start=False, stop=True), tm)
            tmD = tm.c
            p.wait("scalar", tm, tmD)
            p.wait("scalar", vm)
            p.run("scalar", lambda e, psD=psD: e.activation(out=sb["gd"][:, :], in_=psD[:, :], func=AF.Silu), sm)
            silu_c[sl] = sm.c
            p.wait("vector", sm, silu_c[sl])
            p.wait("vector", s1_s[sl], sld)
            p.run("vector", lambda e, sl=sl: e.tensor_tensor(out=sb["sgt"][:, :], in0=sb["gd"][:, :], in1=sb["s1c"][:, :] if sl == 0 else sb["s1c2"][:, :], op=OP.mult), vm)
            sgt_v = vm.c
            p.wait("tensor", vm, sgt_v)
            for j in range(4):
                p.run("tensor", lambda e, j=j: e.transpose(ps[4][:, 128*j:128*(j+1)], sb["sgt"][:, 128*j:128*(j+1)], sb["identf"][:, :]), tm if j == 3 else None)
            p.wait("vector", tm)
            for j in range(4):
                p.run("vector", lambda e, oc=oc, j=j: e.tensor_tensor(out=sb["xr"][j][:, 128*oc:128*(oc+1)], in0=sb["xr"][j][:, 128*oc:128*(oc+1)], in1=ps[4][:, 128*j:128*(j+1)], op=OP.add), vm)
            sgt_done[sl] = vm.c
        p.wait("gpsimd", vm)
        for j in range(4):
            p.dma("gpsimd", t["out_rows"][512*rt+128*j:512*rt+128*(j+1), :], sb["xr"][j][:, :], qg)

def build_D(consts):
    nc = bass.Bass("TRN2", target_bir_lowering=False)
    es = ExitStack()
    t = {}
    def din(name, shape, dt):
        h = nc.dram_tensor(name, shape, dt, kind="ExternalInput")
        t[name] = h.ap()
        return h
    def dout(name, shape, dt):
        t[name] = nc.dram_tensor(name, shape, dt, kind="ExternalOutput").ap()
    din("identf", [128, 128], F32)
    din("h_scr", [NGF, 128, NOWN], BF16)
    din("wd_bf", [NGF, 128, DM], BF16)
    din("s1T", [NFC, 128, NOWN], F32)
    din("x1_dev", [RPAD, DM], F32)
    dout("out_rows", [NOWN, DM], F32)
    sb = {}
    def asb(name, shape, dt):
        sb[name] = es.enter_context(nc.sbuf_tensor("s_" + name, shape, dt))
    asb("identf", [128, 128], F32)
    asb("h_sb", [128, NGF * 512], BF16)
    sb["xr"] = [es.enter_context(nc.sbuf_tensor(f"s_xr{j}", [128, DM], F32)) for j in range(4)]
    asb("wd_sb", [128, NGF * 128], BF16); asb("wd_sb2", [128, NGF * 128], BF16)
    asb("s1c", [128, 512], F32); asb("s1c2", [128, 512], F32)
    asb("gd", [128, 512], F32); asb("sgt", [128, 512], F32)
    asb("vjunk", [128, 32], F32)
    ps_h = [es.enter_context(nc.psum_tensor(f"pb{i}", [128, 512], F32)) for i in range(8)]
    p = Prog(nc, es)
    sba = {k: (v if isinstance(v, list) else v[:, :]) for k, v in sb.items()}
    sba["xr"] = [x[:, :] for x in sb["xr"]]
    psa = [x[:, :] for x in ps_h]
    emit_D(p, t, sba, psa)
    qg = p.sem("qg"); qs = p.sem("qs")
    p.wait("sync", qg); p.wait("sync", qs)
    p.emit()
    return nc, es

def run_D(inputs, x1_list, s1_list, h_list, wd_list):
    consts = _get_consts()
    in_maps = []
    for core in range(8):
        m = {"identf": consts["identf"], "h_scr": h_list[core], "wd_bf": wd_list[core],
             "s1T": s1_list[core], "x1_dev": x1_list[core]}
        in_maps.append(m)
    nc, es = build_D(consts)
    res = run_bass_kernel_spmd(nc, in_maps, core_ids=list(range(8)))
    es.close()
    return [r["out_rows"] for r in res.results]


_PHASE_ALLOCS = {
 "A": [("identf",[128,128],"F32"),("ident16",[128,128],"BF16"),("J16",[128,128],"BF16"),
  ("W1re",[128,128],"F32"),("W1imneg",[128,128],"F32"),("W1im",[128,128],"F32"),
  ("twre",[128,32],"F32"),("twim",[128,32],"F32"),("W2re",[32,32],"F32"),("W2imneg",[32,32],"F32"),
  ("VcReT",[64,64],"F32"),("VcImT",[64,64],"F32"),("Bp",[64,1],"F32"),("dD",[1,1],"F32"),("lstep",[1,1],"F32"),
  ("ones1f",[1,512],"F32"),("sm64",[64,20],"F32"),("sm64t",[64,16],"F32"),("smT",[18,64],"F32"),
  ("step",[1,4],"F32"),("r2bc",[128,1],"F32"),
  ("itan_re",[128,32],"F32"),("itan_im",[128,32],"F32"),("c_re",[128,32],"F32"),("c_im",[128,32],"F32"),
  ("lam_re",[128,64],"F32"),("lam_im",[128,64],"F32"),
  ("v00re",[128,64],"F32"),("v00im",[128,64],"F32"),("v01re",[128,64],"F32"),("v01im",[128,64],"F32"),
  ("v10re",[128,64],"F32"),("v10im",[128,64],"F32"),("v11re",[128,64],"F32"),("v11im",[128,64],"F32"),
  ("v00ren",[128,64],"F32"),("v01ren",[128,64],"F32"),("v10ren",[128,64],"F32"),("v11ren",[128,64],"F32"),
  ("k00re",[128,32],"F32"),("k00im",[128,32],"F32"),("k01re",[128,32],"F32"),("k01im",[128,32],"F32"),
  ("k10re",[128,32],"F32"),("k10im",[128,32],"F32"),("k11re",[128,32],"F32"),("k11im",[128,32],"F32"),
  ("g_re",[128,32],"F32"),("g_im",[128,32],"F32"),
  ("dreA",[128,32],"F32"),("dimA",[128,32],"F32"),("u1A",[128,32],"F32"),("u2A",[128,32],"F32"),("invA",[128,32],"F32"),
  ("dreB",[128,32],"F32"),("dimB",[128,32],"F32"),("u1B",[128,32],"F32"),("u2B",[128,32],"F32"),("invB",[128,32],"F32"),
  ("wre",[128,32],"F32"),("wim",[128,32],"F32"),("ure",[128,32],"F32"),("uim",[128,32],"F32"),
  ("atre",[128,32],"F32"),("atim",[128,32],"F32"),("Gre",[128,32],"F32"),("Gim",[128,32],"F32"),
  ("Gpre",[128,32],"F32"),("Gpim",[128,32],"F32"),
  ("r01",[128,32],"F32"),("r02",[128,32],"F32"),("r03",[128,32],"F32"),("r04",[128,32],"F32"),
  ("r05",[128,32],"F32"),("r06",[128,32],"F32"),("r07",[128,32],"F32"),("r08",[128,32],"F32"),
  ("GTre",[32,128],"F32"),("GTim",[32,128],"F32"),("Ksb",[32,128],"F32"),("K16",[32,128],"BF16"),("z16",[1,128],"BF16"),
  ("Ttiles",[128,4096],"BF16"),("big1",[128,4096],"F32"),("w0out",[128,4096],"BF16"),
  ("scrow",[1,512],"F32"),("zrrow",[1,512],"F32"),("dqt",[128,512],"F32"),
  ("ssq",[128,4],"F32"),("ssum",[128,4],"F32"),("mcol",[128,4],"F32"),("ecol",[128,4],"F32"),
  ("vcol",[128,4],"F32"),("scol",[128,4],"F32"),("rstd",[128,4],"F32"),("bneg",[128,4],"F32"),
  ("xn16",[128,4096],"BF16"),("xnT",[128,16384],"BF16"),
  ("w0sb",[128,4096],"BF16"),("w0sb2",[128,4096],"BF16"),("bias_sb",[128,32],"F32"),
  ("gt",[128,512],"F32"),("sgt",[128,512],"F32"),
  ("ur0",[128,4096],"F32"),("ur1",[128,4096],"F32"),("ur2",[128,4096],"F32"),("ur3",[128,4096],"F32")],
 "B": [("ident16",[128,128],"BF16"),("identf",[128,128],"F32"),("J2",[128,128],"BF16"),
  ("ones16",[128,1],"BF16"),("ones1f",[1,512],"F32"),("maskb",[128,20],"F32"),("alphas",[1,4096],"F32"),
  ("cosQ",[128,2560],"BF16"),("sinQ",[128,2560],"BF16"),("cosK",[128,2560],"BF16"),("sinK",[128,2560],"BF16"),
  ("wk_sb",[128,4096],"BF16"),("wv_sb",[128,4096],"BF16"),
  ("bigbuf",[128,16384],"F32"),("xnTb",[128,16384],"BF16"),("xn16",[128,4096],"BF16"),
  ("ssq",[128,4],"F32"),("ssum",[128,4],"F32"),("mcol",[128,4],"F32"),("ecol",[128,4],"F32"),
  ("vcol",[128,4],"F32"),("scol",[128,4],"F32"),("rstd",[128,4],"F32"),("bneg",[128,4],"F32"),
  ("ktmp",[128,512],"BF16"),("kstmp",[128,512],"BF16"),("vtmp",[128,512],"BF16"),
  ("q1t",[128,512],"BF16"),("q2t",[128,512],"BF16"),
  ("K_sb",[128,2560],"BF16"),("V_sb",[128,2560],"BF16"),
  ("wq_sb",[128,4096],"BF16"),("wq_sb2",[128,4096],"BF16"),
  ("attnT",[128,1536],"BF16"),("rec",[1,512],"F32"),("cc",[1,512],"F32"),
  ("oL",[128,256],"F32"),("oR",[128,256],"F32"),("f1",[128,256],"F32"),("f2",[128,256],"F32"),
  ("s1chunk",[128,512],"F32")],
 "C": [("ones1f",[1,128],"F32"),("shifts",[128,1],"U32"),
  ("onescol",[128,1],"BF16"),
  ("dqu0",[128,256],"U32"),("dqu1",[128,256],"U32"),
  ("dqf0",[128,256],"F32"),("dqf1",[128,256],"F32"),
  ("scb0",[128,256],"F32"),("scb1",[128,256],"F32"),
  ("wgu0",[128,16384],"BF16"),("wgu1",[128,16384],"BF16"),
  ("xq",[128,65536],"BF16"),("gsil",[128,512],"BF16"),("h16",[128,512],"BF16"),
  ("S_region",[128,2560],"BF16"),("shrow",[1,512],"F32")],
 "D": [("identf",[128,128],"F32"),("h_sb",[128,44032],"BF16"),
  ("wd_sb",[128,11008],"BF16"),("wd_sb2",[128,11008],"BF16"),
  ("zbdt0",[86,128],"BF16"),("zbdt1",[86,128],"BF16"),("sht",[86,512],"BF16"),
  ("s1c",[128,512],"F32"),("s1c2",[128,512],"F32"),("gd",[128,512],"F32"),("sgt",[128,512],"F32"),
  ("xr0",[128,4096],"F32"),("xr1",[128,4096],"F32"),("xr2",[128,4096],"F32"),("xr3",[128,4096],"F32")],
}
_DT = {"F32": (F32, 4), "BF16": (BF16, 2), "U32": (U32, 4)}

def _pool_views(pool_ap, allocs):
    out = {}
    off = 0
    for name, shape, dts in allocs:
        dt, sz = _DT[dts]
        nb = shape[1] * sz
        off = (off + 31) // 32 * 32
        if dts == "F32":
            v = pool_ap.bitcast(F32)
        elif dts == "U32":
            v = pool_ap.bitcast(U32)
        else:
            v = pool_ap.bitcast(BF16)
        el = off // sz
        out[name] = v[0:shape[0], el:el + shape[1]]
        off += nb
    assert off <= 53136 * 4, f"pool overflow {off}"
    return out

def _phase_barrier(p):
    for eng in Prog.ENGS:
        for s in list(p.sems.values()):
            p.wait(eng, s)

def build_FUSED(consts):
    nc = bass.Bass("TRN2", target_bir_lowering=False)
    es = ExitStack()
    t = {}
    def din(name, shape, dt):
        h = nc.dram_tensor(name, shape, dt, kind="ExternalInput")
        t[name] = h.ap()
        return h
    def dout(name, shape, dt):
        t[name] = nc.dram_tensor(name, shape, dt, kind="ExternalOutput").ap()
    def dint(name, shape, dt):
        h = nc.dram_tensor(name, shape, dt, kind="Internal")
        t[name] = h.ap()
        return h
    for nm, arr in consts.items():
        din(nm, list(arr.shape), BF16 if arr.dtype == BF else F32)
    for nm in ["lamRe", "lamIm", "CtRe", "CtIm", "pRe", "pIm", "qRe", "qIm"]:
        din(nm, [64, 1], F32)
    din("VcReT", [64, 64], F32); din("VcImT", [64, 64], F32)
    din("Bp", [64, 1], F32); din("dD", [1, 1], F32); din("lstep", [1, 1], F32)
    din("shifts", [128, 1], U32)
    din("u_dev", [RPAD, DM], F32)
    din("qwT", [DM, DM], I32)
    din("scalesT", [32, DM], F32); din("zerosT", [32, DM], F32)
    din("bias_dev", [128, 32], F32)
    din("WqT", [DM, DM], F32); din("WoT", [DM, DM], F32)
    din("WkT", [DM, 128], F32); din("WvT", [DM, 128], F32)
    din("cosQ", [128, RPAD], BF16); din("sinQ", [128, RPAD], BF16)
    din("cosK", [128, RPAD], BF16); din("sinK", [128, RPAD], BF16)
    din("maskb", [128, 20], F32); din("alphas", [1, 16 * 256], F32)
    t["qweight_gu_h"] = din("qweight_gu", [512, 2*INTER], U32)
    din("qzeros_gu", [32, 2752], U32)
    din("scales_gu", [32, 2*INTER], F32)
    t["qweight_down_h"] = din("qweight_down", [1376, DM], U32)
    din("qzeros_down", [NGF, 512], U32)
    din("scales_down", [NGF, DM], F32)
    kp = dint("Kpad", [1, 4352], BF16); t["Kpad_t"] = kp
    dint("smTd", [18, 64], F32); dint("stepd", [1, 1], F32)
    dint("w0cache", [NFC, 128, DM], BF16)
    dint("x1_dev", [RPAD, DM], F32)
    dint("xn1T", [NFC, 128, RPAD], BF16)
    dint("wq_bf", [DM, DM], BF16); dint("wo_bf", [DM, DM], BF16)
    dint("s1T", [NFC, 128, NOWN], F32)
    dint("h_scr", [NGF, 128, NOWN], BF16)
    dint("wd_bf", [NGF, 128, DM], BF16)
    t["zb_gu_h"] = dint("zb_gu", [32, 2*INTER], F32)
    dint("zb_d", [NGF, DM], F32)
    dint("S_h", [NGF, NOWN], F32)
    dout("out_rows", [NOWN, DM], F32)

    pool = es.enter_context(nc.sbuf_tensor("pool", [128, 53136], F32))
    pool_ap = pool[:, :]
    vjunk = es.enter_context(nc.sbuf_tensor("s_vjunk", [128, 32], F32))
    ps_h = [es.enter_context(nc.psum_tensor(f"pb{i}", [128, 512], F32)) for i in range(8)]
    psa = [x[:, :] for x in ps_h]

    p = Prog(nc, es)
    sbA = _pool_views(pool_ap, _PHASE_ALLOCS["A"])
    sbA["vjunk"] = vjunk[:, :]
    sbA["ur"] = [sbA.pop("ur%d" % j) for j in range(4)]
    emit_P0(p, t, sbA, psa)
    emit_L0(p, t, sbA, psa)
    _phase_barrier(p)
    sbB = _pool_views(pool_ap, _PHASE_ALLOCS["B"])
    sbB["vjunk"] = vjunk[:, :]
    sbB["bigbuf"] = sbB["bigbuf"]
    sbB["bigbuf_bf"] = sbB["bigbuf"].bitcast(BF16)
    emit_B(p, t, sbB, psa)
    _phase_barrier(p)
    sbC = _pool_views(pool_ap, _PHASE_ALLOCS["C"])
    sbC["vjunk"] = vjunk[:, :]
    xq_u = sbC["xq"].bitcast(U32)
    xq_f = sbC["xq"].bitcast(F32)
    sbC["qzg"] = xq_u[:, 0:688]
    sbC["zqg"] = xq_u[:, 1024:1024+5504]
    sbC["zqgf"] = xq_f[:, 8192:8192+5504]
    sbC["scg"] = xq_f[:, 16384:16384+5504]
    sbC["qzd"] = xq_u[:, 0:512]
    sbC["zqd"] = xq_u[:, 1024:1024+4096]
    sbC["zqdf"] = xq_f[:, 8192:8192+4096]
    sbC["scd"] = xq_f[:, 16384:16384+4096]
    sbC["wdout0"] = sbC["wgu0"][:, 0:4096]
    sbC["wdout1"] = sbC["wgu0"][:, 4096:8192]
    sbC["EG"] = sbC["wgu1"][:, 0:1024]
    Sreg = sbC.pop("S_region")
    sbC["S_sb"] = Sreg[0:32, 0:2048]
    sbC["zbt"] = Sreg[0:32, 2048:2560]
    emit_C(p, t, sbC, psa)
    _phase_barrier(p)
    sbD = _pool_views(pool_ap, _PHASE_ALLOCS["D"])
    sbD["vjunk"] = vjunk[:, :]
    sbD["xr"] = [sbD.pop("xr%d" % j) for j in range(4)]
    emit_D(p, t, sbD, psa)
    if DEV_DUMP2:
        _phase_barrier(p)
        qs = p.sem("qs")
        for nm, shp in [("x1_dev", [RPAD, DM]), ("s1T", [NFC, 128, NOWN]),
                        ("h_scr", [NGF, 128, NOWN]), ("wd_bf", [NGF, 128, DM]),
                        ("S_h", [NGF, NOWN]), ("zb_gu", [32, 2*INTER]), ("zb_d", [NGF, DM])]:
            dout("dbg_" + nm, shp, t[nm].dtype)
            p.dma("sync", t["dbg_" + nm], t[nm], qs)
    qg = p.sem("qg"); qs = p.sem("qs")
    p.wait("sync", qg)
    p.wait("sync", qs)
    p.emit()
    return nc, es

def kernel(**inputs):
    consts = _get_consts()
    in_maps_A = _prep_A_inputs(inputs, consts)
    in_maps_B = _prep_B_inputs(inputs, consts, [None] * 8)
    gsel_gu = np.asarray(inputs["L1_ffn_g_idx_gu"]).reshape(-1, 128)[:, 0]
    gsel_d = np.asarray(inputs["L1_ffn_g_idx_down"]).reshape(-1, 128)[:, 0]
    cbase = {
        "shifts": (4 * (np.arange(128, dtype=np.uint32) % 8)).reshape(128, 1),
        "qweight_gu": np.ascontiguousarray(np.asarray(inputs["L1_ffn_qweight_gu"])),
        "qzeros_gu": np.ascontiguousarray(np.asarray(inputs["L1_ffn_qzeros_gu"])[gsel_gu]),
        "scales_gu": np.ascontiguousarray(np.asarray(inputs["L1_ffn_scales_gu"])[gsel_gu]),
        "qweight_down": np.ascontiguousarray(np.asarray(inputs["L1_ffn_qweight_down"])),
        "qzeros_down": np.ascontiguousarray(np.asarray(inputs["L1_ffn_qzeros_down"])[gsel_d]),
        "scales_down": np.ascontiguousarray(np.asarray(inputs["L1_ffn_scales_down"])[gsel_d]),
    }
    in_maps = []
    for core in range(8):
        m = dict(in_maps_A[core])
        mb = in_maps_B[core]
        for k, v in mb.items():
            if k != "x1_dev":
                m[k] = v
        m.update(cbase)
        in_maps.append(m)
    nc, es = build_FUSED(consts)
    res = run_bass_kernel_spmd(nc, in_maps, core_ids=list(range(8)))
    es.close()
    out = np.zeros((4, SEQ, DM), dtype=np.float32)
    for core in range(8):
        b = core // 2
        q0 = 2048 * (core % 2)
        out[b, q0:q0+2048, :] = res.results[core]["out_rows"]
    if DEV_DUMP2:
        return out, res.results
    return out
